# version 15
# baseline (speedup 1.0000x reference)
import contextlib
import math

import numpy as np
import concourse.bass as bass
import concourse.mybir as mybir
from concourse.bass_utils import run_bass_kernel_spmd

F32 = mybir.dt.float32
BF16 = mybir.dt.bfloat16
I8 = mybir.dt.int8
I16 = mybir.dt.int16
AF = mybir.ActivationFunctionType
ALU = mybir.AluOpType
AX = mybir.AxisListType.X

B, S, D = 2, 4096, 2048
H, DH, DR, LAT, HID = 16, 128, 64, 512, 8192
TOK = 1024
NB = 8
SCALE = 1.0 / math.sqrt(DH + DR)
LN2P = math.log(10000.0) / 32.0
INV2PI = 1.0 / (2.0 * math.pi)
CW1 = 6.28125
CW2 = 2.0 * math.pi - 6.28125
PIH = math.pi / 2.0


def _blocks_for_core(c):
    return [4 * (s - 1) + c if s % 2 == 1 else 4 * s - 1 - c
            for s in range(1, 9)]


GLOBAL_SEMS = {"g0", "ld0", "dv0", "sc0", "lnv", "lns", "rts", "rtv",
               "phW", "phWS", "phA2", "phCK", "phKV", "phAT", "phWO",
               "phF1", "phQ2"}

SWDGE_SEMS = {"w1s0", "w1s1", "w2s0", "w2s1", "wog0", "wog1", "qpsw", "qps",
              "cksw", "ckst", "ckdup", "kvks0", "kvks1", "kvks2", "kvks3",
              "kvvs0", "kvvs1", "kvvs2", "kvvs3", "krs", "f1s0", "f1s1",
              "q2s0", "q2s1", "wst20", "wst21", "f2s", "fsr", "fsr2",
              "aV0", "aV1", "aV2", "aV3", "aQ0", "aQ1",
              "n2l0", "n2l1", "n2l2", "n2l3", "n2l4", "n2l5"}


class _SemView:
    def __init__(self, ctx):
        self.ctx = ctx

    def __getitem__(self, name):
        return self.ctx.sem(name)


class Ctx:

    def __init__(self, nc, blk, st):
        self.nc, self.blk, self.st = nc, blk, st
        self._phys = {}
        self._cnt = {}
        self._pool = []
        self._gpool = []
        self._scope = {}
        self._globals = {}
        self._nphys = 0
        self.sems = _SemView(self)

    def _new_phys(self):
        name = f"S{self._nphys}"
        self._nphys += 1
        self._phys[name] = self.st.enter_context(self.nc.semaphore(name))
        self._cnt[name] = 0
        return name

    def begin_scope(self):
        for log, phys in self._scope.items():
            (self._gpool if log in SWDGE_SEMS else self._pool).append(phys)
        self._scope = {}

    def _resolve(self, name):
        if name in GLOBAL_SEMS:
            if name not in self._globals:
                self._globals[name] = self._new_phys()
            return self._globals[name]
        if name not in self._scope:
            pool = self._gpool if name in SWDGE_SEMS else self._pool
            self._scope[name] = (pool.pop() if pool else self._new_phys())
        return self._scope[name]

    def sem(self, name):
        return self._phys[self._resolve(name)]

    def add(self, name, by):
        phys = self._resolve(name)
        self._cnt[phys] += by
        return self._cnt[phys]

    def val(self, name):
        return self._cnt[self._resolve(name)]


def build_nc():
    nc = bass.Bass("TRN2", target_bir_lowering=False)
    T = {}

    def din(name, shape, dt=F32):
        T[name] = nc.dram_tensor(name, shape, dt, kind="ExternalInput").ap()

    def dscr(name, shape, dt):
        T[name] = nc.dram_tensor(name, shape, dt, kind="Internal").ap()

    din("x_own", [TOK, D])
    din("x_full", [S, D])
    din("pos_own", [1, TOK])
    din("qrel", [128, NB])
    din("wqn", [D, 2048])
    din("wqr", [D, 1024])
    din("wdkv", [D, LAT])
    din("wkr", [D, DR])
    din("wuk", [LAT, 2048])
    din("wuv", [LAT, 2048])
    din("wo", [D, D])
    din("w1", [HID, D])
    din("w2", [D, HID])
    T["y"] = nc.dram_tensor("y", [TOK, D], F32, kind="ExternalOutput").ap()

    dscr("CKH", [4, 128, S], BF16)
    dscr("CKL", [4, 128, S], BF16)
    dscr("KH", [8, 16, 128, 512], BF16)
    dscr("KL", [8, 16, 128, 512], BF16)
    dscr("VSI", [8, 16, 4, 128, 256], BF16)
    dscr("QNH", [16, NB, 128, 128], BF16)
    dscr("QNL", [16, NB, 128, 128], BF16)
    dscr("QRH", [8, NB, 128, 128], BF16)
    dscr("QRL", [8, NB, 128, 128], BF16)
    dscr("WOH", [D, D], BF16)
    dscr("WOL", [D, D], BF16)
    dscr("W1T", [64, 16, 128, 128], BF16)
    dscr("W2T", [64, 16, 128, 128], BF16)
    dscr("HS", [NB, 128, HID], F32)
    dscr("NHS", [64, NB, 128, 128], BF16)
    dscr("YS", [NB, 128, D], F32)

    with contextlib.ExitStack() as st:
        e = st.enter_context
        blk = e(nc.Block())
        p = Ctx(nc, blk, st)
        p.T = T
        sb = {}

        def alloc(name, shape, dt=F32):
            sb[name] = e(nc.sbuf_tensor(name, shape, dt))

        alloc("identF", [128, 128])
        alloc("identB", [128, 128], BF16)
        alloc("ones1", [1, 128])
        alloc("invfr", [1, 128])
        alloc("posq", [1, TOK])
        alloc("kidx", [128, 512])
        alloc("qrl", [128, NB])
        alloc("sgncol", [128, 1])
        alloc("cst", [128, 2])
        alloc("rsxm", [128, NB])
        alloc("rshm", [128, NB])
        alloc("s1T", [64, 128])
        alloc("s2T", [16, 128])
        alloc("cosq", [128, TOK])
        alloc("sinq", [128, TOK])
        alloc("krH", [128, S], BF16)
        alloc("krL", [128, S], BF16)
        alloc("nfm", [128, 16 * TOK], BF16)
        p.sb = sb

        p.PS = [e(nc.psum_tensor(f"ps{i}", [128, 512], F32)) for i in range(7)]
        p.PB = e(nc.psum_tensor("psb", [128, 1024], BF16))
        p.PBF = p.PB[:, :].bitcast(F32)

        ph_consts(p)
        ph_wprep(p)
        ph_a2(p)
        ph_ackv_akv(p)
        ph_attn(p)
        ph_wo(p)
        ph_ffn(p)

    return nc


def _phase_barrier(p, name, wait_sems):
    blk = p.blk
    p.sem(name)

    def sy_fn(sy):
        for s in wait_sems:
            sy.wait_ge(p.sems[s], p.val(s))
        sy.nop().then_inc(p.sems[name], 1)
    blk.sync(sy_fn)
    tgt = p.add(name, 1)
    for engname in ("gpsimd", "vector", "scalar", "tensor"):
        def e_fn(eng, tgt=tgt, name=name):
            eng.wait_ge(p.sems[name], tgt)
        getattr(blk, engname)(e_fn)


def ph_consts(p):
    blk, sb, T = p.blk, p.sb, p.T
    for nm in ("g0", "ld0", "dv0", "sc0"):
        p.sem(nm)

    def g_fn(g):
        g.memset(sb["identF"][:, :], 0.0)
        g.memset(sb["identB"][:, :], 0.0)
        g.memset(sb["ones1"][:1, :], 1.0)
        g.memset(sb["sgncol"][0:32, :], -1.0)
        g.memset(sb["sgncol"][32:64, :], 1.0)
        g.memset(sb["sgncol"][64:96, :], -1.0)
        g.memset(sb["sgncol"][96:128, :], 1.0)
        g.memset(sb["cst"][:, 0:1], PIH)
        g.memset(sb["cst"][:, 1:2], 1e-5)
        g.drain()
        g.affine_select(out=sb["identF"][:, :], in_=sb["identF"][:, :],
                        compare_op=ALU.not_equal, fill=1.0,
                        base=0, pattern=[[-1, 128]], channel_multiplier=1)
        g.affine_select(out=sb["identB"][:, :], in_=sb["identB"][:, :],
                        compare_op=ALU.not_equal, fill=1.0,
                        base=0, pattern=[[-1, 128]], channel_multiplier=1)
        g.iota(sb["invfr"][:1, :], pattern=[[1, 128]], base=0,
               channel_multiplier=0, allow_small_or_imprecise_dtypes=True)
        g.iota(sb["kidx"][:, :], pattern=[[1, 512]], base=0,
               channel_multiplier=0, allow_small_or_imprecise_dtypes=True)
        g.drain()
        g.nop().then_inc(p.sems["g0"], 1)
    blk.gpsimd(g_fn)
    p.add("g0", 1)

    def sy_fn(sy):
        sy.dma_start(sb["posq"][:1, :], T["pos_own"]).then_inc(
            p.sems["ld0"], 16)
        sy.dma_start(sb["qrl"][:, :], T["qrel"]).then_inc(p.sems["ld0"], 16)
    blk.sync(sy_fn)
    p.add("ld0", 32)

    def v_fn(v):
        v.wait_ge(p.sems["g0"], p.val("g0"))
        for k in range(1, 4):
            v.tensor_copy(sb["invfr"][:1, 32 * k:32 * k + 32],
                          sb["invfr"][:1, 0:32])
        v.drain()
        v.tensor_copy(sb["invfr"][:1, 0:1],
                      sb["invfr"][:1, 0:1]).then_inc(p.sems["dv0"], 1)
    blk.vector(v_fn)
    p.add("dv0", 1)

    def sc_fn(sc):
        sc.wait_ge(p.sems["dv0"], p.val("dv0"))
        sc.activation(sb["invfr"][:1, :], sb["invfr"][:1, :], AF.Exp,
                      scale=-LN2P).then_inc(p.sems["sc0"], 1)
    blk.scalar(sc_fn)
    p.add("sc0", 1)


def ph_wprep(p):
    nc, blk, T, sb = p.nc, p.blk, p.T, p.sb
    PS, PB = p.PS, p.PB
    CH = 2048

    p.begin_scope()
    with contextlib.ExitStack() as ph:
        e = ph.enter_context
        wld = e(nc.sbuf_tensor("wp_ld", [128, 2 * CH], F32))
        tb0 = e(nc.sbuf_tensor("wp_t0", [128, 2 * CH], BF16))
        ab = e(nc.sbuf_tensor("wp_ab", [128, CH], BF16))
        bb = e(nc.sbuf_tensor("wp_bb", [128, CH], BF16))
        scol = e(nc.sbuf_tensor("wp_s", [128, 8], F32))
        s1m = e(nc.sbuf_tensor("wp_s1m", [128, 64], F32))
        s2m = e(nc.sbuf_tensor("wp_s2m", [128, 16], F32))
        tstg = e(nc.sbuf_tensor("wp_tstg", [128, 2 * 1024], BF16))
        wdum = e(nc.sbuf_tensor("wp_dum", [128, CH], BF16))

        plans = [("w1", T["w1"], T["W1T"], HID // 128, D, s1m, True),
                 ("w2", T["w2"], T["W2T"], D // 128, HID, s2m, False)]
        for tag, wsrc, wdst, ntile, ncols, smat, rowmaj in plans:
            nch = ncols // CH
            reload = nch > 1
            npl = nch * (2 if reload else 1)
            for nm in ("l0", "l1", "c", "v", "t", "ev", "s0", "s1", "r"):
                p.sem(tag + nm)
            bs = {nm: p.val(tag + nm) for nm in ("l0", "l1", "c", "v", "t",
                                                 "ev", "s0", "s1", "r")}

            prev_c = (("w1c", p.val("w1c")) if tag == "w2" else None)
            prev_t = (("w1t", p.val("w1t")) if tag == "w2" else None)
            prev_s = ((("w1s0", p.val("w1s0")), ("w1s1", p.val("w1s1")))
                      if tag == "w2" else None)

            def sy_fn(sy, tag=tag, wsrc=wsrc, ntile=ntile, nch=nch,
                      reload=reload, npl=npl, bs=bs, prev_c=prev_c):
                L = 0
                for i in range(ntile):
                    for ph2 in range(2 if reload else 1):
                        for ch in range(nch):
                            if L < 2 and prev_c is not None:
                                sy.wait_ge(p.sems[prev_c[0]], prev_c[1])
                            if L >= 2:
                                sy.wait_ge(p.sems[tag + "c"],
                                           bs["c"] + L - 1)
                            sy.dma_start(
                                wld[:, (L % 2) * CH:(L % 2) * CH
                                    + min(CH, ncols)],
                                wsrc[i * 128:(i + 1) * 128,
                                     ch * CH:ch * CH + min(CH, ncols)]
                            ).then_inc(p.sems[tag + "l" + str(L % 2)], 16)
                            L += 1
            blk.sync(sy_fn)
            for sl_ in range(2):
                nl = len([x for x in range(ntile * npl) if x % 2 == sl_])
                p.add(tag + "l" + str(sl_), 16 * nl)

            def sc_fn(sc, tag=tag, ntile=ntile, nch=nch, ncols=ncols,
                      reload=reload, npl=npl, bs=bs, prev_s=prev_s):

                def emit_evacs(i, gch):
                    for g8 in range(2):
                        gidx = (i * nch + gch) * 2 + g8
                        sc.wait_ge(p.sems[tag + "t"], bs["t"] + gidx + 1)
                        if gidx < 2 and prev_s is not None:
                            sc.wait_ge(p.sems[prev_s[0][0]], prev_s[0][1])
                            sc.wait_ge(p.sems[prev_s[1][0]], prev_s[1][1])
                        if gidx >= 2:
                            sc.wait_ge(p.sems[tag + "s" + str(gidx % 2)],
                                       bs["s" + str(gidx % 2)]
                                       + 16 * (gidx // 2))
                        sc.activation(
                            tstg[:, (gidx % 2) * 1024:
                                 (gidx % 2) * 1024 + 1024],
                            PB[:, :], AF.Identity).then_inc(
                            p.sems[tag + "ev"], 1)

                for i in range(ntile):
                    if i >= 1:
                        sc.wait_ge(p.sems[tag + "v"], bs["v"] + i * nch)
                    for ch in range(nch):
                        L = i * npl + ch
                        sc.wait_ge(p.sems[tag + "l" + str(L % 2)],
                                   bs["l" + str(L % 2)] + 16 * (L // 2 + 1))
                        w_t = wld[:, (L % 2) * CH:(L % 2) * CH
                                  + min(CH, ncols)]
                        if L > 0:
                            sc.drain()
                        ins = sc.activation(wdum[:, 0:min(CH, ncols)], w_t,
                                            AF.Abs,
                                            accum_out=scol[:, ch:ch + 1])
                        if reload:
                            ins.then_inc(p.sems[tag + "c"], 1)
                    sc.drain()
                    sc.nop().then_inc(p.sems[tag + "r"], 1)
                    if i >= 1:
                        emit_evacs(i - 1, nch - 1)
                        for gch in range(nch - 1):
                            emit_evacs(i, gch)
                    elif nch > 1:
                        for gch in range(nch - 1):
                            emit_evacs(0, gch)
                emit_evacs(ntile - 1, nch - 1)
            blk.scalar(sc_fn)
            if reload:
                p.add(tag + "c", ntile * nch)
            p.add(tag + "r", ntile)
            p.add(tag + "ev", ntile * nch * 2)

            def v_fn(v, tag=tag, ntile=ntile, nch=nch, ncols=ncols,
                     reload=reload, npl=npl, smat=smat, bs=bs,
                     prev_t=prev_t):
                L = 0
                C = 0
                for i in range(ntile):
                    L += nch
                    v.wait_ge(p.sems[tag + "r"], bs["r"] + i + 1)
                    if nch > 1:
                        v.tensor_reduce(scol[:, 6:7], scol[:, 0:nch], AX,
                                        ALU.add)
                        v.drain()
                        src_c = scol[:, 6:7]
                    else:
                        src_c = scol[:, 0:1]
                    v.tensor_scalar(scol[:, 3:4], src_c, 0.5 / ncols,
                                    0.5e-5, ALU.mult, ALU.max)
                    v.tensor_scalar(scol[:, 4:5], src_c, -0.5 / ncols,
                                    -0.5e-5, ALU.mult, ALU.min)
                    v.tensor_scalar(smat[:, i:i + 1], src_c, 1.0 / ncols,
                                    1e-5, ALU.mult, ALU.max)
                    v.drain()
                    for ch in range(nch):
                        if reload:
                            v.wait_ge(p.sems[tag + "l" + str(L % 2)],
                                      bs["l" + str(L % 2)]
                                      + 16 * (L // 2 + 1))
                            w_t = wld[:, (L % 2) * CH:(L % 2) * CH + CH]
                        else:
                            w_t = wld[:, ((L - 1) % 2) * CH:
                                      ((L - 1) % 2) * CH + ncols]
                        if C < 2 and prev_t is not None:
                            v.wait_ge(p.sems[prev_t[0]], prev_t[1])
                        if C >= 2:
                            v.wait_ge(p.sems[tag + "t"],
                                      bs["t"] + 2 * (C - 1))
                        cw = CH if reload else ncols
                        if C >= 1:
                            v.drain()
                        v.tensor_scalar(ab[:, 0:cw], w_t, scol[:, 3:4], None,
                                        ALU.is_ge)
                        v.tensor_scalar(bb[:, 0:cw], w_t, scol[:, 4:5],
                                        None, ALU.is_le).then_inc(
                            p.sems[tag + "c"], 1)
                        v.drain()
                        v.tensor_tensor(tb0[:, (C % 2) * CH:(C % 2) * CH
                                            + cw],
                                        ab[:, 0:cw], bb[:, 0:cw],
                                        ALU.subtract).then_inc(
                            p.sems[tag + "v"], 1)
                        if reload:
                            L += 1
                        C += 1
            blk.vector(v_fn)
            p.add(tag + "c", ntile * nch)
            p.add(tag + "v", ntile * nch)

            def t_fn(t, tag=tag, ntile=ntile, nch=nch, bs=bs):
                for i in range(ntile):
                    for gch in range(nch):
                        C = i * nch + gch
                        t.wait_ge(p.sems[tag + "v"], bs["v"] + C + 1)
                        for g8 in range(2):
                            gidx = C * 2 + g8
                            if gidx >= 1:
                                t.wait_ge(p.sems[tag + "ev"],
                                          bs["ev"] + gidx)
                            for k in range(8):
                                c0 = (g8 * 8 + k) * 128
                                ins = t.transpose(
                                    PB[:, k * 128:(k + 1) * 128],
                                    tb0[:, (C % 2) * CH + c0:
                                        (C % 2) * CH + c0 + 128],
                                    sb["identB"][:, :])
                                if k == 7:
                                    ins.then_inc(p.sems[tag + "t"], 1)
            blk.tensor(t_fn)
            p.add(tag + "t", ntile * nch * 2)

            def g_fn(g, tag=tag, wdst=wdst, ntile=ntile, nch=nch, bs=bs,
                     rowmaj=rowmaj):
                for i in range(ntile):
                    for gch in range(nch):
                        for g8 in range(2):
                            gidx = (i * nch + gch) * 2 + g8
                            g.wait_ge(p.sems[tag + "ev"], bs["ev"] + gidx + 1)
                            src_ = tstg[:, (gidx % 2) * 1024:
                                        (gidx % 2) * 1024 + 1024].rearrange(
                                "p (k c) -> p k c", k=8)
                            cb0 = gch * 16 + g8 * 8
                            if rowmaj:
                                dst = wdst[i, cb0:cb0 + 8].rearrange(
                                    "k p c -> p k c")
                            else:
                                dst = wdst[cb0:cb0 + 8, i].rearrange(
                                    "k p c -> p k c")
                            g.dma_start(dst, src_).then_inc(
                                p.sems[tag + "s" + str(gidx % 2)], 16)
            blk.gpsimd(g_fn)
            for sl_ in range(2):
                ng = len([x for x in range(ntile * nch * 2) if x % 2 == sl_])
                p.add(tag + "s" + str(sl_), 16 * ng)
        def t2_fn(t):
            t.wait_ge(p.sems["w1v"], p.val("w1v"))
            t.wait_ge(p.sems["w2v"], p.val("w2v"))
            t.wait_ge(p.sems["w1ev"], p.val("w1ev"))
            t.wait_ge(p.sems["w2ev"], p.val("w2ev"))
            t.transpose(PS[0][0:64, 0:128], s1m[:, :], sb["identF"][:, :])
            t.transpose(PS[1][0:16, 0:128], s2m[:, :], sb["identF"][:, :])
            t.drain()
            t.nop().then_inc(p.sem("wst"), 1)
        blk.tensor(t2_fn)
        p.add("wst", 1)

        def v2_fn(v):
            v.wait_ge(p.sems["wst"], p.val("wst"))
            v.tensor_copy(sb["s1T"][0:64, :], PS[0][0:64, 0:128])
            v.tensor_copy(sb["s2T"][0:16, :], PS[1][0:16, 0:128]).then_inc(
                p.sem("wsv"), 1)
        blk.vector(v2_fn)
        p.add("wsv", 1)

        _phase_barrier(p, "phW", ("w1s0", "w1s1", "w2s0", "w2s1", "wsv"))

    p.begin_scope()
    with contextlib.ExitStack() as ph:
        e = ph.enter_context
        wos = e(nc.sbuf_tensor("wp_wos", [128, 2 * 2 * D], F32))
        for nm in ("wol0", "wol1", "wov", "wog0", "wog1"):
            p.sem(nm)
        bwo = {nm: p.val(nm) for nm in ("wol0", "wol1", "wov", "wog0",
                                        "wog1")}

        def sy3_fn(sy):
            for i in range(16):
                if i >= 2:
                    sy.wait_ge(p.sems["wov"], bwo["wov"] + i - 1)
                    sy.wait_ge(p.sems["wog" + str(i % 2)],
                               bwo["wog" + str(i % 2)] + 32 * (i // 2))
                sy.dma_start(wos[:, (i % 2) * 2 * D:(i % 2) * 2 * D + D],
                             T["wo"][i * 128:(i + 1) * 128, :]).then_inc(
                    p.sems["wol" + str(i % 2)], 16)
        blk.sync(sy3_fn)
        p.add("wol0", 16 * 8)
        p.add("wol1", 16 * 8)

        def v3_fn(v):
            for i in range(16):
                o = (i % 2) * 2 * D
                v.wait_ge(p.sems["wol" + str(i % 2)],
                          bwo["wol" + str(i % 2)] + 16 * (i // 2 + 1))
                hi = wos[:, o + D:o + 2 * D].bitcast(BF16)[:, 0:D]
                lo = wos[:, o + D:o + 2 * D].bitcast(BF16)[:, D:2 * D]
                v.tensor_copy(hi, wos[:, o:o + D])
                v.drain()
                v.tensor_tensor(lo, wos[:, o:o + D], hi,
                                ALU.subtract).then_inc(p.sems["wov"], 1)
        blk.vector(v3_fn)
        p.add("wov", 16)

        def g3_fn(g):
            for i in range(16):
                o = (i % 2) * 2 * D
                g.wait_ge(p.sems["wov"], bwo["wov"] + i + 1)
                hi = wos[:, o + D:o + 2 * D].bitcast(BF16)[:, 0:D]
                lo = wos[:, o + D:o + 2 * D].bitcast(BF16)[:, D:2 * D]
                g.dma_start(T["WOH"][i * 128:(i + 1) * 128, :],
                            hi).then_inc(p.sems["wog" + str(i % 2)], 16)
                g.dma_start(T["WOL"][i * 128:(i + 1) * 128, :],
                            lo).then_inc(p.sems["wog" + str(i % 2)], 16)
        blk.gpsimd(g3_fn)
        p.add("wog0", 32 * 8)
        p.add("wog1", 32 * 8)

        _phase_barrier(p, "phWS", ("wog0", "wog1"))


def ph_a2(p):
    nc, blk, T, sb = p.nc, p.blk, p.T, p.sb
    PS, PB, PBF = p.PS, p.PB, p.PBF

    p.begin_scope()
    with contextlib.ExitStack() as ph:
        e = ph.enter_context
        xt = e(nc.sbuf_tensor("a2_x", [128, 2 * D], F32))
        xn = e(nc.sbuf_tensor("a2_xn", [128, 2 * D], F32))
        sq = e(nc.sbuf_tensor("a2_sq", [128, D // 2], F32))
        cc = e(nc.sbuf_tensor("a2_cc", [128, 32], F32))
        pq = e(nc.sbuf_tensor("a2_p", [128, D], F32))
        c8 = e(nc.sbuf_tensor("a2_c8", [128, D], I8))
        qb1 = e(nc.sbuf_tensor("a2_b1", [128, D], BF16))
        qb2 = e(nc.sbuf_tensor("a2_b2", [128, D], BF16))
        sd = e(nc.sbuf_tensor("a2_sd", [128, D], BF16))
        ntm = e(nc.sbuf_tensor("a2_ntm", [128, D], BF16))
        xfh = e(nc.sbuf_tensor("a2_xfh", [128, 16 * 512], BF16))
        xfl = e(nc.sbuf_tensor("a2_xfl", [128, 16 * 512], BF16))
        rs = e(nc.sbuf_tensor("a2_rs", [128, 5 * 512], F32))
        rc16 = e(nc.sbuf_tensor("a2_rc16", [128, 512], I16))
        wqt = e(nc.sbuf_tensor("a2_wqt", [128, 2 * 512], F32))
        wqh = e(nc.sbuf_tensor("a2_wqh", [128, 2 * 512], BF16))
        wql = e(nc.sbuf_tensor("a2_wql", [128, 2 * 512], BF16))
        qstg = e(nc.sbuf_tensor("a2_qstg", [128, 4 * 512], F32))
        qhib = e(nc.sbuf_tensor("a2_qhib", [128, 4 * 512], BF16))
        qlob = e(nc.sbuf_tensor("a2_qlob", [128, 4 * 512], BF16))
        rsw = e(nc.sbuf_tensor("a2_rsw", [128, 512], F32))
        cosq = e(nc.sbuf_tensor("a2_cosq", [128, TOK], F32))
        sinq = e(nc.sbuf_tensor("a2_sinq", [128, TOK], F32))

        for nm in ("rtm", "rts", "rtv", "lnv", "lns", "a2l0", "a2l1",
                   "a2sx", "qsc", "qdv", "a2t", "a2en", "a2ex", "a2fin",
                   "qpl0", "qpl1", "qpv", "qpm", "qpdc", "qpec", "qpsw",
                   "qprp", "qphl", "qps", "qpe"):
            p.sem(nm)

        base_rtm = p.val("rtm")
        base_rts = p.val("rts")
        base_qps = p.val("qps")
        base_qphl = p.val("qphl")

        def t_tab(t):
            t.wait_ge(p.sems["sc0"], p.val("sc0"))
            t.wait_ge(p.sems["ld0"], p.val("ld0"))
            for half in range(2):
                if half == 1:
                    t.wait_ge(p.sems["rts"], base_rts + 1)
                t.matmul(PS[6][:, 0:512], sb["invfr"][:1, :],
                         sb["posq"][:1, half * 512:(half + 1) * 512],
                         start=True, stop=True).then_inc(p.sems["rtm"], 1)
        blk.tensor(t_tab)
        p.add("rtm", 2)

        tabs = []
        for half in range(2):
            t1 = p.add("rts", 1)
            t2 = p.add("rtv", 1)
            t3 = p.add("rts", 1)
            t4 = p.add("rtv", 1)
            tabs.append((t1, t2, t3, t4))

        def sc_tab(sc):
            for half in range(2):
                t1, t2, t3, t4 = tabs[half]
                cos_o = cosq[:, half * 512:(half + 1) * 512]
                sin_o = sinq[:, half * 512:(half + 1) * 512]
                sc.wait_ge(p.sems["rtm"], base_rtm + half + 1)
                sc.activation(rs[:, 0:512], PS[6][:, 0:512], AF.Identity)
                sc.drain()
                sc.activation(rs[:, 512:1024], PS[6][:, 0:512], AF.Identity,
                              scale=INV2PI).then_inc(p.sems["rts"], 1)
                sc.wait_ge(p.sems["rtv"], t2)
                sc.activation(sin_o, rs[:, 0:512], AF.Sin)
                sc.drain()
                sc.activation(rs[:, 512:1024], rs[:, 0:512], AF.Abs)
                sc.drain()
                sc.activation(cos_o, rs[:, 512:1024], AF.Sin, scale=-1.0,
                              bias=sb["cst"][:, 0:1]).then_inc(p.sems["rts"], 1)
        blk.scalar(sc_tab)

        def v_tab(v):
            for half in range(2):
                t1, t2, t3, t4 = tabs[half]
                sin_o = sinq[:, half * 512:(half + 1) * 512]
                v.wait_ge(p.sems["rts"], t1)
                v.tensor_copy(rc16[:, 0:512], rs[:, 512:1024])
                v.drain()
                b1r = rs[:, 1024:1536].bitcast(BF16)[:, 0:512]
                b2r = rs[:, 1536:2048].bitcast(BF16)[:, 0:512]
                v.scalar_tensor_tensor(b1r, rs[:, 512:1024], -0.5,
                                       rc16[:, 0:512], ALU.add, ALU.is_ge)
                v.scalar_tensor_tensor(b2r, rs[:, 512:1024], 0.5,
                                       rc16[:, 0:512], ALU.add, ALU.is_le)
                v.drain()
                v.tensor_tensor(rs[:, 2048:2560], b1r, b2r, ALU.subtract)
                v.drain()
                v.tensor_tensor(rs[:, 2048:2560], rs[:, 2048:2560],
                                rc16[:, 0:512], ALU.add)
                v.drain()
                v.scalar_tensor_tensor(rs[:, 0:512], rs[:, 2048:2560], -CW1,
                                       rs[:, 0:512], ALU.mult, ALU.add)
                v.drain()
                v.scalar_tensor_tensor(rs[:, 0:512], rs[:, 2048:2560], -CW2,
                                       rs[:, 0:512], ALU.mult, ALU.add)
                v.drain()
                v.tensor_scalar(rs[:, 0:512], rs[:, 0:512], math.pi,
                                -math.pi, ALU.min, ALU.max).then_inc(
                    p.sems["rtv"], 1)
                v.wait_ge(p.sems["rts"], t3)
                v.tensor_scalar(sin_o, sin_o, sb["sgncol"][:, 0:1], None,
                                ALU.mult).then_inc(p.sems["rtv"], 1)
        blk.vector(v_tab)

        for half in range(2):
            bplan = []
            for tbl in range(4):
                tb = half * 4 + tbl
                d = {"tb": tb, "tbl": tbl}
                d["ld"] = p.add("a2l" + str(tb % 2), 16)
                d["ln1"] = p.add("lnv", 1)
                d["ln2"] = p.add("lns", 1)
                d["ln3"] = p.add("lnv", 1)
                d["ln4"] = p.add("lns", 1)
                d["sx"] = p.add("a2sx", 1)
                d["q1"] = p.add("qsc", 1)
                d["q2"] = p.add("qdv", 1)
                d["tt"] = [p.add("a2t", 1) for _ in range(6)]
                d["en"] = [p.add("a2en", 1) for _ in range(2)]
                d["ex"] = [p.add("a2ex", 1) for _ in range(4)]
                d["fin"] = p.add("a2fin", 1)
                bplan.append(d)
            allb = getattr(p, "_a2_allb", [])
            allb.extend(bplan)
            p._a2_allb = allb

            def sy_fn(sy, bplan=bplan, allb=list(allb)):
                for d in bplan:
                    tb = d["tb"]
                    if tb >= 2:
                        sy.wait_ge(p.sems["a2fin"], allb[tb - 2]["fin"])
                    sy.dma_start(
                        xt[:, (tb % 2) * D:(tb % 2) * D + D],
                        T["x_own"][tb * 128:(tb + 1) * 128, :]).then_inc(
                        p.sems["a2l" + str(tb % 2)], 16)
            blk.sync(sy_fn)

            def v_fn(v, bplan=bplan, allb=list(allb), half=half):
                for d in bplan:
                    tb, tbl = d["tb"], d["tbl"]
                    sl = (tb % 2) * D
                    co = (tb % 2) * 16
                    v.wait_ge(p.sems["a2l" + str(tb % 2)], d["ld"])
                    if tb >= 2:
                        v.wait_ge(p.sems["lns"], allb[tb - 2]["ln4"])
                    v.reduce_sum(cc[:, co:co + 1], xt[:, sl:sl + D], AX)
                    v.drain()
                    v.tensor_scalar(cc[:, co + 1:co + 2], cc[:, co:co + 1],
                                    1.0 / D, None, ALU.mult)
                    v.drain()
                    v.tensor_scalar(cc[:, co + 2:co + 3],
                                    cc[:, co + 1:co + 2], -1.0, None,
                                    ALU.mult).then_inc(p.sems["lnv"], 1)
                    v.wait_ge(p.sems["lns"], d["ln2"])
                    v.tensor_scalar(cc[:, co + 6:co + 7],
                                    cc[:, co + 1:co + 2],
                                    cc[:, co + 5:co + 6], -1.0, ALU.mult,
                                    ALU.mult).then_inc(p.sems["lnv"], 1)
                    v.wait_ge(p.sems["lns"], d["ln4"])
                    v.tensor_reduce(cc[:, co + 8:co + 9], xn[:, sl:sl + D],
                                    AX, ALU.max, apply_absolute_value=True)
                    v.drain()
                    v.tensor_scalar(cc[:, co + 9:co + 10],
                                    cc[:, co + 8:co + 9], 1e-5, None,
                                    ALU.max)
                    v.drain()
                    v.reciprocal(cc[:, co + 10:co + 11],
                                 cc[:, co + 9:co + 10])
                    v.drain()
                    v.tensor_scalar(cc[:, co + 11:co + 12],
                                    cc[:, co + 10:co + 11], 127.0, None,
                                    ALU.mult)
                    v.drain()
                    v.tensor_scalar(sb["rsxm"][:, tb:tb + 1],
                                    cc[:, co + 9:co + 10], 1.0 / 127.0, None,
                                    ALU.mult).then_inc(p.sems["a2sx"], 1)
                    v.wait_ge(p.sems["qsc"], d["q1"])
                    if tb >= 1:
                        v.wait_ge(p.sems["a2t"], allb[tb - 1]["tt"][1])
                    v.scalar_tensor_tensor(qb1[:, :], pq[:, :], -0.5,
                                           c8[:, :], ALU.add, ALU.is_ge)
                    v.scalar_tensor_tensor(qb2[:, :], pq[:, :], 0.5,
                                           c8[:, :], ALU.add, ALU.is_le)
                    v.drain()
                    v.tensor_tensor(sd[:, :], qb1[:, :], qb2[:, :],
                                    ALU.subtract)
                    v.drain()
                    v.tensor_tensor(ntm[:, :], sd[:, :], c8[:, :],
                                    ALU.add).then_inc(p.sems["qdv"], 1)
                    for g8 in range(2):
                        v.wait_ge(p.sems["a2t"], d["tt"][g8])
                        for k in range(8):
                            dc = g8 * 8 + k
                            v.tensor_copy(
                                sb["nfm"][:, dc * TOK + tb * 128:
                                          dc * TOK + tb * 128 + 128],
                                PB[:, k * 128:(k + 1) * 128])
                        v.drain()
                        v.memset(cc[:, 15:16], 0.0).then_inc(
                            p.sems["a2en"], 1)
                    for g4 in range(4):
                        v.wait_ge(p.sems["a2t"], d["tt"][2 + g4])
                        for k in range(4):
                            dc = g4 * 4 + k
                            col = dc * 512 + tbl * 128
                            v.tensor_copy(xfh[:, col:col + 128],
                                          PS[k][:, 0:128])
                            v.drain()
                            v.tensor_tensor(xfl[:, col:col + 128],
                                            PS[k][:, 0:128],
                                            xfh[:, col:col + 128],
                                            ALU.subtract)
                        v.drain()
                        v.memset(cc[:, 15:16], 0.0).then_inc(
                            p.sems["a2ex"], 1)
                    v.drain()
                    v.memset(cc[:, 15:16], 0.0).then_inc(
                        p.sems["a2fin"], 1)
            blk.vector(v_fn)

            def sc_fn(sc, bplan=bplan, allb=list(allb)):
                for d in bplan:
                    tb = d["tb"]
                    sl = (tb % 2) * D
                    co = (tb % 2) * 16
                    sc.wait_ge(p.sems["lnv"], d["ln1"])
                    sc.activation(sq[:, :], xt[:, sl:sl + D // 2], AF.Square,
                                  bias=cc[:, co + 2:co + 3],
                                  accum_out=cc[:, co + 3:co + 4])
                    sc.drain()
                    sc.activation(sq[:, :], xt[:, sl + D // 2:sl + D],
                                  AF.Square, bias=cc[:, co + 2:co + 3],
                                  accum_out=cc[:, co + 7:co + 8])
                    sc.drain()
                    sc.activation(cc[:, co + 3:co + 4],
                                  cc[:, co + 3:co + 4], AF.Identity,
                                  bias=cc[:, co + 7:co + 8])
                    sc.drain()
                    sc.activation(cc[:, co + 4:co + 5],
                                  cc[:, co + 3:co + 4], AF.Ln, bias=sb["cst"][:, 1:2],
                                  scale=1.0 / D)
                    sc.drain()
                    sc.activation(cc[:, co + 5:co + 6],
                                  cc[:, co + 4:co + 5], AF.Exp,
                                  scale=-0.5).then_inc(p.sems["lns"], 1)
                    sc.wait_ge(p.sems["lnv"], d["ln3"])
                    sc.activation(xn[:, sl:sl + D], xt[:, sl:sl + D],
                                  AF.Identity, bias=cc[:, co + 6:co + 7],
                                  scale=cc[:, co + 5:co + 6]).then_inc(
                        p.sems["lns"], 1)
                    sc.wait_ge(p.sems["a2sx"], d["sx"])
                    if tb >= 1:
                        sc.wait_ge(p.sems["qdv"], allb[tb - 1]["q2"])
                    sc.activation(pq[:, :], xn[:, sl:sl + D], AF.Identity,
                                  scale=cc[:, co + 11:co + 12])
                    sc.drain()
                    sc.activation(c8[:, :], xn[:, sl:sl + D], AF.Identity,
                                  scale=cc[:, co + 11:co + 12]).then_inc(
                        p.sems["qsc"], 1)
            blk.scalar(sc_fn)

            def t_fn(t, bplan=bplan, allb=list(allb)):
                for d in bplan:
                    tb = d["tb"]
                    sl = (tb % 2) * D
                    t.wait_ge(p.sems["qdv"], d["q2"])
                    for g8 in range(2):
                        gprev = tb * 2 + g8 - 1
                        if gprev >= 0:
                            t.wait_ge(p.sems["a2en"],
                                      allb[gprev // 2]["en"][gprev % 2])
                        for k in range(8):
                            dc = g8 * 8 + k
                            ins = t.transpose(PB[:, k * 128:(k + 1) * 128],
                                              ntm[:, dc * 128:(dc + 1) * 128],
                                              sb["identB"][:, :])
                            if k == 7:
                                ins.then_inc(p.sems["a2t"], 1)
                    t.wait_ge(p.sems["lns"], d["ln4"])
                    for g4 in range(4):
                        gprev = tb * 4 + g4 - 1
                        if gprev >= 0:
                            t.wait_ge(p.sems["a2ex"],
                                      allb[gprev // 4]["ex"][gprev % 4])
                        for k in range(4):
                            dc = g4 * 4 + k
                            ins = t.transpose(
                                PS[k][:, 0:128],
                                xn[:, sl + dc * 128:sl + (dc + 1) * 128],
                                sb["identF"][:, :])
                            if k == 3:
                                ins.then_inc(p.sems["a2t"], 1)
            blk.tensor(t_fn)

            qplan = []
            for qfg in range(6):
                for dc in range(16):
                    idx = qfg * 16 + dc
                    qplan.append({"qfg": qfg, "dc": dc,
                                  "ld": p.add("qpl" + str(idx % 2), 16),
                                  "v": p.add("qpv", 1)})
            qm = [p.add("qpm", 1) for _ in range(6)]

            def syq_fn(sy, qplan=qplan, half=half):
                for it in qplan:
                    qfg, dc = it["qfg"], it["dc"]
                    rope = qfg >= 4
                    wsrc = T["wqr"] if rope else T["wqn"]
                    csrc = (qfg - 4) * 512 if rope else qfg * 512
                    idx = qfg * 16 + dc
                    if idx >= 2:
                        sy.wait_ge(p.sems["qpv"], qplan[idx - 2]["v"])
                    sy.dma_start(
                        wqt[:, (idx % 2) * 512:(idx % 2) * 512 + 512],
                        wsrc[dc * 128:(dc + 1) * 128,
                             csrc:csrc + 512]).then_inc(
                        p.sems["qpl" + str(idx % 2)], 16)
            blk.sync(syq_fn)

            base_qpdc = p.val("qpdc")
            p.add("qpdc", 96)

            def mk_split(it):
                def f(v, it=it):
                    idx = it["qfg"] * 16 + it["dc"]
                    o = (idx % 2) * 512
                    v.wait_ge(p.sems["qpl" + str(idx % 2)], it["ld"])
                    if idx >= 2:
                        v.wait_ge(p.sems["qpdc"], base_qpdc + idx - 1)
                    v.tensor_copy(wqh[:, o:o + 512], wqt[:, o:o + 512])
                    v.drain()
                    v.tensor_tensor(wql[:, o:o + 512], wqt[:, o:o + 512],
                                    wqh[:, o:o + 512],
                                    ALU.subtract).then_inc(p.sems["qpv"], 1)
                return f

            base_qpe = p.val("qpe")

            def tq_fn(t, qplan=qplan, bplan=bplan, base_qpe=base_qpe):
                t.wait_ge(p.sems["a2fin"], bplan[3]["fin"])
                for qfg in range(6):
                    for dc in range(16):
                        idx = qfg * 16 + dc
                        o = (idx % 2) * 512
                        t.wait_ge(p.sems["qpv"], qplan[idx]["v"])
                        if dc == 0 and (qfg > 0 or base_qpe > 0):
                            t.wait_ge(p.sems["qpe"], base_qpe + qfg)
                        for qi in range(4):
                            bank = PS[qi][:, 0:512]
                            lo = dc * 512
                            ws = slice(o + qi * 128, o + qi * 128 + 128)
                            t.matmul(bank, wqh[:, ws], xfh[:, lo:lo + 512],
                                     start=(dc == 0), stop=False)
                            t.matmul(bank, wqh[:, ws], xfl[:, lo:lo + 512],
                                     start=False, stop=False)
                            ins = t.matmul(bank, wql[:, ws],
                                           xfh[:, lo:lo + 512],
                                           start=False, stop=(dc == 15))
                            if qi == 3:
                                ins.then_inc(p.sems["qpdc"], 1)
                    t.drain()
                    t.nop().then_inc(p.sems["qpm"], 1)
            blk.tensor(tq_fn)

            sc_steps, v_steps, g_steps = [], [], []
            for qfg in range(6):
                rope = qfg >= 4
                for dc in range(16):
                    v_steps.append(mk_split(qplan[qfg * 16 + dc]))
                for qi in range(4):
                    bank = PS[qi][:, 0:512]
                    stg = qstg[:, qi * 512:qi * 512 + 512]
                    hib = qhib[:, qi * 512:qi * 512 + 512]
                    lob = qlob[:, qi * 512:qi * 512 + 512]
                    qfc = (qfg - 4) * 4 + qi if rope else qfg * 4 + qi
                    gq = half * 6 + qfg
                    tec = p.add("qpec", 1)
                    sc_steps.append(lambda sc, bank=bank, stg=stg,
                                    tm=qm[qfg], prev_hl=base_qphl + 4 * gq,
                                    gq=gq: (
                        sc.wait_ge(p.sems["qpm"], tm),
                        (sc.wait_ge(p.sems["qphl"], prev_hl)
                         if gq >= 1 else None),
                        sc.activation(stg, bank, AF.Identity).then_inc(
                            p.sems["qpec"], 1)))
                    if rope:
                        prev_rp = p.val("qprp")
                        tsw = p.add("qpsw", 64)
                        g_steps.append(lambda g, stg=stg, tec=tec,
                                       prev_rp=prev_rp: (
                            g.wait_ge(p.sems["qpec"], tec),
                            (g.wait_ge(p.sems["qprp"], prev_rp)
                             if prev_rp > 0 else None),
                            g.dma_start(rsw[0:32, :],
                                        stg[32:64, :]).then_inc(
                                p.sems["qpsw"], 16),
                            g.dma_start(rsw[32:64, :],
                                        stg[0:32, :]).then_inc(
                                p.sems["qpsw"], 16),
                            g.dma_start(rsw[64:96, :],
                                        stg[96:128, :]).then_inc(
                                p.sems["qpsw"], 16),
                            g.dma_start(rsw[96:128, :],
                                        stg[64:96, :]).then_inc(
                                p.sems["qpsw"], 16)))
                        trp = p.add("qprp", 1)
                        v_steps.append(lambda v, stg=stg, half=half,
                                       tsw=tsw: (
                            v.wait_ge(p.sems["qpsw"], tsw),
                            v.tensor_tensor(
                                stg, stg,
                                cosq[:, half * 512:half * 512 + 512],
                                ALU.mult),
                            v.tensor_tensor(
                                rsw[:, :], rsw[:, :],
                                sinq[:, half * 512:half * 512 + 512],
                                ALU.mult),
                            v.drain(),
                            v.tensor_tensor(stg, stg, rsw[:, :],
                                            ALU.add).then_inc(
                                p.sems["qprp"], 1)))
                        rdy = ("qprp", trp)
                    else:
                        rdy = ("qpec", tec)
                    thl = p.add("qphl", 1)
                    v_steps.append(lambda v, stg=stg, hib=hib, lob=lob,
                                   rdy=rdy, gq=gq, qi=qi: (
                        v.wait_ge(p.sems[rdy[0]], rdy[1]),
                        (v.wait_ge(p.sems["qps"], base_qps + 128 * 4 * gq)
                         if gq >= 1 and qi == 0 else None),
                        v.tensor_copy(hib, stg),
                        v.drain(),
                        v.tensor_tensor(lob, stg, hib,
                                        ALU.subtract).then_inc(
                            p.sems["qphl"], 1)))
                    tst = p.add("qps", 128)
                    g_steps.append(lambda g, qfc=qfc, half=half, rope=rope,
                                   thl=thl, hib=hib, lob=lob: (
                        g.wait_ge(p.sems["qphl"], thl),
                        [g.dma_start(
                            (T["QRH"] if rope else T["QNH"])[qfc,
                                                             half * 4 + sbk],
                            hib.rearrange("p (s c) -> p s c", s=4)[:, sbk]
                        ).then_inc(p.sems["qps"], 16) for sbk in range(4)],
                        [g.dma_start(
                            (T["QRL"] if rope else T["QNL"])[qfc,
                                                             half * 4 + sbk],
                            lob.rearrange("p (s c) -> p s c", s=4)[:, sbk]
                        ).then_inc(p.sems["qps"], 16) for sbk in range(4)]))
                tqe = p.add("qpe", 1)
                v_steps.append(lambda v, tqe=tqe: (
                    v.drain(),
                    v.memset(cc[:, 15:16], 0.0).then_inc(
                        p.sems["qpe"], 1)))

            blk.scalar(lambda sc, steps=sc_steps: [f(sc) for f in steps])
            blk.vector(lambda v, steps=v_steps: [f(v) for f in steps])
            blk.gpsimd(lambda g, steps=g_steps: [f(g) for f in steps])

        _phase_barrier(p, "phA2", ("qps", "a2fin", "qpe"))


def ph_ackv_akv(p):
    nc, blk, T, sb = p.nc, p.blk, p.T, p.sb
    PS, PB = p.PS, p.PB

    p.begin_scope()
    with contextlib.ExitStack() as ph:
        e = ph.enter_context
        xt = e(nc.sbuf_tensor("ck_x", [128, 2 * D], F32))
        xn = e(nc.sbuf_tensor("ck_xn", [128, 2 * D], F32))
        sq = e(nc.sbuf_tensor("ck_sq", [128, D // 2], F32))
        cc = e(nc.sbuf_tensor("ck_cc", [128, 32], F32))
        xfh = e(nc.sbuf_tensor("ck_xfh", [128, 16 * 512], BF16))
        xfl = e(nc.sbuf_tensor("ck_xfl", [128, 16 * 512], BF16))
        wdh = e(nc.sbuf_tensor("ck_wdh", [128, 16 * LAT], BF16))
        wdl = e(nc.sbuf_tensor("ck_wdl", [128, 16 * LAT], BF16))
        wrh = e(nc.sbuf_tensor("ck_wrh", [128, 16 * DR], BF16))
        wrl = e(nc.sbuf_tensor("ck_wrl", [128, 16 * DR], BF16))
        wtmp = e(nc.sbuf_tensor("ck_wtmp", [128, LAT + DR], F32))
        ckhs = e(nc.sbuf_tensor("ck_ckhs", [128, 4 * 512], BF16))
        ckls = e(nc.sbuf_tensor("ck_ckls", [128, 4 * 512], BF16))
        rs = e(nc.sbuf_tensor("ck_rs", [128, 5 * 512], F32))
        rc16 = e(nc.sbuf_tensor("ck_rc16", [128, 512], I16))
        ctab = e(nc.sbuf_tensor("ck_ctab", [128, 512], F32))
        stab = e(nc.sbuf_tensor("ck_stab", [128, 512], F32))
        krr = e(nc.sbuf_tensor("ck_krr", [128, 512], F32))
        krw = e(nc.sbuf_tensor("ck_krw", [128, 512], F32))
        posg = e(nc.sbuf_tensor("ck_posg", [1, 512], F32))

        for nm in ("ckwl", "ckwv", "ckl0", "ckl1", "ckt", "ckev", "ckm",
                   "ckec", "cklo", "cksw", "ckkr", "ckst", "ckfin", "ckdup",
                   "ckpg"):
            p.sem(nm)

        bw = {"l": p.val("ckwl"), "v": p.val("ckwv")}

        def syw(sy):
            for dc in range(16):
                if dc >= 1:
                    sy.wait_ge(p.sems["ckwv"], bw["v"] + dc)
                sy.dma_start(wtmp[:, 0:LAT],
                             T["wdkv"][dc * 128:(dc + 1) * 128, :]).then_inc(
                    p.sems["ckwl"], 16)
                sy.dma_start(wtmp[:, LAT:LAT + DR],
                             T["wkr"][dc * 128:(dc + 1) * 128, :]).then_inc(
                    p.sems["ckwl"], 16)
        blk.sync(syw)
        p.add("ckwl", 32 * 16)

        def vw(v):
            for dc in range(16):
                v.wait_ge(p.sems["ckwl"], bw["l"] + 32 * (dc + 1))
                v.tensor_copy(wdh[:, dc * LAT:(dc + 1) * LAT], wtmp[:, 0:LAT])
                v.tensor_copy(wrh[:, dc * DR:(dc + 1) * DR],
                              wtmp[:, LAT:LAT + DR])
                v.drain()
                v.tensor_tensor(wdl[:, dc * LAT:(dc + 1) * LAT],
                                wtmp[:, 0:LAT],
                                wdh[:, dc * LAT:(dc + 1) * LAT], ALU.subtract)
                v.tensor_tensor(wrl[:, dc * DR:(dc + 1) * DR],
                                wtmp[:, LAT:LAT + DR],
                                wrh[:, dc * DR:(dc + 1) * DR],
                                ALU.subtract).then_inc(p.sems["ckwv"], 1)
        blk.vector(vw)
        p.add("ckwv", 16)

        gplan = []
        for g in range(8):
            d = {"g": g, "ld": [], "ln": [], "tt": [], "ev": []}
            for bi in range(4):
                d["ld"].append(p.add("ckl" + str((g * 4 + bi) % 2), 16))
                d["ln"].append(tuple(p.add(nm, 1) for nm in
                                     ("lnv", "lns", "lnv", "lns")))
                d["tt"].append([p.add("ckt", 1) for _ in range(4)])
                d["ev"].append([p.add("ckev", 1) for _ in range(4)])
            d["pg"] = p.add("ckpg", 1)
            d["tabm"] = p.add("ckm", 1)
            d["rt1"] = p.add("rts", 1)
            d["rt2"] = p.add("rtv", 1)
            d["rt3"] = p.add("rts", 1)
            d["rt4"] = p.add("rtv", 1)
            d["mm"] = p.add("ckm", 1)
            d["ec"] = p.add("ckec", 1)
            d["lo"] = p.add("cklo", 1)
            d["st"] = p.add("ckst", 128)
            d["sw"] = p.add("cksw", 32)
            d["kr"] = p.add("ckkr", 1)
            d["fin"] = p.add("ckfin", 1)
            gplan.append(d)

        def sy_fn(sy):
            for d in gplan:
                g = d["g"]
                for bi in range(4):
                    gb = g * 4 + bi
                    if gb >= 2:
                        pg, pb = divmod(gb - 2, 4)
                        sy.wait_ge(p.sems["lns"], gplan[pg]["ln"][pb][3])
                    sy.dma_start(
                        xt[:, (gb % 2) * D:(gb % 2) * D + D],
                        T["x_full"][gb * 128:(gb + 1) * 128, :]).then_inc(
                        p.sems["ckl" + str(gb % 2)], 16)
        blk.sync(sy_fn)

        def v_fn(v):
            for d in gplan:
                g = d["g"]
                for bi in range(4):
                    gb = g * 4 + bi
                    co = (gb % 2) * 16
                    sl = (gb % 2) * D
                    v.wait_ge(p.sems["ckl" + str((g * 4 + bi) % 2)],
                              d["ld"][bi])
                    if gb >= 2:
                        pg, pb = divmod(gb - 2, 4)
                        v.wait_ge(p.sems["lns"], gplan[pg]["ln"][pb][3])
                    v.reduce_sum(cc[:, co:co + 1], xt[:, sl:sl + D], AX)
                    v.drain()
                    v.tensor_scalar(cc[:, co + 1:co + 2], cc[:, co:co + 1],
                                    1.0 / D, None, ALU.mult)
                    v.drain()
                    v.tensor_scalar(cc[:, co + 2:co + 3],
                                    cc[:, co + 1:co + 2], -1.0, None,
                                    ALU.mult).then_inc(p.sems["lnv"], 1)
                    v.wait_ge(p.sems["lns"], d["ln"][bi][1])
                    v.tensor_scalar(cc[:, co + 6:co + 7],
                                    cc[:, co + 1:co + 2],
                                    cc[:, co + 5:co + 6], -1.0, ALU.mult,
                                    ALU.mult).then_inc(p.sems["lnv"], 1)
                    for g4 in range(4):
                        v.wait_ge(p.sems["ckt"], d["tt"][bi][g4])
                        if g >= 1 and bi == 0 and g4 == 0:
                            v.wait_ge(p.sems["ckm"], gplan[g - 1]["mm"])
                        for k in range(4):
                            dc = g4 * 4 + k
                            col = dc * 512 + bi * 128
                            v.tensor_copy(xfh[:, col:col + 128],
                                          PS[k][:, 0:128])
                            v.drain()
                            v.tensor_tensor(xfl[:, col:col + 128],
                                            PS[k][:, 0:128],
                                            xfh[:, col:col + 128],
                                            ALU.subtract)
                        v.drain()
                        v.memset(cc[:, 15:16], 0.0).then_inc(
                            p.sems["ckev"], 1)
                v.wait_ge(p.sems["rts"], d["rt1"])
                v.tensor_copy(rc16[:, 0:512], rs[:, 512:1024])
                v.drain()
                b1r = rs[:, 1024:1536].bitcast(BF16)[:, 0:512]
                b2r = rs[:, 1536:2048].bitcast(BF16)[:, 0:512]
                v.scalar_tensor_tensor(b1r, rs[:, 512:1024], -0.5,
                                       rc16[:, 0:512], ALU.add, ALU.is_ge)
                v.scalar_tensor_tensor(b2r, rs[:, 512:1024], 0.5,
                                       rc16[:, 0:512], ALU.add, ALU.is_le)
                v.drain()
                v.tensor_tensor(rs[:, 2048:2560], b1r, b2r, ALU.subtract)
                v.drain()
                v.tensor_tensor(rs[:, 2048:2560], rs[:, 2048:2560],
                                rc16[:, 0:512], ALU.add)
                v.drain()
                v.scalar_tensor_tensor(rs[:, 0:512], rs[:, 2048:2560], -CW1,
                                       rs[:, 0:512], ALU.mult, ALU.add)
                v.drain()
                v.scalar_tensor_tensor(rs[:, 0:512], rs[:, 2048:2560], -CW2,
                                       rs[:, 0:512], ALU.mult, ALU.add)
                v.drain()
                v.tensor_scalar(rs[:, 0:512], rs[:, 0:512], math.pi,
                                -math.pi, ALU.min, ALU.max).then_inc(
                    p.sems["rtv"], 1)
                v.wait_ge(p.sems["rts"], d["rt3"])
                v.tensor_scalar(stab[:, :], stab[:, :], sb["sgncol"][:, 0:1],
                                None, ALU.mult).then_inc(p.sems["rtv"], 1)
                v.wait_ge(p.sems["ckec"], d["ec"])
                for lc in range(4):
                    v.scalar_tensor_tensor(
                        ckls[:, lc * 512:(lc + 1) * 512], PS[lc][:, 0:512],
                        1.0, ckhs[:, lc * 512:(lc + 1) * 512],
                        ALU.mult, ALU.subtract)
                v.drain()
                v.memset(cc[:, 15:16], 0.0).then_inc(
                    p.sems["cklo"], 1)
                v.wait_ge(p.sems["cksw"], d["sw"])
                v.tensor_tensor(krr[0:64, :], krr[0:64, :], ctab[0:64, :],
                                ALU.mult)
                v.tensor_tensor(krw[0:64, :], krw[0:64, :], stab[0:64, :],
                                ALU.mult)
                v.drain()
                v.tensor_tensor(krr[0:64, :], krr[0:64, :], krw[0:64, :],
                                ALU.add)
                v.drain()
                v.tensor_copy(sb["krH"][0:64, g * 512:(g + 1) * 512],
                              krr[0:64, :])
                v.drain()
                v.tensor_tensor(sb["krL"][0:64, g * 512:(g + 1) * 512],
                                krr[0:64, :],
                                sb["krH"][0:64, g * 512:(g + 1) * 512],
                                ALU.subtract).then_inc(p.sems["ckkr"], 1)
                v.drain()
                v.memset(cc[:, 15:16], 0.0).then_inc(
                    p.sems["ckfin"], 1)
        blk.vector(v_fn)

        def sc_fn(sc):
            for d in gplan:
                g = d["g"]
                for bi in range(4):
                    gb = g * 4 + bi
                    co = (gb % 2) * 16
                    sl = (gb % 2) * D
                    sc.wait_ge(p.sems["lnv"], d["ln"][bi][0])
                    sc.activation(sq[:, :], xt[:, sl:sl + D // 2], AF.Square,
                                  bias=cc[:, co + 2:co + 3],
                                  accum_out=cc[:, co + 3:co + 4])
                    sc.drain()
                    sc.activation(sq[:, :], xt[:, sl + D // 2:sl + D],
                                  AF.Square, bias=cc[:, co + 2:co + 3],
                                  accum_out=cc[:, co + 7:co + 8])
                    sc.drain()
                    sc.activation(cc[:, co + 3:co + 4],
                                  cc[:, co + 3:co + 4], AF.Identity,
                                  bias=cc[:, co + 7:co + 8])
                    sc.drain()
                    sc.activation(cc[:, co + 4:co + 5],
                                  cc[:, co + 3:co + 4], AF.Ln, bias=sb["cst"][:, 1:2],
                                  scale=1.0 / D)
                    sc.drain()
                    sc.activation(cc[:, co + 5:co + 6],
                                  cc[:, co + 4:co + 5], AF.Exp,
                                  scale=-0.5).then_inc(p.sems["lns"], 1)
                    sc.wait_ge(p.sems["lnv"], d["ln"][bi][2])
                    if gb >= 2:
                        pg, pb = divmod(gb - 2, 4)
                        sc.wait_ge(p.sems["ckt"], gplan[pg]["tt"][pb][3])
                    sc.activation(xn[:, sl:sl + D], xt[:, sl:sl + D],
                                  AF.Identity, bias=cc[:, co + 6:co + 7],
                                  scale=cc[:, co + 5:co + 6]).then_inc(
                        p.sems["lns"], 1)
                sc.wait_ge(p.sems["ckm"], d["tabm"])
                sc.activation(rs[:, 0:512], PS[4][:, 0:512], AF.Identity)
                sc.drain()
                sc.activation(rs[:, 512:1024], PS[4][:, 0:512], AF.Identity,
                              scale=INV2PI).then_inc(p.sems["rts"], 1)
                sc.wait_ge(p.sems["rtv"], d["rt2"])
                sc.activation(stab[:, :], rs[:, 0:512], AF.Sin)
                sc.drain()
                sc.activation(rs[:, 512:1024], rs[:, 0:512], AF.Abs)
                sc.drain()
                sc.activation(ctab[:, :], rs[:, 512:1024], AF.Sin, scale=-1.0,
                              bias=sb["cst"][:, 0:1]).then_inc(p.sems["rts"], 1)
                sc.wait_ge(p.sems["ckm"], d["mm"])
                if g >= 1:
                    sc.wait_ge(p.sems["ckst"], gplan[g - 1]["st"])
                    sc.wait_ge(p.sems["ckkr"], gplan[g - 1]["kr"])
                for lc in range(4):
                    sc.activation(ckhs[:, lc * 512:(lc + 1) * 512],
                                  PS[lc][:, 0:512], AF.Identity)
                sc.drain()
                sc.activation(krr[0:64, :], PS[5][0:64, 0:512],
                              AF.Identity).then_inc(p.sems["ckec"], 1)
        blk.scalar(sc_fn)

        def t_fn(t):
            t.wait_ge(p.sems["ckwv"], p.val("ckwv"))
            for d in gplan:
                g = d["g"]
                if g >= 1:
                    t.wait_ge(p.sems["cklo"], gplan[g - 1]["lo"])
                    t.wait_ge(p.sems["ckec"], gplan[g - 1]["ec"])
                for bi in range(4):
                    gb = g * 4 + bi
                    t.wait_ge(p.sems["lns"], d["ln"][bi][3])
                    for g4 in range(4):
                        gid = gb * 4 + g4
                        if gid >= 1:
                            pg2, rem = divmod(gid - 1, 16)
                            pbi, pg4 = divmod(rem, 4)
                            t.wait_ge(p.sems["ckev"],
                                      gplan[pg2]["ev"][pbi][pg4])
                        for k in range(4):
                            dc = g4 * 4 + k
                            ins = t.transpose(
                                PS[k][:, 0:128],
                                xn[:, (gb % 2) * D + dc * 128:
                                   (gb % 2) * D + dc * 128 + 128],
                                sb["identF"][:, :])
                            if k == 3:
                                ins.then_inc(p.sems["ckt"], 1)
                t.wait_ge(p.sems["ckpg"], d["pg"])
                if g >= 1:
                    t.wait_ge(p.sems["rts"], gplan[g - 1]["rt3"])
                t.matmul(PS[4][:, 0:512], sb["invfr"][:1, :], posg[:1, :],
                         start=True, stop=True).then_inc(p.sems["ckm"], 1)
                t.wait_ge(p.sems["ckev"], d["ev"][3][3])
                for dc in range(16):
                    for lc in range(4):
                        w1s = slice(dc * LAT + lc * 128,
                                    dc * LAT + lc * 128 + 128)
                        t.matmul(PS[lc][:, 0:512], wdh[:, w1s],
                                 xfh[:, dc * 512:(dc + 1) * 512],
                                 start=(dc == 0), stop=False)
                        t.matmul(PS[lc][:, 0:512], wdh[:, w1s],
                                 xfl[:, dc * 512:(dc + 1) * 512],
                                 start=False, stop=False)
                        t.matmul(PS[lc][:, 0:512], wdl[:, w1s],
                                 xfh[:, dc * 512:(dc + 1) * 512],
                                 start=False, stop=(dc == 15))
                    wrs = slice(dc * DR, (dc + 1) * DR)
                    t.matmul(PS[5][0:64, 0:512], wrh[:, wrs],
                             xfh[:, dc * 512:(dc + 1) * 512],
                             start=(dc == 0), stop=False)
                    t.matmul(PS[5][0:64, 0:512], wrh[:, wrs],
                             xfl[:, dc * 512:(dc + 1) * 512],
                             start=False, stop=False)
                    ins = t.matmul(PS[5][0:64, 0:512], wrl[:, wrs],
                                   xfh[:, dc * 512:(dc + 1) * 512],
                                   start=False, stop=(dc == 15))
                    if dc == 15:
                        ins.then_inc(p.sems["ckm"], 1)
        blk.tensor(t_fn)

        def g_fn(g_):
            for d in gplan:
                g = d["g"]
                if g >= 1:
                    g_.wait_ge(p.sems["ckm"], gplan[g - 1]["tabm"])
                g_.iota(posg[:1, :], pattern=[[1, 512]], base=g * 512,
                        channel_multiplier=0,
                        allow_small_or_imprecise_dtypes=True)
                g_.drain()
                g_.nop().then_inc(p.sems["ckpg"], 1)
                g_.wait_ge(p.sems["ckec"], d["ec"])
                g_.dma_start(krw[0:32, :], krr[32:64, :]).then_inc(
                    p.sems["cksw"], 16)
                g_.dma_start(krw[32:64, :], krr[0:32, :]).then_inc(
                    p.sems["cksw"], 16)
                g_.wait_ge(p.sems["cklo"], d["lo"])
                for lc in range(4):
                    g_.dma_start(T["CKH"][lc, :, g * 512:(g + 1) * 512],
                                 ckhs[:, lc * 512:(lc + 1) * 512]).then_inc(
                        p.sems["ckst"], 16)
                    g_.dma_start(T["CKL"][lc, :, g * 512:(g + 1) * 512],
                                 ckls[:, lc * 512:(lc + 1) * 512]).then_inc(
                        p.sems["ckst"], 16)
        blk.gpsimd(g_fn)

        def gdup(g):
            g.wait_ge(p.sems["ckkr"], p.val("ckkr"))
            g.dma_start(sb["krH"][64:128, :],
                        sb["krH"][0:64, :]).then_inc(p.sems["ckdup"], 16)
            g.dma_start(sb["krL"][64:128, :],
                        sb["krL"][0:64, :]).then_inc(p.sems["ckdup"], 16)
        blk.gpsimd(gdup)
        p.add("ckdup", 32)

        _phase_barrier(p, "phCK", ("ckfin", "ckst", "ckdup"))

    p.begin_scope()
    with contextlib.ExitStack() as ph:
        e = ph.enter_context
        wukh = e(nc.sbuf_tensor("kv_wukh", [128, 4 * 2048], BF16))
        wukl = e(nc.sbuf_tensor("kv_wukl", [128, 4 * 2048], BF16))
        wuvh = e(nc.sbuf_tensor("kv_wuvh", [128, 4 * 2048], BF16))
        wuvl = e(nc.sbuf_tensor("kv_wuvl", [128, 4 * 2048], BF16))
        wtm2 = e(nc.sbuf_tensor("kv_wtmp", [128, 2 * 2048], F32))
        ckh = e(nc.sbuf_tensor("kv_ckh", [128, 3 * 2048], BF16))
        ckl = e(nc.sbuf_tensor("kv_ckl", [128, 3 * 2048], BF16))
        kstg = e(nc.sbuf_tensor("kv_kstg", [128, 4 * 512], BF16))
        kstl = e(nc.sbuf_tensor("kv_kstl", [128, 4 * 512], BF16))
        vstg = e(nc.sbuf_tensor("kv_vstg", [128, 4 * 512], BF16))
        vstl = e(nc.sbuf_tensor("kv_vstl", [128, 4 * 512], BF16))

        for nm in ("kvwl", "kvwv", "kvcl0", "kvcl1", "kvcl2", "kvm",
                   "kvhe", "kvle",
                   "kvks0", "kvks1", "kvks2", "kvks3",
                   "kvvm", "kvve", "kvvl", "kvvs0", "kvvs1", "kvvs2", "kvvs3",
                   "krs"):
            p.sem(nm)
        bkv = {nm: p.val(nm) for nm in ("kvwl", "kvwv")}

        def syw2(sy):
            for lc in range(4):
                if lc >= 1:
                    sy.wait_ge(p.sems["kvwv"], bkv["kvwv"] + lc)
                sy.dma_start(wtm2[:, 0:2048],
                             T["wuk"][lc * 128:(lc + 1) * 128, :]).then_inc(
                    p.sems["kvwl"], 16)
                sy.dma_start(wtm2[:, 2048:4096],
                             T["wuv"][lc * 128:(lc + 1) * 128, :]).then_inc(
                    p.sems["kvwl"], 16)
        blk.sync(syw2)
        p.add("kvwl", 32 * 4)

        def vw2(v):
            for lc in range(4):
                v.wait_ge(p.sems["kvwl"], bkv["kvwl"] + 32 * (lc + 1))
                v.tensor_copy(wukh[:, lc * 2048:(lc + 1) * 2048],
                              wtm2[:, 0:2048])
                v.tensor_copy(wuvh[:, lc * 2048:(lc + 1) * 2048],
                              wtm2[:, 2048:4096])
                v.drain()
                v.tensor_tensor(wukl[:, lc * 2048:(lc + 1) * 2048],
                                wtm2[:, 0:2048],
                                wukh[:, lc * 2048:(lc + 1) * 2048],
                                ALU.subtract)
                v.tensor_tensor(wuvl[:, lc * 2048:(lc + 1) * 2048],
                                wtm2[:, 2048:4096],
                                wuvh[:, lc * 2048:(lc + 1) * 2048],
                                ALU.subtract).then_inc(p.sems["kvwv"], 1)
        blk.vector(vw2)
        p.add("kvwv", 4)

        kgplan = []
        for kg in range(8):
            d = {"kg": kg}
            d["cl"] = p.add("kvcl" + str(kg % 3), 128)
            d["k"] = [p.add("kvm", 1) for _ in range(16)]
            d["khe"] = [p.add("kvhe", 1) for _ in range(16)]
            d["kle"] = [p.add("kvle", 1) for _ in range(16)]
            d["kst"] = [p.add("kvks" + str((kg * 16 + j) % 4), 32)
                        for j in range(16)]
            d["v"] = [p.add("kvvm", 1) for _ in range(16)]
            d["ve"] = [p.add("kvve", 1) for _ in range(16)]
            d["vl"] = [p.add("kvvl", 1) for _ in range(16)]
            d["vst"] = [p.add("kvvs" + str((kg * 16 + j) % 4), 32)
                        for j in range(16)]
            kgplan.append(d)

        def syc(sy):
            for d in kgplan:
                kg = d["kg"]
                if kg >= 3:
                    sy.wait_ge(p.sems["kvvm"], kgplan[kg - 3]["v"][15])
                o = (kg % 3) * 2048
                for lc in range(4):
                    sy.dma_start(ckh[:, o + lc * 512:o + (lc + 1) * 512],
                                 T["CKH"][lc, :, kg * 512:(kg + 1) * 512]
                                 ).then_inc(p.sems["kvcl" + str(kg % 3)], 16)
                    sy.dma_start(ckl[:, o + lc * 512:o + (lc + 1) * 512],
                                 T["CKL"][lc, :, kg * 512:(kg + 1) * 512]
                                 ).then_inc(p.sems["kvcl" + str(kg % 3)], 16)
        blk.sync(syc)

        def t_kv(t):
            t.wait_ge(p.sems["kvwv"], p.val("kvwv"))
            for d in kgplan:
                kg = d["kg"]
                o = (kg % 3) * 2048
                t.wait_ge(p.sems["kvcl" + str(kg % 3)], d["cl"])
                for kfc in range(16):
                    i = kg * 16 + kfc
                    bank = PS[i % 4]
                    if i >= 4:
                        pi = i - 4
                        t.wait_ge(p.sems["kvle"],
                                  kgplan[pi // 16]["kle"][pi % 16])
                    for lc in range(4):
                        ws = slice(lc * 2048 + kfc * 128,
                                   lc * 2048 + kfc * 128 + 128)
                        cs = slice(o + lc * 512, o + (lc + 1) * 512)
                        t.matmul(bank[:, 0:512], wukh[:, ws], ckh[:, cs],
                                 start=(lc == 0), stop=False)
                        t.matmul(bank[:, 0:512], wukh[:, ws], ckl[:, cs],
                                 start=False, stop=False)
                        ins = t.matmul(bank[:, 0:512], wukl[:, ws],
                                       ckh[:, cs],
                                       start=False, stop=(lc == 3))
                        if lc == 3:
                            ins.then_inc(p.sems["kvm"], 1)
                for tb4 in range(4):
                    for vfc in range(4):
                        i = kg * 16 + tb4 * 4 + vfc
                        bank = PS[4 + i % 3]
                        if i >= 3:
                            pi = i - 3
                            t.wait_ge(p.sems["kvvl"],
                                      kgplan[pi // 16]["vl"][pi % 16])
                        for lc in range(4):
                            cs = slice(o + lc * 512 + tb4 * 128,
                                       o + lc * 512 + tb4 * 128 + 128)
                            ws = slice(lc * 2048 + vfc * 512,
                                       lc * 2048 + vfc * 512 + 512)
                            t.matmul(bank[:, 0:512], ckh[:, cs], wuvh[:, ws],
                                     start=(lc == 0), stop=False)
                            t.matmul(bank[:, 0:512], ckh[:, cs], wuvl[:, ws],
                                     start=False, stop=False)
                            ins = t.matmul(bank[:, 0:512], ckl[:, cs],
                                           wuvh[:, ws],
                                           start=False, stop=(lc == 3))
                            if lc == 3:
                                ins.then_inc(p.sems["kvvm"], 1)
        blk.tensor(t_kv)

        def sc_kv(sc):
            for d in kgplan:
                kg = d["kg"]
                for kfc in range(16):
                    i = kg * 16 + kfc
                    bank = PS[i % 4]
                    sc.wait_ge(p.sems["kvm"], d["k"][kfc])
                    if i >= 4:
                        pi = i - 4
                        sc.wait_ge(p.sems["kvks" + str(pi % 4)],
                                   kgplan[pi // 16]["kst"][pi % 16])
                    sc.activation(kstg[:, (i % 4) * 512:(i % 4) * 512 + 512],
                                  bank[:, 0:512], AF.Identity).then_inc(
                        p.sems["kvhe"], 1)
                for tb4 in range(4):
                    for vfc in range(4):
                        i = kg * 16 + tb4 * 4 + vfc
                        bank = PS[4 + i % 3]
                        sc.wait_ge(p.sems["kvvm"], d["v"][tb4 * 4 + vfc])
                        if i >= 4:
                            pi = i - 4
                            sc.wait_ge(p.sems["kvvs" + str(pi % 4)],
                                       kgplan[pi // 16]["vst"][pi % 16])
                        sc.activation(
                            vstg[:, (i % 4) * 512:(i % 4) * 512 + 512],
                            bank[:, 0:512], AF.Identity).then_inc(
                            p.sems["kvve"], 1)
        blk.scalar(sc_kv)

        def v_kv(v):
            for d in kgplan:
                kg = d["kg"]
                for kfc in range(16):
                    i = kg * 16 + kfc
                    bank = PS[i % 4]
                    v.wait_ge(p.sems["kvhe"], d["khe"][kfc])
                    v.scalar_tensor_tensor(
                        kstl[:, (i % 4) * 512:(i % 4) * 512 + 512],
                        bank[:, 0:512], 1.0,
                        kstg[:, (i % 4) * 512:(i % 4) * 512 + 512],
                        ALU.mult, ALU.subtract).then_inc(p.sems["kvle"], 1)
                for j in range(16):
                    i = kg * 16 + j
                    bank = PS[4 + i % 3]
                    v.wait_ge(p.sems["kvve"], d["ve"][j])
                    if i >= 4:
                        pi = i - 4
                        v.wait_ge(p.sems["kvvs" + str(pi % 4)],
                                  kgplan[pi // 16]["vst"][pi % 16])
                    v.scalar_tensor_tensor(
                        vstl[:, (i % 4) * 512:(i % 4) * 512 + 512],
                        bank[:, 0:512], 1.0,
                        vstg[:, (i % 4) * 512:(i % 4) * 512 + 512],
                        ALU.mult, ALU.subtract).then_inc(p.sems["kvvl"], 1)
        blk.vector(v_kv)

        def g_kv(g):
            for d in kgplan:
                kg = d["kg"]
                for kfc in range(16):
                    i = kg * 16 + kfc
                    g.wait_ge(p.sems["kvle"], d["kle"][kfc])
                    g.dma_start(T["KH"][kg, kfc],
                                kstg[:, (i % 4) * 512:(i % 4) * 512 + 512]
                                ).then_inc(p.sems["kvks" + str(i % 4)], 16)
                    g.dma_start(T["KL"][kg, kfc],
                                kstl[:, (i % 4) * 512:(i % 4) * 512 + 512]
                                ).then_inc(p.sems["kvks" + str(i % 4)], 16)
                for tb4 in range(4):
                    for vfc in range(4):
                        i = kg * 16 + tb4 * 4 + vfc
                        g.wait_ge(p.sems["kvvl"], d["vl"][tb4 * 4 + vfc])
                        for src_b, coff in ((vstg, 0), (vstl, 128)):
                            dap = bass.AP(
                                T["VSI"].tensor,
                                ((kg * 16 + vfc * 4) * 4 + tb4) * 128 * 256
                                + coff,
                                [[256, 128], [4 * 128 * 256, 4], [1, 128]])
                            g.dma_start(
                                dap,
                                src_b[:, (i % 4) * 512:(i % 4) * 512 + 512]
                                .rearrange("p (h c) -> p h c", h=4)
                            ).then_inc(p.sems["kvvs" + str(i % 4)], 16)
        blk.gpsimd(g_kv)

        _phase_barrier(p, "phKV", ("kvks0", "kvks1", "kvks2", "kvks3",
                                   "kvvs0", "kvvs1", "kvvs2", "kvvs3"))


def ph_attn(p):
    nc, blk, T, sb = p.nc, p.blk, p.T, p.sb
    PS, PB, PBF = p.PS, p.PB, p.PBF

    p.attn_stack = contextlib.ExitStack()
    ea = p.attn_stack.enter_context
    afh = ea(nc.sbuf_tensor("at_afh", [128, 16 * TOK], BF16))
    afl = ea(nc.sbuf_tensor("at_afl", [128, 16 * TOK], BF16))
    p.afh, p.afl = afh, afl

    p.begin_scope()
    with contextlib.ExitStack() as ph:
        e = ph.enter_context
        qn = e(nc.sbuf_tensor("at_qn", [128, 2 * 512], BF16))
        qr = e(nc.sbuf_tensor("at_qr", [128, 2 * 512], BF16))
        kbuf = e(nc.sbuf_tensor("at_k", [128, 4 * 2048], BF16))
        vb = e(nc.sbuf_tensor("at_vb", [128, 4 * 2048], BF16))
        pbuf = e(nc.sbuf_tensor("at_p", [128, 2 * 1024], F32))
        pbh = e(nc.sbuf_tensor("at_pbh", [128, 2 * 1024], BF16))
        pbl = e(nc.sbuf_tensor("at_pbl", [128, 2 * 1024], BF16))
        pth = e(nc.sbuf_tensor("at_pth", [128, 2 * 1024], BF16))
        ptl = e(nc.sbuf_tensor("at_ptl", [128, 2 * 1024], BF16))
        lmat = e(nc.sbuf_tensor("at_l", [128, 16], F32))
        rlc = e(nc.sbuf_tensor("at_rl", [128, 4], F32))
        msk = e(nc.sbuf_tensor("at_msk", [128, 512], F32))
        ofold = e(nc.sbuf_tensor("at_of", [128, 256], F32))
        aohi = e(nc.sbuf_tensor("at_aoh", [128, 256], BF16))
        aolo = e(nc.sbuf_tensor("at_aol", [128, 256], BF16))

        PS4B = PS[4][:, :].bitcast(BF16)
        PS5B = PS[5][:, :].bitcast(BF16)

        for nm in ("aQ0", "aQ1", "aK0", "aK1", "aK2", "aK3", "aSC", "aEX",
                   "aEH", "aPL", "aTR", "aTV", "aAV", "aMK", "aRL", "aFD",
                   "aHI", "aLO", "aTA", "aAE", "aQZ"):
            p.sem(nm)

        def g_z(g):
            for qs2 in (0, 1):
                for hh2 in (0, 1):
                    co = qs2 * 512 + hh2 * 256 + 128
                    ins = g.memset(qr[64:128, co:co + 128], 0.0)
                    if qs2 == 1 and hh2 == 1:
                        ins.then_inc(p.sems["aQZ"], 1)
        blk.gpsimd(g_z)
        p.add("aQZ", 1)

        slots = [(s, hp) for s in range(1, 9) for hp in range(8)]
        ck_of = {}
        n = 0
        for s, hp in slots:
            for kc in range(s):
                ck_of[(s, hp, kc)] = n
                n += 1
        NCK = n

        for nm in ("aV0", "aV1", "aV2", "aV3"):
            p.sem(nm)
        base = {nm: p.val(nm) for nm in ("aQ0", "aQ1", "aK0", "aK1", "aK2",
                                         "aK3", "aV0", "aV1", "aV2", "aV3",
                                         "aSC", "aEX", "aEH", "aPL", "aTR",
                                         "aTV", "aAV", "aMK", "aRL", "aFD",
                                         "aHI", "aLO", "aTA", "aAE")}
        qtot = {0: base["aQ0"], 1: base["aQ1"]}
        ktot = {i: base["aK" + str(i)] for i in range(4)}
        vtot = {i: base["aV" + str(i)] for i in range(4)}
        q_tgt = {}
        k_tgt = {}
        v_tgt = {}
        for i, (s, hp) in enumerate(slots):
            qs = i % 2
            qtot[qs] += 160
            q_tgt[(s, hp)] = (qs, qtot[qs])
            for kc in range(s):
                ck = ck_of[(s, hp, kc)]
                sl4 = ck % 4
                ktot[sl4] += 32
                k_tgt[(s, hp, kc)] = (sl4, ktot[sl4])
                vtot[sl4] += 16
                v_tgt[(s, hp, kc)] = (sl4, vtot[sl4])
        sc_after = {}
        ex_after = {}
        av_after = {}
        nn = 0
        for i, (s, hp) in enumerate(slots):
            nn += s
            sc_after[i] = base["aSC"] + nn
            ex_after[i] = base["aEX"] + nn
            av_after[i] = base["aAV"] + nn

        def A(name, k):
            return base[name] + k

        def sy_fn(sy):
            for i, (s, hp) in enumerate(slots):
                h1 = 2 * hp
                for kc in range(s):
                    ck = ck_of[(s, hp, kc)]
                    sl4, kv_ = k_tgt[(s, hp, kc)]
                    ks = "aK" + str(sl4)
                    if ck >= 4:
                        sy.wait_ge(p.sems["aSC"], A("aSC", ck - 3))
                    o = sl4 * 2048
                    sy.dma_start(
                        kbuf[:, o:o + 1024].rearrange("p (h c) -> p h c",
                                                      h=2),
                        T["KH"][kc, h1:h1 + 2].rearrange("h p c -> p h c")
                    ).then_inc(p.sems[ks], 16)
                    sy.dma_start(
                        kbuf[:, o + 1024:o + 2048].rearrange(
                            "p (h c) -> p h c", h=2),
                        T["KL"][kc, h1:h1 + 2].rearrange("h p c -> p h c")
                    ).then_inc(p.sems[ks], 16)
        blk.sync(sy_fn)
        for sl4 in range(4):
            p.add("aK" + str(sl4), ktot[sl4] - base["aK" + str(sl4)])

        def g_fn(g):
            for i, (s, hp) in enumerate(slots):
                qs, qv = q_tgt[(s, hp)]
                h1 = 2 * hp
                if i >= 2:
                    g.wait_ge(p.sems["aSC"], sc_after[i - 2])
                qo = qs * 512
                ro = qs * 512
                g.dma_start(
                    qn[:, qo:qo + 128], T["QNH"][h1, s - 1]).then_inc(
                    p.sems["aQ" + str(qs)], 16)
                g.dma_start(
                    qn[:, qo + 128:qo + 256], T["QNL"][h1, s - 1]).then_inc(
                    p.sems["aQ" + str(qs)], 16)
                g.dma_start(
                    qn[:, qo + 256:qo + 384], T["QNH"][h1 + 1,
                                                       s - 1]).then_inc(
                    p.sems["aQ" + str(qs)], 16)
                g.dma_start(
                    qn[:, qo + 384:qo + 512], T["QNL"][h1 + 1,
                                                       s - 1]).then_inc(
                    p.sems["aQ" + str(qs)], 16)
                for hh in range(2):
                    co = ro + hh * 256
                    rr = slice(hh * 64, hh * 64 + 64)
                    g.dma_start(qr[0:64, co:co + 128],
                                T["QRH"][hp, s - 1][rr]).then_inc(
                        p.sems["aQ" + str(qs)], 16)
                    g.dma_start(qr[64:128, co:co + 128],
                                T["QRL"][hp, s - 1][rr]).then_inc(
                        p.sems["aQ" + str(qs)], 16)
                    g.dma_start(qr[0:64, co + 128:co + 256],
                                T["QRH"][hp, s - 1][rr]).then_inc(
                        p.sems["aQ" + str(qs)], 16)
                for kc in range(s):
                    ck = ck_of[(s, hp, kc)]
                    sl4, vv_ = v_tgt[(s, hp, kc)]
                    if ck >= 4:
                        g.wait_ge(p.sems["aAV"], A("aAV", ck - 3))
                    g.dma_start(
                        vb[:, sl4 * 2048:(sl4 + 1) * 2048].rearrange(
                            "p (h kb c) -> p h kb c", h=2, kb=4),
                        T["VSI"][kc, h1:h1 + 2].rearrange(
                            "h kb p c -> p h kb c")
                    ).then_inc(p.sems["aV" + str(sl4)], 16)
        blk.gpsimd(g_fn)
        for qs in (0, 1):
            p.add("aQ" + str(qs), qtot[qs] - base["aQ" + str(qs)])
        for sl4 in range(4):
            p.add("aV" + str(sl4), vtot[sl4] - base["aV" + str(sl4)])

        def v_fn(v):
            for i, (s, hp) in enumerate(slots):
                if hp == 0:
                    v.tensor_scalar(msk[:, :], sb["kidx"][:, :],
                                    sb["qrl"][:, s - 1:s], 0.0, ALU.subtract,
                                    ALU.is_gt)
                    v.drain()
                    v.tensor_scalar(msk[:, :], msk[:, :], -1e9, None,
                                    ALU.mult)
                    v.drain()

                def subs(kc):
                    ck = ck_of[(s, hp, kc)]
                    po = (ck % 2) * 1024
                    v.wait_ge(p.sems["aEH"], A("aEH", ck + 1))
                    v.tensor_tensor(pbl[:, po:po + 512], pbuf[:, po:po + 512],
                                    pbh[:, po:po + 512], ALU.subtract)
                    v.tensor_tensor(pbl[:, po + 512:po + 1024],
                                    pbuf[:, po + 512:po + 1024],
                                    pbh[:, po + 512:po + 1024],
                                    ALU.subtract).then_inc(p.sems["aPL"], 1)

                def evac(kc):
                    ck = ck_of[(s, hp, kc)]
                    po = (ck % 2) * 1024
                    v.wait_ge(p.sems["aTR"], A("aTR", ck + 1))
                    if ck >= 2:
                        v.wait_ge(p.sems["aAV"], A("aAV", ck - 1))
                    v.tensor_copy(pth[:, po:po + 1024], PS4B)
                    v.tensor_copy(ptl[:, po:po + 1024],
                                  PS5B).then_inc(p.sems["aTV"], 1)

                for kc in range(s):
                    if kc == s - 1:
                        ck = ck_of[(s, hp, kc)]
                        v.wait_ge(p.sems["aSC"], A("aSC", ck + 1))
                        bs1 = PS[0 + (kc % 2)]
                        bs2 = PS[2 + (kc % 2)]
                        v.tensor_tensor(bs1[:, 0:512], bs1[:, 0:512],
                                        msk[:, :], ALU.add)
                        v.tensor_tensor(bs2[:, 0:512], bs2[:, 0:512],
                                        msk[:, :],
                                        ALU.add).then_inc(p.sems["aMK"], 1)
                    subs(kc)
                    if kc >= 1:
                        evac(kc - 1)
                evac(s - 1)

                v.wait_ge(p.sems["aEX"], ex_after[i])
                for hh in range(2):
                    v.tensor_reduce(rlc[:, hh:hh + 1],
                                    lmat[:, hh * 8:hh * 8 + s], AX, ALU.add)
                v.drain()
                v.reciprocal(rlc[:, 0:1], rlc[:, 0:1])
                v.reciprocal(rlc[:, 1:2], rlc[:, 1:2]).then_inc(
                    p.sems["aRL"], 1)
                v.wait_ge(p.sems["aAV"], av_after[i])
                v.tensor_copy(ofold[:, 0:128], PS[6][:, 128:256])
                v.tensor_copy(ofold[:, 128:256], PS[6][:, 384:512])
                v.drain()
                v.tensor_tensor(ofold[:, 0:128], PS[6][:, 0:128],
                                ofold[:, 0:128], ALU.add)
                v.tensor_tensor(ofold[:, 128:256], PS[6][:, 256:384],
                                ofold[:, 128:256],
                                ALU.add).then_inc(p.sems["aFD"], 1)
                v.wait_ge(p.sems["aHI"], A("aHI", i + 1))
                v.scalar_tensor_tensor(aolo[:, 0:128], ofold[:, 0:128],
                                       rlc[:, 0:1], aohi[:, 0:128],
                                       ALU.mult, ALU.subtract)
                v.scalar_tensor_tensor(aolo[:, 128:256], ofold[:, 128:256],
                                       rlc[:, 1:2], aohi[:, 128:256],
                                       ALU.mult, ALU.subtract).then_inc(
                    p.sems["aLO"], 1)
                v.wait_ge(p.sems["aTA"], A("aTA", i + 1))
                h1, h2 = 2 * hp, 2 * hp + 1
                c1 = h1 * TOK + (s - 1) * 128
                c2 = h2 * TOK + (s - 1) * 128
                v.tensor_copy(afh[:, c1:c1 + 128], PB[:, 0:128])
                v.tensor_copy(afl[:, c1:c1 + 128], PB[:, 128:256])
                v.tensor_copy(afh[:, c2:c2 + 128], PB[:, 256:384])
                v.tensor_copy(afl[:, c2:c2 + 128],
                              PB[:, 384:512]).then_inc(p.sems["aAE"], 1)
        blk.vector(v_fn)
        p.add("aPL", NCK)
        p.add("aTV", NCK)
        p.add("aMK", 64)
        p.add("aRL", 64)
        p.add("aFD", 64)
        p.add("aLO", 64)
        p.add("aAE", 64)

        def sc_fn(sc):
            for i, (s, hp) in enumerate(slots):
                for kc in range(s):
                    ck = ck_of[(s, hp, kc)]
                    po = (ck % 2) * 1024
                    sc.wait_ge(p.sems["aSC"], A("aSC", ck + 1))
                    if kc == s - 1:
                        sc.wait_ge(p.sems["aMK"], A("aMK", i + 1))
                    if ck >= 2:
                        sc.wait_ge(p.sems["aPL"], A("aPL", ck - 1))
                    sc.activation(pbuf[:, po:po + 512],
                                  PS[0 + (kc % 2)][:, 0:512], AF.Exp,
                                  scale=SCALE,
                                  accum_out=lmat[:, kc:kc + 1])
                    sc.activation(pbuf[:, po + 512:po + 1024],
                                  PS[2 + (kc % 2)][:, 0:512], AF.Exp,
                                  scale=SCALE,
                                  accum_out=lmat[:, 8 + kc:9 + kc]
                                  ).then_inc(p.sems["aEX"], 1)
                    if ck >= 2:
                        sc.wait_ge(p.sems["aTR"], A("aTR", ck - 1))
                    sc.activation(pbh[:, po:po + 512],
                                  PS[0 + (kc % 2)][:, 0:512], AF.Exp,
                                  scale=SCALE)
                    sc.activation(pbh[:, po + 512:po + 1024],
                                  PS[2 + (kc % 2)][:, 0:512], AF.Exp,
                                  scale=SCALE).then_inc(p.sems["aEH"], 1)
                sc.wait_ge(p.sems["aRL"], A("aRL", i + 1))
                sc.wait_ge(p.sems["aFD"], A("aFD", i + 1))
                if i >= 1:
                    sc.wait_ge(p.sems["aTA"], A("aTA", i))
                sc.activation(aohi[:, 0:128], ofold[:, 0:128], AF.Identity,
                              scale=rlc[:, 0:1])
                sc.activation(aohi[:, 128:256], ofold[:, 128:256],
                              AF.Identity,
                              scale=rlc[:, 1:2]).then_inc(p.sems["aHI"], 1)
        blk.scalar(sc_fn)
        p.add("aEX", NCK)
        p.add("aEH", NCK)
        p.add("aHI", 64)

        def t_fn(t):
            t.wait_ge(p.sems["phKV"], p.val("phKV"))
            t.wait_ge(p.sems["aQZ"], p.val("aQZ"))
            for i, (s, hp) in enumerate(slots):
                qs, qv = q_tgt[(s, hp)]
                t.wait_ge(p.sems["aQ" + str(qs)], qv)
                for step in range(s + 2):
                    if step < s:
                        kc = step
                        ck = ck_of[(s, hp, kc)]
                        sl4, kv_ = k_tgt[(s, hp, kc)]
                        t.wait_ge(p.sems["aK" + str(sl4)], kv_)
                        if ck >= 2:
                            t.wait_ge(p.sems["aEH"], A("aEH", ck - 1))
                        o = sl4 * 2048
                        qo = qs * 512
                        ro = qs * 512
                        krh = sb["krH"][:, kc * 512:(kc + 1) * 512]
                        krl = sb["krL"][:, kc * 512:(kc + 1) * 512]
                        for hh in range(2):
                            bank = PS[(0 if hh == 0 else 2) + (kc % 2)]
                            kh_s = kbuf[:, o + hh * 512:o + hh * 512 + 512]
                            kl_s = kbuf[:, o + 1024 + hh * 512:
                                        o + 1024 + hh * 512 + 512]
                            qnh = qn[:, qo + hh * 256:qo + hh * 256 + 128]
                            qnl = qn[:, qo + hh * 256 + 128:
                                     qo + hh * 256 + 256]
                            qrs = qr[:, ro + hh * 256:ro + hh * 256 + 128]
                            qr2 = qr[:, ro + hh * 256 + 128:
                                     ro + hh * 256 + 256]
                            t.matmul(bank[:, 0:512], qnh, kh_s,
                                     start=True, stop=False)
                            t.matmul(bank[:, 0:512], qnh, kl_s,
                                     start=False, stop=False)
                            t.matmul(bank[:, 0:512], qnl, kh_s,
                                     start=False, stop=False)
                            t.matmul(bank[:, 0:512], qrs, krh,
                                     start=False, stop=False)
                            ins = t.matmul(bank[:, 0:512], qr2, krl,
                                           start=False, stop=True)
                            if hh == 1:
                                ins.then_inc(p.sems["aSC"], 1)
                    if 1 <= step <= s:
                        kc = step - 1
                        ck = ck_of[(s, hp, kc)]
                        po = (ck % 2) * 1024
                        t.wait_ge(p.sems["aEH"], A("aEH", ck + 1))
                        t.wait_ge(p.sems["aPL"], A("aPL", ck + 1))
                        if ck >= 1:
                            t.wait_ge(p.sems["aTV"], A("aTV", ck))
                        for t8 in range(8):
                            t.transpose(PS4B[:, t8 * 128:(t8 + 1) * 128],
                                        pbh[:, po + t8 * 128:
                                            po + (t8 + 1) * 128],
                                        sb["identB"][:, :])
                        for t8 in range(8):
                            ins = t.transpose(
                                PS5B[:, t8 * 128:(t8 + 1) * 128],
                                pbl[:, po + t8 * 128:po + (t8 + 1) * 128],
                                sb["identB"][:, :])
                            if t8 == 7:
                                ins.then_inc(p.sems["aTR"], 1)
                    if 2 <= step <= s + 1:
                        kc = step - 2
                        ck = ck_of[(s, hp, kc)]
                        po = (ck % 2) * 1024
                        t.wait_ge(p.sems["aTV"], A("aTV", ck + 1))
                        sl4v, vv_ = v_tgt[(s, hp, kc)]
                        t.wait_ge(p.sems["aV" + str(sl4v)], vv_)
                        if kc == 0 and i >= 1:
                            t.wait_ge(p.sems["aFD"], A("aFD", i))
                        sl3 = ck % 4
                        for hh in range(2):
                            X = hh * 256
                            for kb in range(4):
                                lsl = slice(po + (hh * 4 + kb) * 128,
                                            po + (hh * 4 + kb) * 128 + 128)
                                v256 = slice(sl3 * 2048 + hh * 1024
                                             + kb * 256,
                                             sl3 * 2048 + hh * 1024
                                             + kb * 256 + 256)
                                v128 = slice(v256.start, v256.start + 128)
                                t.matmul(PS[6][:, X:X + 256],
                                         pth[:, lsl], vb[:, v256],
                                         start=(kc == 0 and hh == 0
                                                and kb == 0),
                                         stop=False)
                                ins = t.matmul(
                                    PS[6][:, X:X + 128],
                                    ptl[:, lsl], vb[:, v128],
                                    start=False,
                                    stop=(kc == s - 1 and hh == 1
                                          and kb == 3))
                                if hh == 1 and kb == 3:
                                    ins.then_inc(p.sems["aAV"], 1)
                t.wait_ge(p.sems["aLO"], A("aLO", i + 1))
                if i >= 1:
                    t.wait_ge(p.sems["aAE"], A("aAE", i))
                t.transpose(PB[:, 0:128], aohi[:, 0:128], sb["identB"][:, :])
                t.transpose(PB[:, 128:256], aolo[:, 0:128],
                            sb["identB"][:, :])
                t.transpose(PB[:, 256:384], aohi[:, 128:256],
                            sb["identB"][:, :])
                t.transpose(PB[:, 384:512], aolo[:, 128:256],
                            sb["identB"][:, :]).then_inc(p.sems["aTA"], 1)
        blk.tensor(t_fn)
        p.add("aSC", NCK)
        p.add("aTR", NCK)
        p.add("aAV", NCK)
        p.add("aTA", 64)

        _phase_barrier(p, "phAT", ("aAE",))


def ph_wo(p):
    nc, blk, T, sb = p.nc, p.blk, p.T, p.sb
    PS, PB, PBF = p.PS, p.PB, p.PBF
    afh, afl = p.afh, p.afl

    p.begin_scope()
    with contextlib.ExitStack() as ph:
        e = ph.enter_context
        wot = e(nc.sbuf_tensor("wo_t", [128, 6 * 1024], BF16))
        xtl = e(nc.sbuf_tensor("wo_x", [128, 2 * 8 * 512], F32))
        ystg = e(nc.sbuf_tensor("wo_y", [128, 8 * 512], F32))

        for nm in ("wl20", "wl21", "wl22", "wl23", "wl24", "wl25",
                   "wx20", "wx21", "wm2", "whc", "wev2",
                   "wst20", "wst21"):
            p.sem(nm)
        bs = {nm: p.val(nm)
              for nm in ("wl20", "wl21", "wl22", "wl23", "wl24", "wl25",
                         "wx20", "wx21", "wm2", "whc",
                         "wev2", "wst20", "wst21")}

        def sy_fn(sy):
            for dmc in range(4):
                for sl in range(8):
                    if dmc >= 2:
                        sy.wait_ge(p.sems["wev2"],
                                   bs["wev2"] + (dmc - 1) * 8)
                    sy.dma_start(
                        xtl[:, (dmc % 2) * 4096 + sl * 512:
                            (dmc % 2) * 4096 + (sl + 1) * 512],
                        T["x_own"][sl * 128:(sl + 1) * 128,
                                   dmc * 512:(dmc + 1) * 512]).then_inc(
                        p.sems["wx2" + str(dmc % 2)], 16)
                for hc in range(16):
                    idx = dmc * 16 + hc
                    if idx >= 6:
                        sy.wait_ge(p.sems["whc"], bs["whc"] + idx - 5)
                    o = (idx % 6) * 1024
                    sy.dma_start(wot[:, o:o + 512],
                                 T["WOH"][hc * 128:(hc + 1) * 128,
                                          dmc * 512:(dmc + 1) * 512]
                                 ).then_inc(p.sems["wl2" + str(idx % 6)], 16)
                    sy.dma_start(wot[:, o + 512:o + 1024],
                                 T["WOL"][hc * 128:(hc + 1) * 128,
                                          dmc * 512:(dmc + 1) * 512]
                                 ).then_inc(p.sems["wl2" + str(idx % 6)], 16)
        blk.sync(sy_fn)
        for s6 in range(6):
            p.add("wl2" + str(s6),
                  32 * len([x for x in range(64) if x % 6 == s6]))
        p.add("wx20", 16 * 16)
        p.add("wx21", 16 * 16)

        def t_fn(t):
            for dmc in range(4):
                if dmc >= 1:
                    t.wait_ge(p.sems["wev2"], bs["wev2"] + dmc * 8)
                for hc in range(16):
                    idx = dmc * 16 + hc
                    t.wait_ge(p.sems["wl2" + str(idx % 6)],
                              bs["wl2" + str(idx % 6)] + 32 * (idx // 6 + 1))
                    o = (idx % 6) * 1024
                    for sl in range(8):
                        bank = PS[sl][:, 0:512] if sl < 7 else PBF
                        ac = hc * TOK + sl * 128
                        t.matmul(bank, afh[:, ac:ac + 128], wot[:, o:o + 512],
                                 start=(hc == 0), stop=False)
                        t.matmul(bank, afh[:, ac:ac + 128],
                                 wot[:, o + 512:o + 1024],
                                 start=False, stop=False)
                        ins = t.matmul(bank, afl[:, ac:ac + 128],
                                       wot[:, o:o + 512],
                                       start=False, stop=(hc == 15))
                        if sl == 7:
                            ins.then_inc(p.sems["whc"], 1)
                t.drain()
                t.nop().then_inc(p.sems["wm2"], 1)
        blk.tensor(t_fn)
        p.add("wm2", 4)
        p.add("whc", 64)

        def v_fn(v):
            for dmc in range(4):
                v.wait_ge(p.sems["wm2"], bs["wm2"] + dmc + 1)
                v.wait_ge(p.sems["wx2" + str(dmc % 2)],
                          bs["wx2" + str(dmc % 2)] + 128 * (dmc // 2 + 1))
                for sl in range(8):
                    bank = PS[sl][:, 0:512] if sl < 7 else PBF
                    if dmc >= 1:
                        v.wait_ge(p.sems["wst2" + str((dmc - 1) % 2)],
                                  bs["wst2" + str((dmc - 1) % 2)]
                                  + 128 * ((dmc - 1) // 2 + 1))
                    v.tensor_tensor(
                        ystg[:, sl * 512:(sl + 1) * 512], bank,
                        xtl[:, (dmc % 2) * 4096 + sl * 512:
                            (dmc % 2) * 4096 + (sl + 1) * 512],
                        ALU.add)
                    v.drain()
                    v.tensor_copy(ystg[:, sl * 512:sl * 512 + 1],
                                  ystg[:, sl * 512:sl * 512 + 1]).then_inc(
                        p.sems["wev2"], 1)
        blk.vector(v_fn)
        p.add("wev2", 32)

        def g_fn(g):
            for dmc in range(4):
                for sl in range(8):
                    g.wait_ge(p.sems["wev2"], bs["wev2"] + dmc * 8 + sl + 1)
                    g.dma_start(T["YS"][sl, :, dmc * 512:(dmc + 1) * 512],
                                ystg[:, sl * 512:(sl + 1) * 512]).then_inc(
                        p.sems["wst2" + str(dmc % 2)], 16)
        blk.gpsimd(g_fn)
        p.add("wst20", 16 * 16)
        p.add("wst21", 16 * 16)

        _phase_barrier(p, "phWO", ("wst20", "wst21"))

    p.attn_stack.close()


def ph_ffn(p):
    nc, blk, T, sb = p.nc, p.blk, p.T, p.sb
    PS, PB, PBF = p.PS, p.PB, p.PBF
    nfm = sb["nfm"]

    p.begin_scope()
    with contextlib.ExitStack() as ph:
        e = ph.enter_context
        s1b = e(nc.sbuf_tensor("f_s1b", [128, HID], F32))
        s1row = e(nc.sbuf_tensor("f_s1row", [1, HID], F32))
        w1t = e(nc.sbuf_tensor("f_w1t", [128, 6 * 512], BF16))
        hst = e(nc.sbuf_tensor("f_hst", [128, 8 * 512], F32))

        for nm in ("fsr", "fbm", "fb", "f1l0", "f1l1", "f1m", "f1dc",
                   "f1e", "f1s0", "f1s1"):
            p.sem(nm)

        bfb = {"fb": p.val("fb"), "fbm": p.val("fbm")}

        def g_b(g):
            for r in range(64):
                g.dma_start(s1row[:1, r * 128:(r + 1) * 128],
                            sb["s1T"][r:r + 1, :]).then_inc(p.sems["fsr"], 16)
        blk.gpsimd(g_b)
        p.add("fsr", 16 * 64)

        def t_b(t):
            t.wait_ge(p.sems["fsr"], p.val("fsr"))
            for c in range(16):
                if c >= 2:
                    t.wait_ge(p.sems["fb"], bfb["fb"] + c - 1)
                t.matmul(PS[c % 2][:, 0:512], sb["ones1"][:1, :],
                         s1row[:1, c * 512:(c + 1) * 512],
                         start=True, stop=True).then_inc(p.sems["fbm"], 1)
        blk.tensor(t_b)
        p.add("fbm", 16)

        def v_b(v):
            for c in range(16):
                v.wait_ge(p.sems["fbm"], bfb["fbm"] + c + 1)
                v.tensor_copy(s1b[:, c * 512:(c + 1) * 512],
                              PS[c % 2][:, 0:512]).then_inc(p.sems["fb"], 1)
        blk.vector(v_b)
        p.add("fb", 16)

        b1 = {nm: p.val(nm) for nm in ("f1l0", "f1l1", "f1l2", "f1l3",
                                       "f1l4", "f1l5", "f1m", "f1dc",
                                       "f1e", "f1s0", "f1s1")}

        def sy1(sy):
            for pas in range(2):
                for hc in range(16):
                    for dc in range(16):
                        idx = (pas * 16 + hc) * 16 + dc
                        if idx >= 6:
                            sy.wait_ge(p.sems["f1dc"], b1["f1dc"] + idx - 5)
                        o = (idx % 6) * 512
                        inap = bass.AP(T["W1T"].tensor,
                                       (hc * 4 * 16 + dc) * 128 * 128,
                                       [[128, 128], [16 * 128 * 128, 4],
                                        [1, 128]])
                        sy.dma_start(
                            w1t[:, o:o + 512].rearrange(
                                "p (j c) -> p j c", j=4),
                            inap).then_inc(p.sems["f1l" + str(idx % 6)], 16)
        blk.sync(sy1)
        for s6 in range(6):
            p.add("f1l" + str(s6),
                  16 * len([x for x in range(512) if x % 6 == s6]))

        def t1(t):
            t.wait_ge(p.sems["fb"], p.val("fb"))
            for pas in range(2):
                for hc in range(16):
                    g = pas * 16 + hc
                    bq = (hc % 2) * 4
                    if g >= 2:
                        t.wait_ge(p.sems["f1e"], b1["f1e"] + 4 * (g - 1))
                    for dc in range(16):
                        idx = g * 16 + dc
                        t.wait_ge(p.sems["f1l" + str(idx % 6)],
                                  b1["f1l" + str(idx % 6)]
                                  + 16 * (idx // 6 + 1))
                        for tb4 in range(4):
                            tb = pas * 4 + tb4
                            bank = (PS[bq + tb4][:, 0:512] if bq + tb4 < 7
                                    else PBF)
                            ins = t.matmul(bank,
                                           nfm[:, dc * TOK + tb * 128:
                                               dc * TOK + tb * 128 + 128],
                                           w1t[:, (idx % 6) * 512:
                                               (idx % 6) * 512 + 512],
                                           start=(dc == 0), stop=(dc == 15))
                            if tb4 == 3:
                                ins.then_inc(p.sems["f1dc"], 1)
                    t.drain()
                    t.nop().then_inc(p.sems["f1m"], 1)
        blk.tensor(t1)
        p.add("f1m", 32)
        p.add("f1dc", 512)

        def v1(v):
            for pas in range(2):
                for hc in range(16):
                    g = pas * 16 + hc
                    bq = (hc % 2) * 4
                    v.wait_ge(p.sems["f1m"], b1["f1m"] + g + 1)
                    for tb4 in range(4):
                        tb = pas * 4 + tb4
                        bank = (PS[bq + tb4][:, 0:512] if bq + tb4 < 7
                                else PBF)
                        so = ((g % 2) * 4 + tb4) * 512
                        if g >= 2:
                            v.wait_ge(p.sems["f1s" + str(g % 2)],
                                      b1["f1s" + str(g % 2)]
                                      + 64 * (g // 2))
                        v.scalar_tensor_tensor(
                            hst[:, so:so + 512], bank,
                            sb["rsxm"][:, tb:tb + 1],
                            s1b[:, hc * 512:(hc + 1) * 512],
                            ALU.mult, ALU.mult)
                        v.drain()
                        v.tensor_copy(hst[:, so:so + 1],
                                      hst[:, so:so + 1]).then_inc(
                            p.sems["f1e"], 1)
        blk.vector(v1)
        p.add("f1e", 32 * 4)

        def g1(g):
            for pas in range(2):
                for hc in range(16):
                    gg = pas * 16 + hc
                    for tb4 in range(4):
                        tb = pas * 4 + tb4
                        so = ((gg % 2) * 4 + tb4) * 512
                        n = gg * 4 + tb4
                        g.wait_ge(p.sems["f1e"], b1["f1e"] + n + 1)
                        g.dma_start(T["HS"][tb, :, hc * 512:(hc + 1) * 512],
                                    hst[:, so:so + 512]).then_inc(
                            p.sems["f1s" + str(gg % 2)], 16)
        blk.gpsimd(g1)
        p.add("f1s0", 16 * 64)
        p.add("f1s1", 16 * 64)

        _phase_barrier(p, "phF1", ("f1s0", "f1s1"))

    p.begin_scope()
    with contextlib.ExitStack() as ph:
        e = ph.enter_context
        hbuf = e(nc.sbuf_tensor("f_h", [128, HID], F32))
        gbuf = e(nc.sbuf_tensor("f_g", [128, HID], F32))
        c8b = e(nc.sbuf_tensor("f_c8", [128, HID], I8))
        fb1 = e(nc.sbuf_tensor("f_b1", [128, HID], BF16))
        fb2 = e(nc.sbuf_tensor("f_b2", [128, HID], BF16))
        sdb = e(nc.sbuf_tensor("f_sd", [128, HID], BF16))
        ccq = e(nc.sbuf_tensor("f_cc", [128, 8], F32))
        nstg = e(nc.sbuf_tensor("f_nstg", [128, 2 * 1024], BF16))

        for nm in ("q2l", "q2g", "q2x", "q2n", "q2t", "q2e", "q2s0",
                   "q2s1"):
            p.sem(nm)
        b2 = {nm: p.val(nm) for nm in ("q2l", "q2g", "q2x", "q2n", "q2t",
                                       "q2e", "q2s0", "q2s1")}

        def sy2(sy):
            for tb in range(NB):
                if tb >= 1:
                    sy.wait_ge(p.sems["q2n"], b2["q2n"] + tb)
                sy.dma_start(hbuf[:, :], T["HS"][tb]).then_inc(
                    p.sems["q2l"], 16)
        blk.sync(sy2)
        p.add("q2l", 16 * NB)

        def sc2(sc):
            for tb in range(NB):
                sc.wait_ge(p.sems["q2l"], b2["q2l"] + 16 * (tb + 1))
                if tb >= 1:
                    sc.wait_ge(p.sems["q2n"], b2["q2n"] + tb)
                sc.activation(gbuf[:, :], hbuf[:, :],
                              AF.Gelu_apprx_tanh).then_inc(p.sems["q2g"], 1)
                sc.wait_ge(p.sems["q2x"], b2["q2x"] + 2 * tb + 1)
                sc.activation(hbuf[:, :], gbuf[:, :], AF.Identity,
                              scale=ccq[:, 3:4])
                sc.drain()
                sc.activation(c8b[:, :], gbuf[:, :], AF.Identity,
                              scale=ccq[:, 3:4]).then_inc(p.sems["q2g"], 1)
        blk.scalar(sc2)
        p.add("q2g", 2 * NB)

        def v2(v):
            for tb in range(NB):
                v.wait_ge(p.sems["q2g"], b2["q2g"] + 2 * tb + 1)
                v.tensor_reduce(ccq[:, 0:1], gbuf[:, :], AX, ALU.max,
                                apply_absolute_value=True)
                v.drain()
                v.tensor_scalar(ccq[:, 1:2], ccq[:, 0:1], 1e-5, None, ALU.max)
                v.drain()
                v.reciprocal(ccq[:, 2:3], ccq[:, 1:2])
                v.drain()
                v.tensor_scalar(ccq[:, 3:4], ccq[:, 2:3], 127.0, None,
                                ALU.mult)
                v.drain()
                v.tensor_scalar(sb["rshm"][:, tb:tb + 1], ccq[:, 1:2],
                                1.0 / 127.0, None, ALU.mult).then_inc(
                    p.sems["q2x"], 1)
                v.wait_ge(p.sems["q2g"], b2["q2g"] + 2 * tb + 2)
                v.scalar_tensor_tensor(fb1[:, :], hbuf[:, :], -0.5,
                                       c8b[:, :], ALU.add, ALU.is_ge)
                v.scalar_tensor_tensor(fb2[:, :], hbuf[:, :], 0.5,
                                       c8b[:, :], ALU.add, ALU.is_le)
                v.drain()
                v.tensor_tensor(sdb[:, :], fb1[:, :], fb2[:, :], ALU.subtract)
                v.drain()
                v.tensor_tensor(fb1[:, :], sdb[:, :], c8b[:, :],
                                ALU.add).then_inc(p.sems["q2x"], 1)
                for g8 in range(8):
                    gi = tb * 8 + g8
                    v.wait_ge(p.sems["q2t"], b2["q2t"] + gi + 1)
                    if gi >= 2:
                        v.wait_ge(p.sems["q2s" + str(gi % 2)],
                                  b2["q2s" + str(gi % 2)] + 128 * (gi // 2))
                    v.tensor_copy(nstg[:, (gi % 2) * 1024:
                                       (gi % 2) * 1024 + 1024], PB[:, :])
                    v.drain()
                    v.memset(ccq[:, 7:8], 0.0).then_inc(
                        p.sems["q2e"], 1)
                v.drain()
                v.memset(ccq[:, 7:8], 0.0).then_inc(
                    p.sems["q2n"], 1)
        blk.vector(v2)
        p.add("q2x", 2 * NB)
        p.add("q2e", 8 * NB)
        p.add("q2n", NB)

        def t2(t):
            for tb in range(NB):
                t.wait_ge(p.sems["q2x"], b2["q2x"] + 2 * tb + 2)
                for g8 in range(8):
                    gi = tb * 8 + g8
                    if gi >= 1:
                        t.wait_ge(p.sems["q2e"], b2["q2e"] + gi)
                    for k in range(8):
                        hcc = g8 * 8 + k
                        ins = t.transpose(PB[:, k * 128:(k + 1) * 128],
                                          fb1[:, hcc * 128:(hcc + 1) * 128],
                                          sb["identB"][:, :])
                        if k == 7:
                            ins.then_inc(p.sems["q2t"], 1)
        blk.tensor(t2)
        p.add("q2t", 8 * NB)

        def g2(g):
            for tb in range(NB):
                for g8 in range(8):
                    gi = tb * 8 + g8
                    g.wait_ge(p.sems["q2e"], b2["q2e"] + gi + 1)
                    src_ = nstg[:, (gi % 2) * 1024:(gi % 2) * 1024
                                + 1024].rearrange("p (k c) -> p k c", k=8)
                    dst = T["NHS"][g8 * 8:g8 * 8 + 8, tb].rearrange(
                        "k p c -> p k c")
                    for k in range(8):
                        g.dma_start(dst[:, k], src_[:, k]).then_inc(
                            p.sems["q2s" + str(gi % 2)], 16)
        blk.gpsimd(g2)
        p.add("q2s0", 128 * 32)
        p.add("q2s1", 128 * 32)

        _phase_barrier(p, "phQ2", ("q2s0", "q2s1"))

    p.begin_scope()
    with contextlib.ExitStack() as ph:
        e = ph.enter_context
        s2b = e(nc.sbuf_tensor("f_s2b", [128, D], F32))
        s2row = e(nc.sbuf_tensor("f_s2row", [1, D], F32))
        w2t = e(nc.sbuf_tensor("f_w2t", [128, 6 * 512], BF16))
        nh2 = e(nc.sbuf_tensor("f_nh2", [128, 6 * 512], BF16))
        ytl = e(nc.sbuf_tensor("f_ytl", [128, 2 * 4 * 512], F32))
        ostg = e(nc.sbuf_tensor("f_ostg", [128, 4 * 512], F32))

        for nm in ("fcm", "fc", "n2l0", "n2l1", "n2l2", "n2l3", "n2l4",
                   "n2l5", "f2l0", "f2l1", "f2l2", "f2l3", "f2l4", "f2l5",
                   "f2y", "f2m", "f2hc", "f2e", "f2s"):
            p.sem(nm)

        def g_c(g):
            for r in range(16):
                g.dma_start(s2row[:1, r * 128:(r + 1) * 128],
                            sb["s2T"][r:r + 1, :]).then_inc(p.sems["fsr2"],
                                                           16)
        p.sem("fsr2")
        blk.gpsimd(g_c)
        p.add("fsr2", 16 * 16)

        bfc = {"fc": p.val("fc"), "fcm": p.val("fcm")}

        def t_c(t):
            t.wait_ge(p.sems["fsr2"], p.val("fsr2"))
            for c in range(4):
                if c >= 2:
                    t.wait_ge(p.sems["fc"], bfc["fc"] + c - 1)
                t.matmul(PS[c % 2][:, 0:512], sb["ones1"][:1, :],
                         s2row[:1, c * 512:(c + 1) * 512],
                         start=True, stop=True).then_inc(p.sems["fcm"], 1)
        blk.tensor(t_c)
        p.add("fcm", 4)

        def v_c(v):
            for c in range(4):
                v.wait_ge(p.sems["fcm"], bfc["fcm"] + c + 1)
                v.tensor_copy(s2b[:, c * 512:(c + 1) * 512],
                              PS[c % 2][:, 0:512]).then_inc(p.sems["fc"], 1)
        blk.vector(v_c)
        p.add("fc", 4)

        b3 = {nm: p.val(nm) for nm in ("n2l0", "n2l1", "n2l2", "n2l3",
                                       "n2l4", "n2l5", "f2l0", "f2l1",
                                       "f2l2", "f2l3", "f2l4", "f2l5",
                                       "f2y", "f2m", "f2hc", "f2e", "f2s")}

        def sy3(sy):
            for pas in range(2):
                for dmc in range(4):
                    i = pas * 4 + dmc
                    if i >= 1:
                        sy.wait_ge(p.sems["f2e"], b3["f2e"] + 4 * (i - 1) + 1)
                    for tb4 in range(4):
                        tb = pas * 4 + tb4
                        sy.dma_start(
                            ytl[:, (i % 2) * 2048 + tb4 * 512:
                                (i % 2) * 2048 + (tb4 + 1) * 512],
                            T["YS"][tb, :, dmc * 512:(dmc + 1) * 512]
                        ).then_inc(p.sems["f2y"], 16)
                    for hc in range(64):
                        idx = i * 64 + hc
                        if idx >= 6:
                            sy.wait_ge(p.sems["f2hc"], b3["f2hc"] + idx - 5)
                        o = (idx % 6) * 512
                        inap = bass.AP(T["W2T"].tensor,
                                       (hc * 16 + dmc * 4) * 128 * 128,
                                       [[128, 128], [128 * 128, 4], [1, 128]])
                        sy.dma_start(
                            w2t[:, o:o + 512].rearrange(
                                "p (j c) -> p j c", j=4),
                            inap).then_inc(p.sems["f2l" + str(idx % 6)], 16)
        blk.sync(sy3)

        def g3l(g):
            def stores(i2):
                for tb4 in range(4):
                    tb = (i2 // 4) * 4 + tb4
                    dmc2 = i2 % 4
                    so = tb4 * 512
                    g.wait_ge(p.sems["f2e"],
                              b3["f2e"] + i2 * 4 + tb4 + 1)
                    g.dma_start(T["y"][tb * 128:(tb + 1) * 128,
                                       dmc2 * 512:(dmc2 + 1) * 512],
                                ostg[:, so:so + 512]).then_inc(
                        p.sems["f2s"], 16)

            for pas in range(2):
                for dmc in range(4):
                    i = pas * 4 + dmc
                    for hc in range(64):
                        idx = i * 64 + hc
                        if idx >= 6:
                            g.wait_ge(p.sems["f2hc"], b3["f2hc"] + idx - 5)
                        o = (idx % 6) * 512
                        nsrc = T["NHS"][hc, pas * 4:pas * 4 + 4].rearrange(
                            "t p c -> p t c")
                        g.dma_start(
                            nh2[:, o:o + 512].rearrange(
                                "p (t c) -> p t c", t=4),
                            nsrc).then_inc(p.sems["n2l" + str(idx % 6)], 16)
                    if i >= 1:
                        stores(i - 1)
            stores(7)
            g.wait_ge(p.sems["f2s"], p.val("f2s") + 16 * 32)
        blk.gpsimd(g3l)
        p.add("f2s", 16 * 32)
        p.add("f2y", 16 * 32)
        for s6 in range(6):
            n6 = 16 * len([x for x in range(512) if x % 6 == s6])
            p.add("f2l" + str(s6), n6)
            p.add("n2l" + str(s6), n6)

        def t3(t):
            t.wait_ge(p.sems["fc"], p.val("fc"))
            for pas in range(2):
                for dmc in range(4):
                    i = pas * 4 + dmc
                    bq = (dmc % 2) * 4
                    if i >= 2:
                        t.wait_ge(p.sems["f2e"], b3["f2e"] + 4 * (i - 1))
                    for hc in range(64):
                        idx = i * 64 + hc
                        t.wait_ge(p.sems["f2l" + str(idx % 6)],
                                  b3["f2l" + str(idx % 6)]
                                  + 16 * (idx // 6 + 1))
                        t.wait_ge(p.sems["n2l" + str(idx % 6)],
                                  b3["n2l" + str(idx % 6)]
                                  + 16 * (idx // 6 + 1))
                        o = (idx % 6) * 512
                        for tb4 in range(4):
                            bank = (PS[bq + tb4][:, 0:512] if bq + tb4 < 7
                                    else PBF)
                            ins = t.matmul(bank,
                                           nh2[:, o + tb4 * 128:
                                               o + (tb4 + 1) * 128],
                                           w2t[:, o:o + 512],
                                           start=(hc == 0), stop=(hc == 63))
                            if tb4 == 3:
                                ins.then_inc(p.sems["f2hc"], 1)
                    t.drain()
                    t.nop().then_inc(p.sems["f2m"], 1)
        blk.tensor(t3)
        p.add("f2m", 8)
        p.add("f2hc", 512)

        def v3(v):
            for pas in range(2):
                for dmc in range(4):
                    i = pas * 4 + dmc
                    bq = (dmc % 2) * 4
                    v.wait_ge(p.sems["f2m"], b3["f2m"] + i + 1)
                    v.wait_ge(p.sems["f2y"], b3["f2y"] + 16 * 4 * (i + 1))
                    for tb4 in range(4):
                        tb = pas * 4 + tb4
                        bank = (PS[bq + tb4][:, 0:512] if bq + tb4 < 7
                                else PBF)
                        so = tb4 * 512
                        if i >= 1:
                            v.wait_ge(p.sems["f2s"],
                                      b3["f2s"] + 16 * 4 * i)
                        v.scalar_tensor_tensor(
                            ostg[:, so:so + 512], bank,
                            sb["rshm"][:, tb:tb + 1],
                            s2b[:, dmc * 512:(dmc + 1) * 512],
                            ALU.mult, ALU.mult)
                        v.drain()
                        v.tensor_tensor(
                            ostg[:, so:so + 512], ostg[:, so:so + 512],
                            ytl[:, (i % 2) * 2048 + tb4 * 512:
                                (i % 2) * 2048 + (tb4 + 1) * 512],
                            ALU.add)
                        v.drain()
                        v.tensor_copy(ostg[:, so:so + 1],
                                      ostg[:, so:so + 1]).then_inc(
                            p.sems["f2e"], 1)
        blk.vector(v3)
        p.add("f2e", 32)





_NC_CACHE = None


def _get_nc():
    global _NC_CACHE
    if _NC_CACHE is None:
        _NC_CACHE = build_nc()
    return _NC_CACHE


def kernel(L0_attn_x, L0_attn_Wq, L0_attn_Wdkv, L0_attn_Wkr, L0_attn_Wukv,
           L0_attn_Wo, L0_ffn_w1, L0_ffn_w2):
    x = np.ascontiguousarray(np.asarray(L0_attn_x, dtype=np.float32))
    Wq3 = np.asarray(L0_attn_Wq, dtype=np.float32).reshape(D, H, DH + DR)
    wqn = np.ascontiguousarray(Wq3[:, :, :DH].reshape(D, H * DH))
    wqr = np.ascontiguousarray(Wq3[:, :, DH:].reshape(D, H * DR))
    Wukv3 = np.asarray(L0_attn_Wukv, dtype=np.float32).reshape(LAT, H, 2 * DH)
    wuk = np.ascontiguousarray(Wukv3[:, :, :DH].reshape(LAT, H * DH))
    wuv = np.ascontiguousarray(Wukv3[:, :, DH:].reshape(LAT, H * DH))

    nc = _get_nc()
    in_maps = []
    for core in range(8):
        b, c = core // 4, core % 4
        blocks = _blocks_for_core(c)
        rows = np.concatenate([np.arange(j * 128, (j + 1) * 128)
                               for j in blocks])
        qr = np.zeros((128, NB), dtype=np.float32)
        for si, j in enumerate(blocks):
            s = si + 1
            qr[:, si] = j * 128 + np.arange(128) - (4 * s - 4) * 128
        in_maps.append({
            "x_own": np.ascontiguousarray(x[b][rows]),
            "x_full": np.ascontiguousarray(x[b]),
            "pos_own": rows.astype(np.float32).reshape(1, TOK),
            "qrel": qr,
            "wqn": wqn, "wqr": wqr,
            "wdkv": np.ascontiguousarray(
                np.asarray(L0_attn_Wdkv, dtype=np.float32)),
            "wkr": np.ascontiguousarray(
                np.asarray(L0_attn_Wkr, dtype=np.float32)),
            "wuk": wuk, "wuv": wuv,
            "wo": np.ascontiguousarray(
                np.asarray(L0_attn_Wo, dtype=np.float32)),
            "w1": np.ascontiguousarray(
                np.asarray(L0_ffn_w1, dtype=np.float32)),
            "w2": np.ascontiguousarray(
                np.asarray(L0_ffn_w2, dtype=np.float32)),
        })

    res = run_bass_kernel_spmd(nc, in_maps, core_ids=list(range(8)))

    out = np.empty((B, S, D), dtype=np.float32)
    for core in range(8):
        b, c = core // 4, core % 4
        blocks = _blocks_for_core(c)
        yo = np.asarray(res.results[core]["y"])
        for si, j in enumerate(blocks):
            out[b, j * 128:(j + 1) * 128] = yo[si * 128:(si + 1) * 128]
    return out


# version 16
# speedup vs baseline: 1.0035x; 1.0035x over previous
import contextlib
import math

import numpy as np
import concourse.bass as bass
import concourse.mybir as mybir
from concourse.bass_utils import run_bass_kernel_spmd

F32 = mybir.dt.float32
BF16 = mybir.dt.bfloat16
I8 = mybir.dt.int8
I16 = mybir.dt.int16
AF = mybir.ActivationFunctionType
ALU = mybir.AluOpType
AX = mybir.AxisListType.X

B, S, D = 2, 4096, 2048
H, DH, DR, LAT, HID = 16, 128, 64, 512, 8192
TOK = 1024
NB = 8
SCALE = 1.0 / math.sqrt(DH + DR)
LN2P = math.log(10000.0) / 32.0
INV2PI = 1.0 / (2.0 * math.pi)
CW1 = 6.28125
CW2 = 2.0 * math.pi - 6.28125
PIH = math.pi / 2.0


def _blocks_for_core(c):
    return [4 * (s - 1) + c if s % 2 == 1 else 4 * s - 1 - c
            for s in range(1, 9)]


GLOBAL_SEMS = {"g0", "ld0", "dv0", "sc0", "lnv", "lns", "rts", "rtv",
               "phW", "phWS", "phA2", "phCK", "phKV", "phAT", "phWO",
               "phF1", "phQ2"}

SWDGE_SEMS = {"w1s0", "w1s1", "w2s0", "w2s1", "wog0", "wog1", "qpsw", "qps",
              "cksw", "ckst", "ckdup", "kvks0", "kvks1", "kvks2", "kvks3",
              "kvvs0", "kvvs1", "kvvs2", "kvvs3", "krs", "f1s0", "f1s1",
              "q2s0", "q2s1", "wst20", "wst21", "f2s", "fsr", "fsr2",
              "aV0", "aV1", "aV2", "aV3", "aQ0", "aQ1",
              "n2l0", "n2l1", "n2l2", "n2l3", "n2l4", "n2l5"}


class _SemView:
    def __init__(self, ctx):
        self.ctx = ctx

    def __getitem__(self, name):
        return self.ctx.sem(name)


class Ctx:

    def __init__(self, nc, blk, st):
        self.nc, self.blk, self.st = nc, blk, st
        self._phys = {}
        self._cnt = {}
        self._pool = []
        self._gpool = []
        self._scope = {}
        self._globals = {}
        self._nphys = 0
        self.sems = _SemView(self)

    def _new_phys(self):
        name = f"S{self._nphys}"
        self._nphys += 1
        self._phys[name] = self.st.enter_context(self.nc.semaphore(name))
        self._cnt[name] = 0
        return name

    def begin_scope(self):
        for log, phys in self._scope.items():
            (self._gpool if log in SWDGE_SEMS else self._pool).append(phys)
        self._scope = {}

    def _resolve(self, name):
        if name in GLOBAL_SEMS:
            if name not in self._globals:
                self._globals[name] = self._new_phys()
            return self._globals[name]
        if name not in self._scope:
            pool = self._gpool if name in SWDGE_SEMS else self._pool
            self._scope[name] = (pool.pop() if pool else self._new_phys())
        return self._scope[name]

    def sem(self, name):
        return self._phys[self._resolve(name)]

    def add(self, name, by):
        phys = self._resolve(name)
        self._cnt[phys] += by
        return self._cnt[phys]

    def val(self, name):
        return self._cnt[self._resolve(name)]


def build_nc():
    nc = bass.Bass("TRN2", target_bir_lowering=False)
    T = {}

    def din(name, shape, dt=F32):
        T[name] = nc.dram_tensor(name, shape, dt, kind="ExternalInput").ap()

    def dscr(name, shape, dt):
        T[name] = nc.dram_tensor(name, shape, dt, kind="Internal").ap()

    din("x_own", [TOK, D])
    din("x_full", [S, D])
    din("pos_own", [1, TOK])
    din("qrel", [128, NB])
    din("wqn", [D, 2048])
    din("wqr", [D, 1024])
    din("wdkv", [D, LAT])
    din("wkr", [D, DR])
    din("wuk", [LAT, 2048])
    din("wuv", [LAT, 2048])
    din("wo", [D, D])
    din("w1", [HID, D])
    din("w2", [D, HID])
    T["y"] = nc.dram_tensor("y", [TOK, D], F32, kind="ExternalOutput").ap()

    dscr("CKH", [4, 128, S], BF16)
    dscr("CKL", [4, 128, S], BF16)
    dscr("KH", [8, 16, 128, 512], BF16)
    dscr("KL", [8, 16, 128, 512], BF16)
    dscr("VSI", [8, 16, 4, 128, 256], BF16)
    dscr("QNH", [16, NB, 128, 128], BF16)
    dscr("QNL", [16, NB, 128, 128], BF16)
    dscr("QRH", [8, NB, 128, 128], BF16)
    dscr("QRL", [8, NB, 128, 128], BF16)
    dscr("WOH", [D, D], BF16)
    dscr("WOL", [D, D], BF16)
    dscr("W1T", [64, 16, 128, 128], BF16)
    dscr("W2T", [64, 16, 128, 128], BF16)
    dscr("HS", [NB, 128, HID], F32)
    dscr("NHS", [64, NB, 128, 128], BF16)
    dscr("YS", [NB, 128, D], F32)

    with contextlib.ExitStack() as st:
        e = st.enter_context
        blk = e(nc.Block())
        p = Ctx(nc, blk, st)
        p.T = T
        sb = {}

        def alloc(name, shape, dt=F32):
            sb[name] = e(nc.sbuf_tensor(name, shape, dt))

        alloc("identF", [128, 128])
        alloc("identB", [128, 128], BF16)
        alloc("ones1", [1, 128])
        alloc("invfr", [1, 128])
        alloc("posq", [1, TOK])
        alloc("kidx", [128, 512])
        alloc("qrl", [128, NB])
        alloc("sgncol", [128, 1])
        alloc("cst", [128, 2])
        alloc("rsxm", [128, NB])
        alloc("rshm", [128, NB])
        alloc("s1T", [64, 128])
        alloc("s2T", [16, 128])
        alloc("cosq", [128, TOK])
        alloc("sinq", [128, TOK])
        alloc("krH", [128, S], BF16)
        alloc("krL", [128, S], BF16)
        alloc("nfm", [128, 16 * TOK], BF16)
        p.sb = sb

        p.PS = [e(nc.psum_tensor(f"ps{i}", [128, 512], F32)) for i in range(7)]
        p.PB = e(nc.psum_tensor("psb", [128, 1024], BF16))
        p.PBF = p.PB[:, :].bitcast(F32)

        ph_consts(p)
        ph_wprep(p)
        ph_a2(p)
        ph_ackv_akv(p)
        ph_attn(p)
        ph_wo(p)
        ph_ffn(p)

    return nc


def _phase_barrier(p, name, wait_sems):
    blk = p.blk
    p.sem(name)

    def sy_fn(sy):
        for s in wait_sems:
            sy.wait_ge(p.sems[s], p.val(s))
        sy.nop().then_inc(p.sems[name], 1)
    blk.sync(sy_fn)
    tgt = p.add(name, 1)
    for engname in ("gpsimd", "vector", "scalar", "tensor"):
        def e_fn(eng, tgt=tgt, name=name):
            eng.wait_ge(p.sems[name], tgt)
        getattr(blk, engname)(e_fn)


def ph_consts(p):
    blk, sb, T = p.blk, p.sb, p.T
    for nm in ("g0", "ld0", "dv0", "sc0"):
        p.sem(nm)

    def g_fn(g):
        g.memset(sb["identF"][:, :], 0.0)
        g.memset(sb["identB"][:, :], 0.0)
        g.memset(sb["ones1"][:1, :], 1.0)
        g.memset(sb["sgncol"][0:32, :], -1.0)
        g.memset(sb["sgncol"][32:64, :], 1.0)
        g.memset(sb["sgncol"][64:96, :], -1.0)
        g.memset(sb["sgncol"][96:128, :], 1.0)
        g.memset(sb["cst"][:, 0:1], PIH)
        g.memset(sb["cst"][:, 1:2], 1e-5)
        g.drain()
        g.affine_select(out=sb["identF"][:, :], in_=sb["identF"][:, :],
                        compare_op=ALU.not_equal, fill=1.0,
                        base=0, pattern=[[-1, 128]], channel_multiplier=1)
        g.affine_select(out=sb["identB"][:, :], in_=sb["identB"][:, :],
                        compare_op=ALU.not_equal, fill=1.0,
                        base=0, pattern=[[-1, 128]], channel_multiplier=1)
        g.iota(sb["invfr"][:1, :], pattern=[[1, 128]], base=0,
               channel_multiplier=0, allow_small_or_imprecise_dtypes=True)
        g.iota(sb["kidx"][:, :], pattern=[[1, 512]], base=0,
               channel_multiplier=0, allow_small_or_imprecise_dtypes=True)
        g.drain()
        g.nop().then_inc(p.sems["g0"], 1)
    blk.gpsimd(g_fn)
    p.add("g0", 1)

    def sy_fn(sy):
        sy.dma_start(sb["posq"][:1, :], T["pos_own"]).then_inc(
            p.sems["ld0"], 16)
        sy.dma_start(sb["qrl"][:, :], T["qrel"]).then_inc(p.sems["ld0"], 16)
    blk.sync(sy_fn)
    p.add("ld0", 32)

    def v_fn(v):
        v.wait_ge(p.sems["g0"], p.val("g0"))
        for k in range(1, 4):
            v.tensor_copy(sb["invfr"][:1, 32 * k:32 * k + 32],
                          sb["invfr"][:1, 0:32])
        v.drain()
        v.tensor_copy(sb["invfr"][:1, 0:1],
                      sb["invfr"][:1, 0:1]).then_inc(p.sems["dv0"], 1)
    blk.vector(v_fn)
    p.add("dv0", 1)

    def sc_fn(sc):
        sc.wait_ge(p.sems["dv0"], p.val("dv0"))
        sc.activation(sb["invfr"][:1, :], sb["invfr"][:1, :], AF.Exp,
                      scale=-LN2P).then_inc(p.sems["sc0"], 1)
    blk.scalar(sc_fn)
    p.add("sc0", 1)


def ph_wprep(p):
    nc, blk, T, sb = p.nc, p.blk, p.T, p.sb
    PS, PB = p.PS, p.PB
    CH = 2048

    p.begin_scope()
    with contextlib.ExitStack() as ph:
        e = ph.enter_context
        wld = e(nc.sbuf_tensor("wp_ld", [128, 2 * CH], F32))
        tb0 = e(nc.sbuf_tensor("wp_t0", [128, 2 * CH], BF16))
        ab = e(nc.sbuf_tensor("wp_ab", [128, CH], BF16))
        bb = e(nc.sbuf_tensor("wp_bb", [128, CH], BF16))
        scol = e(nc.sbuf_tensor("wp_s", [128, 8], F32))
        s1m = e(nc.sbuf_tensor("wp_s1m", [128, 64], F32))
        s2m = e(nc.sbuf_tensor("wp_s2m", [128, 16], F32))
        tstg = e(nc.sbuf_tensor("wp_tstg", [128, 2 * 1024], BF16))
        wdum = e(nc.sbuf_tensor("wp_dum", [128, CH], BF16))

        plans = [("w1", T["w1"], T["W1T"], HID // 128, D, s1m, True),
                 ("w2", T["w2"], T["W2T"], D // 128, HID, s2m, False)]
        for tag, wsrc, wdst, ntile, ncols, smat, rowmaj in plans:
            nch = ncols // CH
            reload = nch > 1
            npl = nch * (2 if reload else 1)
            for nm in ("l0", "l1", "c", "v", "t", "ev", "s0", "s1", "r"):
                p.sem(tag + nm)
            bs = {nm: p.val(tag + nm) for nm in ("l0", "l1", "c", "v", "t",
                                                 "ev", "s0", "s1", "r")}

            prev_c = (("w1c", p.val("w1c")) if tag == "w2" else None)
            prev_t = (("w1t", p.val("w1t")) if tag == "w2" else None)
            prev_s = ((("w1s0", p.val("w1s0")), ("w1s1", p.val("w1s1")))
                      if tag == "w2" else None)

            def sy_fn(sy, tag=tag, wsrc=wsrc, ntile=ntile, nch=nch,
                      reload=reload, npl=npl, bs=bs, prev_c=prev_c):
                L = 0
                for i in range(ntile):
                    for ph2 in range(2 if reload else 1):
                        for ch in range(nch):
                            if L < 2 and prev_c is not None:
                                sy.wait_ge(p.sems[prev_c[0]], prev_c[1])
                            if L >= 2:
                                sy.wait_ge(p.sems[tag + "c"],
                                           bs["c"] + L - 1)
                            sy.dma_start(
                                wld[:, (L % 2) * CH:(L % 2) * CH
                                    + min(CH, ncols)],
                                wsrc[i * 128:(i + 1) * 128,
                                     ch * CH:ch * CH + min(CH, ncols)]
                            ).then_inc(p.sems[tag + "l" + str(L % 2)], 16)
                            L += 1
            blk.sync(sy_fn)
            for sl_ in range(2):
                nl = len([x for x in range(ntile * npl) if x % 2 == sl_])
                p.add(tag + "l" + str(sl_), 16 * nl)

            def sc_fn(sc, tag=tag, ntile=ntile, nch=nch, ncols=ncols,
                      reload=reload, npl=npl, bs=bs, prev_s=prev_s):

                def emit_evacs(i, gch):
                    for g8 in range(2):
                        gidx = (i * nch + gch) * 2 + g8
                        sc.wait_ge(p.sems[tag + "t"], bs["t"] + gidx + 1)
                        if gidx < 2 and prev_s is not None:
                            sc.wait_ge(p.sems[prev_s[0][0]], prev_s[0][1])
                            sc.wait_ge(p.sems[prev_s[1][0]], prev_s[1][1])
                        if gidx >= 2:
                            sc.wait_ge(p.sems[tag + "s" + str(gidx % 2)],
                                       bs["s" + str(gidx % 2)]
                                       + 16 * (gidx // 2))
                        sc.activation(
                            tstg[:, (gidx % 2) * 1024:
                                 (gidx % 2) * 1024 + 1024],
                            PB[:, :], AF.Identity).then_inc(
                            p.sems[tag + "ev"], 1)

                for i in range(ntile):
                    if i >= 1:
                        sc.wait_ge(p.sems[tag + "v"], bs["v"] + i * nch)
                    for ch in range(nch):
                        L = i * npl + ch
                        sc.wait_ge(p.sems[tag + "l" + str(L % 2)],
                                   bs["l" + str(L % 2)] + 16 * (L // 2 + 1))
                        w_t = wld[:, (L % 2) * CH:(L % 2) * CH
                                  + min(CH, ncols)]
                        if L > 0:
                            sc.drain()
                        ins = sc.activation(wdum[:, 0:min(CH, ncols)], w_t,
                                            AF.Abs,
                                            accum_out=scol[:, ch:ch + 1])
                        if reload:
                            ins.then_inc(p.sems[tag + "c"], 1)
                    sc.drain()
                    sc.nop().then_inc(p.sems[tag + "r"], 1)
                    if i >= 1:
                        emit_evacs(i - 1, nch - 1)
                        for gch in range(nch - 1):
                            emit_evacs(i, gch)
                    elif nch > 1:
                        for gch in range(nch - 1):
                            emit_evacs(0, gch)
                emit_evacs(ntile - 1, nch - 1)
            blk.scalar(sc_fn)
            if reload:
                p.add(tag + "c", ntile * nch)
            p.add(tag + "r", ntile)
            p.add(tag + "ev", ntile * nch * 2)

            def v_fn(v, tag=tag, ntile=ntile, nch=nch, ncols=ncols,
                     reload=reload, npl=npl, smat=smat, bs=bs,
                     prev_t=prev_t):
                L = 0
                C = 0
                for i in range(ntile):
                    L += nch
                    v.wait_ge(p.sems[tag + "r"], bs["r"] + i + 1)
                    if nch > 1:
                        v.tensor_reduce(scol[:, 6:7], scol[:, 0:nch], AX,
                                        ALU.add)
                        v.drain()
                        src_c = scol[:, 6:7]
                    else:
                        src_c = scol[:, 0:1]
                    v.tensor_scalar(scol[:, 3:4], src_c, 0.5 / ncols,
                                    0.5e-5, ALU.mult, ALU.max)
                    v.tensor_scalar(scol[:, 4:5], src_c, -0.5 / ncols,
                                    -0.5e-5, ALU.mult, ALU.min)
                    v.tensor_scalar(smat[:, i:i + 1], src_c, 1.0 / ncols,
                                    1e-5, ALU.mult, ALU.max)
                    v.drain()
                    for ch in range(nch):
                        if reload:
                            v.wait_ge(p.sems[tag + "l" + str(L % 2)],
                                      bs["l" + str(L % 2)]
                                      + 16 * (L // 2 + 1))
                            w_t = wld[:, (L % 2) * CH:(L % 2) * CH + CH]
                        else:
                            w_t = wld[:, ((L - 1) % 2) * CH:
                                      ((L - 1) % 2) * CH + ncols]
                        if C < 2 and prev_t is not None:
                            v.wait_ge(p.sems[prev_t[0]], prev_t[1])
                        if C >= 2:
                            v.wait_ge(p.sems[tag + "t"],
                                      bs["t"] + 2 * (C - 1))
                        cw = CH if reload else ncols
                        if C >= 1:
                            v.drain()
                        v.tensor_scalar(ab[:, 0:cw], w_t, scol[:, 3:4], None,
                                        ALU.is_ge)
                        v.tensor_scalar(bb[:, 0:cw], w_t, scol[:, 4:5],
                                        None, ALU.is_le).then_inc(
                            p.sems[tag + "c"], 1)
                        v.drain()
                        v.tensor_tensor(tb0[:, (C % 2) * CH:(C % 2) * CH
                                            + cw],
                                        ab[:, 0:cw], bb[:, 0:cw],
                                        ALU.subtract).then_inc(
                            p.sems[tag + "v"], 1)
                        if reload:
                            L += 1
                        C += 1
            blk.vector(v_fn)
            p.add(tag + "c", ntile * nch)
            p.add(tag + "v", ntile * nch)

            def t_fn(t, tag=tag, ntile=ntile, nch=nch, bs=bs):
                for i in range(ntile):
                    for gch in range(nch):
                        C = i * nch + gch
                        t.wait_ge(p.sems[tag + "v"], bs["v"] + C + 1)
                        for g8 in range(2):
                            gidx = C * 2 + g8
                            if gidx >= 1:
                                t.wait_ge(p.sems[tag + "ev"],
                                          bs["ev"] + gidx)
                            for k in range(8):
                                c0 = (g8 * 8 + k) * 128
                                ins = t.transpose(
                                    PB[:, k * 128:(k + 1) * 128],
                                    tb0[:, (C % 2) * CH + c0:
                                        (C % 2) * CH + c0 + 128],
                                    sb["identB"][:, :])
                                if k == 7:
                                    ins.then_inc(p.sems[tag + "t"], 1)
            blk.tensor(t_fn)
            p.add(tag + "t", ntile * nch * 2)

            def g_fn(g, tag=tag, wdst=wdst, ntile=ntile, nch=nch, bs=bs,
                     rowmaj=rowmaj):
                for i in range(ntile):
                    for gch in range(nch):
                        for g8 in range(2):
                            gidx = (i * nch + gch) * 2 + g8
                            g.wait_ge(p.sems[tag + "ev"], bs["ev"] + gidx + 1)
                            src_ = tstg[:, (gidx % 2) * 1024:
                                        (gidx % 2) * 1024 + 1024].rearrange(
                                "p (k c) -> p k c", k=8)
                            cb0 = gch * 16 + g8 * 8
                            if rowmaj:
                                dst = wdst[i, cb0:cb0 + 8].rearrange(
                                    "k p c -> p k c")
                            else:
                                dst = wdst[cb0:cb0 + 8, i].rearrange(
                                    "k p c -> p k c")
                            g.dma_start(dst, src_).then_inc(
                                p.sems[tag + "s" + str(gidx % 2)], 16)
            blk.gpsimd(g_fn)
            for sl_ in range(2):
                ng = len([x for x in range(ntile * nch * 2) if x % 2 == sl_])
                p.add(tag + "s" + str(sl_), 16 * ng)
        def t2_fn(t):
            t.wait_ge(p.sems["w1v"], p.val("w1v"))
            t.wait_ge(p.sems["w2v"], p.val("w2v"))
            t.wait_ge(p.sems["w1ev"], p.val("w1ev"))
            t.wait_ge(p.sems["w2ev"], p.val("w2ev"))
            t.transpose(PS[0][0:64, 0:128], s1m[:, :], sb["identF"][:, :])
            t.transpose(PS[1][0:16, 0:128], s2m[:, :], sb["identF"][:, :])
            t.drain()
            t.nop().then_inc(p.sem("wst"), 1)
        blk.tensor(t2_fn)
        p.add("wst", 1)

        def v2_fn(v):
            v.wait_ge(p.sems["wst"], p.val("wst"))
            v.tensor_copy(sb["s1T"][0:64, :], PS[0][0:64, 0:128])
            v.tensor_copy(sb["s2T"][0:16, :], PS[1][0:16, 0:128]).then_inc(
                p.sem("wsv"), 1)
        blk.vector(v2_fn)
        p.add("wsv", 1)

        _phase_barrier(p, "phW", ("w1s0", "w1s1", "w2s0", "w2s1", "wsv"))

    p.begin_scope()
    with contextlib.ExitStack() as ph:
        e = ph.enter_context
        wos = e(nc.sbuf_tensor("wp_wos", [128, 2 * 2 * D], F32))
        for nm in ("wol0", "wol1", "wov", "wog0", "wog1"):
            p.sem(nm)
        bwo = {nm: p.val(nm) for nm in ("wol0", "wol1", "wov", "wog0",
                                        "wog1")}

        def sy3_fn(sy):
            for i in range(16):
                if i >= 2:
                    sy.wait_ge(p.sems["wov"], bwo["wov"] + i - 1)
                    sy.wait_ge(p.sems["wog" + str(i % 2)],
                               bwo["wog" + str(i % 2)] + 32 * (i // 2))
                sy.dma_start(wos[:, (i % 2) * 2 * D:(i % 2) * 2 * D + D],
                             T["wo"][i * 128:(i + 1) * 128, :]).then_inc(
                    p.sems["wol" + str(i % 2)], 16)
        blk.sync(sy3_fn)
        p.add("wol0", 16 * 8)
        p.add("wol1", 16 * 8)

        def v3_fn(v):
            for i in range(16):
                o = (i % 2) * 2 * D
                v.wait_ge(p.sems["wol" + str(i % 2)],
                          bwo["wol" + str(i % 2)] + 16 * (i // 2 + 1))
                hi = wos[:, o + D:o + 2 * D].bitcast(BF16)[:, 0:D]
                lo = wos[:, o + D:o + 2 * D].bitcast(BF16)[:, D:2 * D]
                v.tensor_copy(hi, wos[:, o:o + D])
                v.drain()
                v.tensor_tensor(lo, wos[:, o:o + D], hi,
                                ALU.subtract).then_inc(p.sems["wov"], 1)
        blk.vector(v3_fn)
        p.add("wov", 16)

        def g3_fn(g):
            for i in range(16):
                o = (i % 2) * 2 * D
                g.wait_ge(p.sems["wov"], bwo["wov"] + i + 1)
                hi = wos[:, o + D:o + 2 * D].bitcast(BF16)[:, 0:D]
                lo = wos[:, o + D:o + 2 * D].bitcast(BF16)[:, D:2 * D]
                g.dma_start(T["WOH"][i * 128:(i + 1) * 128, :],
                            hi).then_inc(p.sems["wog" + str(i % 2)], 16)
                g.dma_start(T["WOL"][i * 128:(i + 1) * 128, :],
                            lo).then_inc(p.sems["wog" + str(i % 2)], 16)
        blk.gpsimd(g3_fn)
        p.add("wog0", 32 * 8)
        p.add("wog1", 32 * 8)

        _phase_barrier(p, "phWS", ("wog0", "wog1"))


def ph_a2(p):
    nc, blk, T, sb = p.nc, p.blk, p.T, p.sb
    PS, PB, PBF = p.PS, p.PB, p.PBF

    p.begin_scope()
    with contextlib.ExitStack() as ph:
        e = ph.enter_context
        xt = e(nc.sbuf_tensor("a2_x", [128, 2 * D], F32))
        xn = e(nc.sbuf_tensor("a2_xn", [128, 2 * D], F32))
        sq = e(nc.sbuf_tensor("a2_sq", [128, D // 2], F32))
        cc = e(nc.sbuf_tensor("a2_cc", [128, 32], F32))
        pq = e(nc.sbuf_tensor("a2_p", [128, D], F32))
        c8 = e(nc.sbuf_tensor("a2_c8", [128, D], I8))
        qb1 = e(nc.sbuf_tensor("a2_b1", [128, D], BF16))
        qb2 = e(nc.sbuf_tensor("a2_b2", [128, D], BF16))
        sd = e(nc.sbuf_tensor("a2_sd", [128, D], BF16))
        ntm = e(nc.sbuf_tensor("a2_ntm", [128, D], BF16))
        xfh = e(nc.sbuf_tensor("a2_xfh", [128, 16 * 512], BF16))
        xfl = e(nc.sbuf_tensor("a2_xfl", [128, 16 * 512], BF16))
        rs = e(nc.sbuf_tensor("a2_rs", [128, 4 * 512], F32))
        rc16 = e(nc.sbuf_tensor("a2_rc16", [128, 512], I16))
        wqt = e(nc.sbuf_tensor("a2_wqt", [128, 2 * 512], F32))
        wqh = e(nc.sbuf_tensor("a2_wqh", [128, 4 * 512], BF16))
        wql = e(nc.sbuf_tensor("a2_wql", [128, 4 * 512], BF16))
        qstg = e(nc.sbuf_tensor("a2_qstg", [128, 4 * 512], F32))
        qhib = e(nc.sbuf_tensor("a2_qhib", [128, 4 * 512], BF16))
        qlob = e(nc.sbuf_tensor("a2_qlob", [128, 4 * 512], BF16))
        rsw = e(nc.sbuf_tensor("a2_rsw", [128, 512], F32))
        cosq = e(nc.sbuf_tensor("a2_cosq", [128, TOK], F32))
        sinq = e(nc.sbuf_tensor("a2_sinq", [128, TOK], F32))

        for nm in ("rtm", "rts", "rtv", "lnv", "lns", "a2l0", "a2l1",
                   "a2sx", "qsc", "qdv", "a2t", "a2en", "a2ex", "a2fin",
                   "qpl0", "qpl1", "qpv", "qpm", "qpdc", "qpec", "qpsw",
                   "qprp", "qphl", "qps", "qpe"):
            p.sem(nm)

        base_rtm = p.val("rtm")
        base_rts = p.val("rts")
        base_qps = p.val("qps")
        base_qphl = p.val("qphl")

        def t_tab(t):
            t.wait_ge(p.sems["sc0"], p.val("sc0"))
            t.wait_ge(p.sems["ld0"], p.val("ld0"))
            for half in range(2):
                if half == 1:
                    t.wait_ge(p.sems["rts"], base_rts + 1)
                t.matmul(PS[6][:, 0:512], sb["invfr"][:1, :],
                         sb["posq"][:1, half * 512:(half + 1) * 512],
                         start=True, stop=True).then_inc(p.sems["rtm"], 1)
        blk.tensor(t_tab)
        p.add("rtm", 2)

        tabs = []
        for half in range(2):
            t1 = p.add("rts", 1)
            t2 = p.add("rtv", 1)
            t3 = p.add("rts", 1)
            t4 = p.add("rtv", 1)
            tabs.append((t1, t2, t3, t4))

        def sc_tab(sc):
            for half in range(2):
                t1, t2, t3, t4 = tabs[half]
                cos_o = cosq[:, half * 512:(half + 1) * 512]
                sin_o = sinq[:, half * 512:(half + 1) * 512]
                sc.wait_ge(p.sems["rtm"], base_rtm + half + 1)
                sc.activation(rs[:, 0:512], PS[6][:, 0:512], AF.Identity)
                sc.drain()
                sc.activation(rs[:, 512:1024], PS[6][:, 0:512], AF.Identity,
                              scale=INV2PI).then_inc(p.sems["rts"], 1)
                sc.wait_ge(p.sems["rtv"], t2)
                sc.activation(sin_o, rs[:, 0:512], AF.Sin)
                sc.drain()
                sc.activation(rs[:, 512:1024], rs[:, 0:512], AF.Abs)
                sc.drain()
                sc.activation(cos_o, rs[:, 512:1024], AF.Sin, scale=-1.0,
                              bias=sb["cst"][:, 0:1]).then_inc(p.sems["rts"], 1)
        blk.scalar(sc_tab)

        def v_tab(v):
            for half in range(2):
                t1, t2, t3, t4 = tabs[half]
                sin_o = sinq[:, half * 512:(half + 1) * 512]
                v.wait_ge(p.sems["rts"], t1)
                v.tensor_copy(rc16[:, 0:512], rs[:, 512:1024])
                v.drain()
                b1r = rs[:, 1024:1536].bitcast(BF16)[:, 0:512]
                b2r = rs[:, 1024:1536].bitcast(BF16)[:, 512:1024]
                v.scalar_tensor_tensor(b1r, rs[:, 512:1024], -0.5,
                                       rc16[:, 0:512], ALU.add, ALU.is_ge)
                v.scalar_tensor_tensor(b2r, rs[:, 512:1024], 0.5,
                                       rc16[:, 0:512], ALU.add, ALU.is_le)
                v.drain()
                v.tensor_tensor(rs[:, 1536:2048], b1r, b2r, ALU.subtract)
                v.drain()
                v.tensor_tensor(rs[:, 1536:2048], rs[:, 1536:2048],
                                rc16[:, 0:512], ALU.add)
                v.drain()
                v.scalar_tensor_tensor(rs[:, 0:512], rs[:, 1536:2048], -CW1,
                                       rs[:, 0:512], ALU.mult, ALU.add)
                v.drain()
                v.scalar_tensor_tensor(rs[:, 0:512], rs[:, 1536:2048], -CW2,
                                       rs[:, 0:512], ALU.mult, ALU.add)
                v.drain()
                v.tensor_scalar(rs[:, 0:512], rs[:, 0:512], math.pi,
                                -math.pi, ALU.min, ALU.max).then_inc(
                    p.sems["rtv"], 1)
                v.wait_ge(p.sems["rts"], t3)
                v.tensor_scalar(sin_o, sin_o, sb["sgncol"][:, 0:1], None,
                                ALU.mult).then_inc(p.sems["rtv"], 1)
        blk.vector(v_tab)

        for half in range(2):
            bplan = []
            for tbl in range(4):
                tb = half * 4 + tbl
                d = {"tb": tb, "tbl": tbl}
                d["ld"] = p.add("a2l" + str(tb % 2), 16)
                d["ln1"] = p.add("lnv", 1)
                d["ln2"] = p.add("lns", 1)
                d["ln3"] = p.add("lnv", 1)
                d["ln4"] = p.add("lns", 1)
                d["sx"] = p.add("a2sx", 1)
                d["q1"] = p.add("qsc", 1)
                d["q2"] = p.add("qdv", 1)
                d["tt"] = [p.add("a2t", 1) for _ in range(6)]
                d["en"] = [p.add("a2en", 1) for _ in range(2)]
                d["ex"] = [p.add("a2ex", 1) for _ in range(4)]
                d["fin"] = p.add("a2fin", 1)
                bplan.append(d)
            allb = getattr(p, "_a2_allb", [])
            allb.extend(bplan)
            p._a2_allb = allb

            def sy_fn(sy, bplan=bplan, allb=list(allb)):
                for d in bplan:
                    tb = d["tb"]
                    if tb >= 2:
                        sy.wait_ge(p.sems["a2fin"], allb[tb - 2]["fin"])
                    sy.dma_start(
                        xt[:, (tb % 2) * D:(tb % 2) * D + D],
                        T["x_own"][tb * 128:(tb + 1) * 128, :]).then_inc(
                        p.sems["a2l" + str(tb % 2)], 16)
            blk.sync(sy_fn)

            def v_fn(v, bplan=bplan, allb=list(allb), half=half):
                for d in bplan:
                    tb, tbl = d["tb"], d["tbl"]
                    sl = (tb % 2) * D
                    co = (tb % 2) * 16
                    v.wait_ge(p.sems["a2l" + str(tb % 2)], d["ld"])
                    if tb >= 2:
                        v.wait_ge(p.sems["lns"], allb[tb - 2]["ln4"])
                    v.reduce_sum(cc[:, co:co + 1], xt[:, sl:sl + D], AX)
                    v.drain()
                    v.tensor_scalar(cc[:, co + 1:co + 2], cc[:, co:co + 1],
                                    1.0 / D, None, ALU.mult)
                    v.drain()
                    v.tensor_scalar(cc[:, co + 2:co + 3],
                                    cc[:, co + 1:co + 2], -1.0, None,
                                    ALU.mult).then_inc(p.sems["lnv"], 1)
                    v.wait_ge(p.sems["lns"], d["ln2"])
                    v.tensor_scalar(cc[:, co + 6:co + 7],
                                    cc[:, co + 1:co + 2],
                                    cc[:, co + 5:co + 6], -1.0, ALU.mult,
                                    ALU.mult).then_inc(p.sems["lnv"], 1)
                    v.wait_ge(p.sems["lns"], d["ln4"])
                    v.tensor_reduce(cc[:, co + 8:co + 9], xn[:, sl:sl + D],
                                    AX, ALU.max, apply_absolute_value=True)
                    v.drain()
                    v.tensor_scalar(cc[:, co + 9:co + 10],
                                    cc[:, co + 8:co + 9], 1e-5, None,
                                    ALU.max)
                    v.drain()
                    v.reciprocal(cc[:, co + 10:co + 11],
                                 cc[:, co + 9:co + 10])
                    v.drain()
                    v.tensor_scalar(cc[:, co + 11:co + 12],
                                    cc[:, co + 10:co + 11], 127.0, None,
                                    ALU.mult)
                    v.drain()
                    v.tensor_scalar(sb["rsxm"][:, tb:tb + 1],
                                    cc[:, co + 9:co + 10], 1.0 / 127.0, None,
                                    ALU.mult).then_inc(p.sems["a2sx"], 1)
                    v.wait_ge(p.sems["qsc"], d["q1"])
                    if tb >= 1:
                        v.wait_ge(p.sems["a2t"], allb[tb - 1]["tt"][1])
                    v.scalar_tensor_tensor(qb1[:, :], pq[:, :], -0.5,
                                           c8[:, :], ALU.add, ALU.is_ge)
                    v.scalar_tensor_tensor(qb2[:, :], pq[:, :], 0.5,
                                           c8[:, :], ALU.add, ALU.is_le)
                    v.drain()
                    v.tensor_tensor(sd[:, :], qb1[:, :], qb2[:, :],
                                    ALU.subtract)
                    v.drain()
                    v.tensor_tensor(ntm[:, :], sd[:, :], c8[:, :],
                                    ALU.add).then_inc(p.sems["qdv"], 1)
                    for g8 in range(2):
                        v.wait_ge(p.sems["a2t"], d["tt"][g8])
                        for k in range(8):
                            dc = g8 * 8 + k
                            v.tensor_copy(
                                sb["nfm"][:, dc * TOK + tb * 128:
                                          dc * TOK + tb * 128 + 128],
                                PB[:, k * 128:(k + 1) * 128])
                        v.drain()
                        v.memset(cc[:, 15:16], 0.0).then_inc(
                            p.sems["a2en"], 1)
                    for g4 in range(4):
                        v.wait_ge(p.sems["a2t"], d["tt"][2 + g4])
                        for k in range(4):
                            dc = g4 * 4 + k
                            col = dc * 512 + tbl * 128
                            v.tensor_copy(xfh[:, col:col + 128],
                                          PS[k][:, 0:128])
                            v.drain()
                            v.tensor_tensor(xfl[:, col:col + 128],
                                            PS[k][:, 0:128],
                                            xfh[:, col:col + 128],
                                            ALU.subtract)
                        v.drain()
                        v.memset(cc[:, 15:16], 0.0).then_inc(
                            p.sems["a2ex"], 1)
                    v.drain()
                    v.memset(cc[:, 15:16], 0.0).then_inc(
                        p.sems["a2fin"], 1)
            blk.vector(v_fn)

            def sc_fn(sc, bplan=bplan, allb=list(allb)):
                for d in bplan:
                    tb = d["tb"]
                    sl = (tb % 2) * D
                    co = (tb % 2) * 16
                    sc.wait_ge(p.sems["lnv"], d["ln1"])
                    sc.activation(sq[:, :], xt[:, sl:sl + D // 2], AF.Square,
                                  bias=cc[:, co + 2:co + 3],
                                  accum_out=cc[:, co + 3:co + 4])
                    sc.drain()
                    sc.activation(sq[:, :], xt[:, sl + D // 2:sl + D],
                                  AF.Square, bias=cc[:, co + 2:co + 3],
                                  accum_out=cc[:, co + 7:co + 8])
                    sc.drain()
                    sc.activation(cc[:, co + 3:co + 4],
                                  cc[:, co + 3:co + 4], AF.Identity,
                                  bias=cc[:, co + 7:co + 8])
                    sc.drain()
                    sc.activation(cc[:, co + 4:co + 5],
                                  cc[:, co + 3:co + 4], AF.Ln, bias=sb["cst"][:, 1:2],
                                  scale=1.0 / D)
                    sc.drain()
                    sc.activation(cc[:, co + 5:co + 6],
                                  cc[:, co + 4:co + 5], AF.Exp,
                                  scale=-0.5).then_inc(p.sems["lns"], 1)
                    sc.wait_ge(p.sems["lnv"], d["ln3"])
                    sc.activation(xn[:, sl:sl + D], xt[:, sl:sl + D],
                                  AF.Identity, bias=cc[:, co + 6:co + 7],
                                  scale=cc[:, co + 5:co + 6]).then_inc(
                        p.sems["lns"], 1)
                    sc.wait_ge(p.sems["a2sx"], d["sx"])
                    if tb >= 1:
                        sc.wait_ge(p.sems["qdv"], allb[tb - 1]["q2"])
                    sc.activation(pq[:, :], xn[:, sl:sl + D], AF.Identity,
                                  scale=cc[:, co + 11:co + 12])
                    sc.drain()
                    sc.activation(c8[:, :], xn[:, sl:sl + D], AF.Identity,
                                  scale=cc[:, co + 11:co + 12]).then_inc(
                        p.sems["qsc"], 1)
            blk.scalar(sc_fn)

            def t_fn(t, bplan=bplan, allb=list(allb)):
                for d in bplan:
                    tb = d["tb"]
                    sl = (tb % 2) * D
                    t.wait_ge(p.sems["qdv"], d["q2"])
                    for g8 in range(2):
                        gprev = tb * 2 + g8 - 1
                        if gprev >= 0:
                            t.wait_ge(p.sems["a2en"],
                                      allb[gprev // 2]["en"][gprev % 2])
                        for k in range(8):
                            dc = g8 * 8 + k
                            ins = t.transpose(PB[:, k * 128:(k + 1) * 128],
                                              ntm[:, dc * 128:(dc + 1) * 128],
                                              sb["identB"][:, :])
                            if k == 7:
                                ins.then_inc(p.sems["a2t"], 1)
                    t.wait_ge(p.sems["lns"], d["ln4"])
                    for g4 in range(4):
                        gprev = tb * 4 + g4 - 1
                        if gprev >= 0:
                            t.wait_ge(p.sems["a2ex"],
                                      allb[gprev // 4]["ex"][gprev % 4])
                        for k in range(4):
                            dc = g4 * 4 + k
                            ins = t.transpose(
                                PS[k][:, 0:128],
                                xn[:, sl + dc * 128:sl + (dc + 1) * 128],
                                sb["identF"][:, :])
                            if k == 3:
                                ins.then_inc(p.sems["a2t"], 1)
            blk.tensor(t_fn)

            qplan = []
            for qfg in range(6):
                for dc in range(16):
                    idx = qfg * 16 + dc
                    qplan.append({"qfg": qfg, "dc": dc,
                                  "ld": p.add("qpl" + str(idx % 2), 16),
                                  "v": p.add("qpv", 1)})
            qm = [p.add("qpm", 1) for _ in range(6)]

            def syq_fn(sy, qplan=qplan, half=half):
                for it in qplan:
                    qfg, dc = it["qfg"], it["dc"]
                    rope = qfg >= 4
                    wsrc = T["wqr"] if rope else T["wqn"]
                    csrc = (qfg - 4) * 512 if rope else qfg * 512
                    idx = qfg * 16 + dc
                    if idx >= 2:
                        sy.wait_ge(p.sems["qpv"], qplan[idx - 2]["v"])
                    sy.dma_start(
                        wqt[:, (idx % 2) * 512:(idx % 2) * 512 + 512],
                        wsrc[dc * 128:(dc + 1) * 128,
                             csrc:csrc + 512]).then_inc(
                        p.sems["qpl" + str(idx % 2)], 16)
            blk.sync(syq_fn)

            base_qpdc = p.val("qpdc")
            p.add("qpdc", 96)

            def mk_split(it):
                def f(v, it=it):
                    idx = it["qfg"] * 16 + it["dc"]
                    ot = (idx % 2) * 512
                    o = (idx % 4) * 512
                    v.wait_ge(p.sems["qpl" + str(idx % 2)], it["ld"])
                    if idx >= 4:
                        v.wait_ge(p.sems["qpdc"], base_qpdc + idx - 3)
                    v.tensor_copy(wqh[:, o:o + 512], wqt[:, ot:ot + 512])
                    v.drain()
                    v.tensor_tensor(wql[:, o:o + 512], wqt[:, ot:ot + 512],
                                    wqh[:, o:o + 512],
                                    ALU.subtract).then_inc(p.sems["qpv"], 1)
                return f

            base_qpe = p.val("qpe")

            def tq_fn(t, qplan=qplan, bplan=bplan, base_qpe=base_qpe):
                t.wait_ge(p.sems["a2fin"], bplan[3]["fin"])
                for qfg in range(6):
                    for dc in range(16):
                        idx = qfg * 16 + dc
                        o = (idx % 4) * 512
                        t.wait_ge(p.sems["qpv"], qplan[idx]["v"])
                        if dc == 0 and (qfg > 0 or base_qpe > 0):
                            t.wait_ge(p.sems["qpe"], base_qpe + qfg)
                        for qi in range(4):
                            bank = PS[qi][:, 0:512]
                            lo = dc * 512
                            ws = slice(o + qi * 128, o + qi * 128 + 128)
                            t.matmul(bank, wqh[:, ws], xfh[:, lo:lo + 512],
                                     start=(dc == 0), stop=False)
                            t.matmul(bank, wqh[:, ws], xfl[:, lo:lo + 512],
                                     start=False, stop=False)
                            ins = t.matmul(bank, wql[:, ws],
                                           xfh[:, lo:lo + 512],
                                           start=False, stop=(dc == 15))
                            if qi == 3:
                                ins.then_inc(p.sems["qpdc"], 1)
                    t.drain()
                    t.nop().then_inc(p.sems["qpm"], 1)
            blk.tensor(tq_fn)

            sc_steps, v_steps, g_steps = [], [], []
            for qfg in range(6):
                rope = qfg >= 4
                for dc in range(16):
                    v_steps.append(mk_split(qplan[qfg * 16 + dc]))
                for qi in range(4):
                    bank = PS[qi][:, 0:512]
                    stg = qstg[:, qi * 512:qi * 512 + 512]
                    hib = qhib[:, qi * 512:qi * 512 + 512]
                    lob = qlob[:, qi * 512:qi * 512 + 512]
                    qfc = (qfg - 4) * 4 + qi if rope else qfg * 4 + qi
                    gq = half * 6 + qfg
                    tec = p.add("qpec", 1)
                    sc_steps.append(lambda sc, bank=bank, stg=stg,
                                    tm=qm[qfg], prev_hl=base_qphl + 4 * gq,
                                    gq=gq: (
                        sc.wait_ge(p.sems["qpm"], tm),
                        (sc.wait_ge(p.sems["qphl"], prev_hl)
                         if gq >= 1 else None),
                        sc.activation(stg, bank, AF.Identity).then_inc(
                            p.sems["qpec"], 1)))
                    if rope:
                        prev_rp = p.val("qprp")
                        tsw = p.add("qpsw", 64)
                        g_steps.append(lambda g, stg=stg, tec=tec,
                                       prev_rp=prev_rp: (
                            g.wait_ge(p.sems["qpec"], tec),
                            (g.wait_ge(p.sems["qprp"], prev_rp)
                             if prev_rp > 0 else None),
                            g.dma_start(rsw[0:32, :],
                                        stg[32:64, :]).then_inc(
                                p.sems["qpsw"], 16),
                            g.dma_start(rsw[32:64, :],
                                        stg[0:32, :]).then_inc(
                                p.sems["qpsw"], 16),
                            g.dma_start(rsw[64:96, :],
                                        stg[96:128, :]).then_inc(
                                p.sems["qpsw"], 16),
                            g.dma_start(rsw[96:128, :],
                                        stg[64:96, :]).then_inc(
                                p.sems["qpsw"], 16)))
                        trp = p.add("qprp", 1)
                        v_steps.append(lambda v, stg=stg, half=half,
                                       tsw=tsw: (
                            v.wait_ge(p.sems["qpsw"], tsw),
                            v.tensor_tensor(
                                stg, stg,
                                cosq[:, half * 512:half * 512 + 512],
                                ALU.mult),
                            v.tensor_tensor(
                                rsw[:, :], rsw[:, :],
                                sinq[:, half * 512:half * 512 + 512],
                                ALU.mult),
                            v.drain(),
                            v.tensor_tensor(stg, stg, rsw[:, :],
                                            ALU.add).then_inc(
                                p.sems["qprp"], 1)))
                        rdy = ("qprp", trp)
                    else:
                        rdy = ("qpec", tec)
                    thl = p.add("qphl", 1)
                    v_steps.append(lambda v, stg=stg, hib=hib, lob=lob,
                                   rdy=rdy, gq=gq, qi=qi: (
                        v.wait_ge(p.sems[rdy[0]], rdy[1]),
                        (v.wait_ge(p.sems["qps"], base_qps + 128 * 4 * gq)
                         if gq >= 1 and qi == 0 else None),
                        v.tensor_copy(hib, stg),
                        v.drain(),
                        v.tensor_tensor(lob, stg, hib,
                                        ALU.subtract).then_inc(
                            p.sems["qphl"], 1)))
                    tst = p.add("qps", 128)
                    g_steps.append(lambda g, qfc=qfc, half=half, rope=rope,
                                   thl=thl, hib=hib, lob=lob: (
                        g.wait_ge(p.sems["qphl"], thl),
                        [g.dma_start(
                            (T["QRH"] if rope else T["QNH"])[qfc,
                                                             half * 4 + sbk],
                            hib.rearrange("p (s c) -> p s c", s=4)[:, sbk]
                        ).then_inc(p.sems["qps"], 16) for sbk in range(4)],
                        [g.dma_start(
                            (T["QRL"] if rope else T["QNL"])[qfc,
                                                             half * 4 + sbk],
                            lob.rearrange("p (s c) -> p s c", s=4)[:, sbk]
                        ).then_inc(p.sems["qps"], 16) for sbk in range(4)]))
                tqe = p.add("qpe", 1)
                v_steps.append(lambda v, tqe=tqe: (
                    v.drain(),
                    v.memset(cc[:, 15:16], 0.0).then_inc(
                        p.sems["qpe"], 1)))

            blk.scalar(lambda sc, steps=sc_steps: [f(sc) for f in steps])
            blk.vector(lambda v, steps=v_steps: [f(v) for f in steps])
            blk.gpsimd(lambda g, steps=g_steps: [f(g) for f in steps])

        _phase_barrier(p, "phA2", ("qps", "a2fin", "qpe"))


def ph_ackv_akv(p):
    nc, blk, T, sb = p.nc, p.blk, p.T, p.sb
    PS, PB = p.PS, p.PB

    p.begin_scope()
    with contextlib.ExitStack() as ph:
        e = ph.enter_context
        xt = e(nc.sbuf_tensor("ck_x", [128, 2 * D], F32))
        xn = e(nc.sbuf_tensor("ck_xn", [128, 2 * D], F32))
        sq = e(nc.sbuf_tensor("ck_sq", [128, D // 2], F32))
        cc = e(nc.sbuf_tensor("ck_cc", [128, 32], F32))
        xfh = e(nc.sbuf_tensor("ck_xfh", [128, 16 * 512], BF16))
        xfl = e(nc.sbuf_tensor("ck_xfl", [128, 16 * 512], BF16))
        wdh = e(nc.sbuf_tensor("ck_wdh", [128, 16 * LAT], BF16))
        wdl = e(nc.sbuf_tensor("ck_wdl", [128, 16 * LAT], BF16))
        wrh = e(nc.sbuf_tensor("ck_wrh", [128, 16 * DR], BF16))
        wrl = e(nc.sbuf_tensor("ck_wrl", [128, 16 * DR], BF16))
        wtmp = e(nc.sbuf_tensor("ck_wtmp", [128, LAT + DR], F32))
        ckhs = e(nc.sbuf_tensor("ck_ckhs", [128, 4 * 512], BF16))
        ckls = e(nc.sbuf_tensor("ck_ckls", [128, 4 * 512], BF16))
        rs = e(nc.sbuf_tensor("ck_rs", [128, 4 * 512], F32))
        rc16 = e(nc.sbuf_tensor("ck_rc16", [128, 512], I16))
        ctab = e(nc.sbuf_tensor("ck_ctab", [128, 512], F32))
        stab = e(nc.sbuf_tensor("ck_stab", [128, 512], F32))
        krr = e(nc.sbuf_tensor("ck_krr", [128, 512], F32))
        krw = e(nc.sbuf_tensor("ck_krw", [128, 512], F32))
        posg = e(nc.sbuf_tensor("ck_posg", [1, 512], F32))

        for nm in ("ckwl", "ckwv", "ckl0", "ckl1", "ckt", "ckev", "ckm",
                   "ckec", "cklo", "cksw", "ckkr", "ckst", "ckfin", "ckdup",
                   "ckpg"):
            p.sem(nm)

        bw = {"l": p.val("ckwl"), "v": p.val("ckwv")}

        def syw(sy):
            for dc in range(16):
                if dc >= 1:
                    sy.wait_ge(p.sems["ckwv"], bw["v"] + dc)
                sy.dma_start(wtmp[:, 0:LAT],
                             T["wdkv"][dc * 128:(dc + 1) * 128, :]).then_inc(
                    p.sems["ckwl"], 16)
                sy.dma_start(wtmp[:, LAT:LAT + DR],
                             T["wkr"][dc * 128:(dc + 1) * 128, :]).then_inc(
                    p.sems["ckwl"], 16)
        blk.sync(syw)
        p.add("ckwl", 32 * 16)

        def vw(v):
            for dc in range(16):
                v.wait_ge(p.sems["ckwl"], bw["l"] + 32 * (dc + 1))
                v.tensor_copy(wdh[:, dc * LAT:(dc + 1) * LAT], wtmp[:, 0:LAT])
                v.tensor_copy(wrh[:, dc * DR:(dc + 1) * DR],
                              wtmp[:, LAT:LAT + DR])
                v.drain()
                v.tensor_tensor(wdl[:, dc * LAT:(dc + 1) * LAT],
                                wtmp[:, 0:LAT],
                                wdh[:, dc * LAT:(dc + 1) * LAT], ALU.subtract)
                v.tensor_tensor(wrl[:, dc * DR:(dc + 1) * DR],
                                wtmp[:, LAT:LAT + DR],
                                wrh[:, dc * DR:(dc + 1) * DR],
                                ALU.subtract).then_inc(p.sems["ckwv"], 1)
        blk.vector(vw)
        p.add("ckwv", 16)

        gplan = []
        for g in range(8):
            d = {"g": g, "ld": [], "ln": [], "tt": [], "ev": []}
            for bi in range(4):
                d["ld"].append(p.add("ckl" + str((g * 4 + bi) % 2), 16))
                d["ln"].append(tuple(p.add(nm, 1) for nm in
                                     ("lnv", "lns", "lnv", "lns")))
                d["tt"].append([p.add("ckt", 1) for _ in range(4)])
                d["ev"].append([p.add("ckev", 1) for _ in range(4)])
            d["pg"] = p.add("ckpg", 1)
            d["tabm"] = p.add("ckm", 1)
            d["rt1"] = p.add("rts", 1)
            d["rt2"] = p.add("rtv", 1)
            d["rt3"] = p.add("rts", 1)
            d["rt4"] = p.add("rtv", 1)
            d["mm"] = p.add("ckm", 1)
            d["ec"] = p.add("ckec", 1)
            d["lo"] = p.add("cklo", 1)
            d["st"] = p.add("ckst", 128)
            d["sw"] = p.add("cksw", 32)
            d["kr"] = p.add("ckkr", 1)
            d["fin"] = p.add("ckfin", 1)
            gplan.append(d)

        def sy_fn(sy):
            for d in gplan:
                g = d["g"]
                for bi in range(4):
                    gb = g * 4 + bi
                    if gb >= 2:
                        pg, pb = divmod(gb - 2, 4)
                        sy.wait_ge(p.sems["lns"], gplan[pg]["ln"][pb][3])
                    sy.dma_start(
                        xt[:, (gb % 2) * D:(gb % 2) * D + D],
                        T["x_full"][gb * 128:(gb + 1) * 128, :]).then_inc(
                        p.sems["ckl" + str(gb % 2)], 16)
        blk.sync(sy_fn)

        def v_fn(v):
            for d in gplan:
                g = d["g"]
                for bi in range(4):
                    gb = g * 4 + bi
                    co = (gb % 2) * 16
                    sl = (gb % 2) * D
                    v.wait_ge(p.sems["ckl" + str((g * 4 + bi) % 2)],
                              d["ld"][bi])
                    if gb >= 2:
                        pg, pb = divmod(gb - 2, 4)
                        v.wait_ge(p.sems["lns"], gplan[pg]["ln"][pb][3])
                    v.reduce_sum(cc[:, co:co + 1], xt[:, sl:sl + D], AX)
                    v.drain()
                    v.tensor_scalar(cc[:, co + 1:co + 2], cc[:, co:co + 1],
                                    1.0 / D, None, ALU.mult)
                    v.drain()
                    v.tensor_scalar(cc[:, co + 2:co + 3],
                                    cc[:, co + 1:co + 2], -1.0, None,
                                    ALU.mult).then_inc(p.sems["lnv"], 1)
                    v.wait_ge(p.sems["lns"], d["ln"][bi][1])
                    v.tensor_scalar(cc[:, co + 6:co + 7],
                                    cc[:, co + 1:co + 2],
                                    cc[:, co + 5:co + 6], -1.0, ALU.mult,
                                    ALU.mult).then_inc(p.sems["lnv"], 1)
                    for g4 in range(4):
                        v.wait_ge(p.sems["ckt"], d["tt"][bi][g4])
                        if g >= 1 and bi == 0 and g4 == 0:
                            v.wait_ge(p.sems["ckm"], gplan[g - 1]["mm"])
                        for k in range(4):
                            dc = g4 * 4 + k
                            col = dc * 512 + bi * 128
                            v.tensor_copy(xfh[:, col:col + 128],
                                          PS[k][:, 0:128])
                            v.drain()
                            v.tensor_tensor(xfl[:, col:col + 128],
                                            PS[k][:, 0:128],
                                            xfh[:, col:col + 128],
                                            ALU.subtract)
                        v.drain()
                        v.memset(cc[:, 15:16], 0.0).then_inc(
                            p.sems["ckev"], 1)
                v.wait_ge(p.sems["rts"], d["rt1"])
                v.tensor_copy(rc16[:, 0:512], rs[:, 512:1024])
                v.drain()
                b1r = rs[:, 1024:1536].bitcast(BF16)[:, 0:512]
                b2r = rs[:, 1024:1536].bitcast(BF16)[:, 512:1024]
                v.scalar_tensor_tensor(b1r, rs[:, 512:1024], -0.5,
                                       rc16[:, 0:512], ALU.add, ALU.is_ge)
                v.scalar_tensor_tensor(b2r, rs[:, 512:1024], 0.5,
                                       rc16[:, 0:512], ALU.add, ALU.is_le)
                v.drain()
                v.tensor_tensor(rs[:, 1536:2048], b1r, b2r, ALU.subtract)
                v.drain()
                v.tensor_tensor(rs[:, 1536:2048], rs[:, 1536:2048],
                                rc16[:, 0:512], ALU.add)
                v.drain()
                v.scalar_tensor_tensor(rs[:, 0:512], rs[:, 1536:2048], -CW1,
                                       rs[:, 0:512], ALU.mult, ALU.add)
                v.drain()
                v.scalar_tensor_tensor(rs[:, 0:512], rs[:, 1536:2048], -CW2,
                                       rs[:, 0:512], ALU.mult, ALU.add)
                v.drain()
                v.tensor_scalar(rs[:, 0:512], rs[:, 0:512], math.pi,
                                -math.pi, ALU.min, ALU.max).then_inc(
                    p.sems["rtv"], 1)
                v.wait_ge(p.sems["rts"], d["rt3"])
                v.tensor_scalar(stab[:, :], stab[:, :], sb["sgncol"][:, 0:1],
                                None, ALU.mult).then_inc(p.sems["rtv"], 1)
                v.wait_ge(p.sems["ckec"], d["ec"])
                for lc in range(4):
                    v.scalar_tensor_tensor(
                        ckls[:, lc * 512:(lc + 1) * 512], PS[lc][:, 0:512],
                        1.0, ckhs[:, lc * 512:(lc + 1) * 512],
                        ALU.mult, ALU.subtract)
                v.drain()
                v.memset(cc[:, 15:16], 0.0).then_inc(
                    p.sems["cklo"], 1)
                v.wait_ge(p.sems["cksw"], d["sw"])
                v.tensor_tensor(krr[0:64, :], krr[0:64, :], ctab[0:64, :],
                                ALU.mult)
                v.tensor_tensor(krw[0:64, :], krw[0:64, :], stab[0:64, :],
                                ALU.mult)
                v.drain()
                v.tensor_tensor(krr[0:64, :], krr[0:64, :], krw[0:64, :],
                                ALU.add)
                v.drain()
                v.tensor_copy(sb["krH"][0:64, g * 512:(g + 1) * 512],
                              krr[0:64, :])
                v.drain()
                v.tensor_tensor(sb["krL"][0:64, g * 512:(g + 1) * 512],
                                krr[0:64, :],
                                sb["krH"][0:64, g * 512:(g + 1) * 512],
                                ALU.subtract).then_inc(p.sems["ckkr"], 1)
                v.drain()
                v.memset(cc[:, 15:16], 0.0).then_inc(
                    p.sems["ckfin"], 1)
        blk.vector(v_fn)

        def sc_fn(sc):
            for d in gplan:
                g = d["g"]
                for bi in range(4):
                    gb = g * 4 + bi
                    co = (gb % 2) * 16
                    sl = (gb % 2) * D
                    sc.wait_ge(p.sems["lnv"], d["ln"][bi][0])
                    sc.activation(sq[:, :], xt[:, sl:sl + D // 2], AF.Square,
                                  bias=cc[:, co + 2:co + 3],
                                  accum_out=cc[:, co + 3:co + 4])
                    sc.drain()
                    sc.activation(sq[:, :], xt[:, sl + D // 2:sl + D],
                                  AF.Square, bias=cc[:, co + 2:co + 3],
                                  accum_out=cc[:, co + 7:co + 8])
                    sc.drain()
                    sc.activation(cc[:, co + 3:co + 4],
                                  cc[:, co + 3:co + 4], AF.Identity,
                                  bias=cc[:, co + 7:co + 8])
                    sc.drain()
                    sc.activation(cc[:, co + 4:co + 5],
                                  cc[:, co + 3:co + 4], AF.Ln, bias=sb["cst"][:, 1:2],
                                  scale=1.0 / D)
                    sc.drain()
                    sc.activation(cc[:, co + 5:co + 6],
                                  cc[:, co + 4:co + 5], AF.Exp,
                                  scale=-0.5).then_inc(p.sems["lns"], 1)
                    sc.wait_ge(p.sems["lnv"], d["ln"][bi][2])
                    if gb >= 2:
                        pg, pb = divmod(gb - 2, 4)
                        sc.wait_ge(p.sems["ckt"], gplan[pg]["tt"][pb][3])
                    sc.activation(xn[:, sl:sl + D], xt[:, sl:sl + D],
                                  AF.Identity, bias=cc[:, co + 6:co + 7],
                                  scale=cc[:, co + 5:co + 6]).then_inc(
                        p.sems["lns"], 1)
                sc.wait_ge(p.sems["ckm"], d["tabm"])
                sc.activation(rs[:, 0:512], PS[4][:, 0:512], AF.Identity)
                sc.drain()
                sc.activation(rs[:, 512:1024], PS[4][:, 0:512], AF.Identity,
                              scale=INV2PI).then_inc(p.sems["rts"], 1)
                sc.wait_ge(p.sems["rtv"], d["rt2"])
                sc.activation(stab[:, :], rs[:, 0:512], AF.Sin)
                sc.drain()
                sc.activation(rs[:, 512:1024], rs[:, 0:512], AF.Abs)
                sc.drain()
                sc.activation(ctab[:, :], rs[:, 512:1024], AF.Sin, scale=-1.0,
                              bias=sb["cst"][:, 0:1]).then_inc(p.sems["rts"], 1)
                sc.wait_ge(p.sems["ckm"], d["mm"])
                if g >= 1:
                    sc.wait_ge(p.sems["ckst"], gplan[g - 1]["st"])
                    sc.wait_ge(p.sems["ckkr"], gplan[g - 1]["kr"])
                for lc in range(4):
                    sc.activation(ckhs[:, lc * 512:(lc + 1) * 512],
                                  PS[lc][:, 0:512], AF.Identity)
                sc.drain()
                sc.activation(krr[0:64, :], PS[5][0:64, 0:512],
                              AF.Identity).then_inc(p.sems["ckec"], 1)
        blk.scalar(sc_fn)

        def t_fn(t):
            t.wait_ge(p.sems["ckwv"], p.val("ckwv"))
            for d in gplan:
                g = d["g"]
                if g >= 1:
                    t.wait_ge(p.sems["cklo"], gplan[g - 1]["lo"])
                    t.wait_ge(p.sems["ckec"], gplan[g - 1]["ec"])
                for bi in range(4):
                    gb = g * 4 + bi
                    t.wait_ge(p.sems["lns"], d["ln"][bi][3])
                    for g4 in range(4):
                        gid = gb * 4 + g4
                        if gid >= 1:
                            pg2, rem = divmod(gid - 1, 16)
                            pbi, pg4 = divmod(rem, 4)
                            t.wait_ge(p.sems["ckev"],
                                      gplan[pg2]["ev"][pbi][pg4])
                        for k in range(4):
                            dc = g4 * 4 + k
                            ins = t.transpose(
                                PS[k][:, 0:128],
                                xn[:, (gb % 2) * D + dc * 128:
                                   (gb % 2) * D + dc * 128 + 128],
                                sb["identF"][:, :])
                            if k == 3:
                                ins.then_inc(p.sems["ckt"], 1)
                t.wait_ge(p.sems["ckpg"], d["pg"])
                if g >= 1:
                    t.wait_ge(p.sems["rts"], gplan[g - 1]["rt3"])
                t.matmul(PS[4][:, 0:512], sb["invfr"][:1, :], posg[:1, :],
                         start=True, stop=True).then_inc(p.sems["ckm"], 1)
                t.wait_ge(p.sems["ckev"], d["ev"][3][3])
                for dc in range(16):
                    for lc in range(4):
                        w1s = slice(dc * LAT + lc * 128,
                                    dc * LAT + lc * 128 + 128)
                        t.matmul(PS[lc][:, 0:512], wdh[:, w1s],
                                 xfh[:, dc * 512:(dc + 1) * 512],
                                 start=(dc == 0), stop=False)
                        t.matmul(PS[lc][:, 0:512], wdh[:, w1s],
                                 xfl[:, dc * 512:(dc + 1) * 512],
                                 start=False, stop=False)
                        t.matmul(PS[lc][:, 0:512], wdl[:, w1s],
                                 xfh[:, dc * 512:(dc + 1) * 512],
                                 start=False, stop=(dc == 15))
                    wrs = slice(dc * DR, (dc + 1) * DR)
                    t.matmul(PS[5][0:64, 0:512], wrh[:, wrs],
                             xfh[:, dc * 512:(dc + 1) * 512],
                             start=(dc == 0), stop=False)
                    t.matmul(PS[5][0:64, 0:512], wrh[:, wrs],
                             xfl[:, dc * 512:(dc + 1) * 512],
                             start=False, stop=False)
                    ins = t.matmul(PS[5][0:64, 0:512], wrl[:, wrs],
                                   xfh[:, dc * 512:(dc + 1) * 512],
                                   start=False, stop=(dc == 15))
                    if dc == 15:
                        ins.then_inc(p.sems["ckm"], 1)
        blk.tensor(t_fn)

        def g_fn(g_):
            for d in gplan:
                g = d["g"]
                if g >= 1:
                    g_.wait_ge(p.sems["ckm"], gplan[g - 1]["tabm"])
                g_.iota(posg[:1, :], pattern=[[1, 512]], base=g * 512,
                        channel_multiplier=0,
                        allow_small_or_imprecise_dtypes=True)
                g_.drain()
                g_.nop().then_inc(p.sems["ckpg"], 1)
                g_.wait_ge(p.sems["ckec"], d["ec"])
                g_.dma_start(krw[0:32, :], krr[32:64, :]).then_inc(
                    p.sems["cksw"], 16)
                g_.dma_start(krw[32:64, :], krr[0:32, :]).then_inc(
                    p.sems["cksw"], 16)
                g_.wait_ge(p.sems["cklo"], d["lo"])
                for lc in range(4):
                    g_.dma_start(T["CKH"][lc, :, g * 512:(g + 1) * 512],
                                 ckhs[:, lc * 512:(lc + 1) * 512]).then_inc(
                        p.sems["ckst"], 16)
                    g_.dma_start(T["CKL"][lc, :, g * 512:(g + 1) * 512],
                                 ckls[:, lc * 512:(lc + 1) * 512]).then_inc(
                        p.sems["ckst"], 16)
        blk.gpsimd(g_fn)

        def gdup(g):
            g.wait_ge(p.sems["ckkr"], p.val("ckkr"))
            g.dma_start(sb["krH"][64:128, :],
                        sb["krH"][0:64, :]).then_inc(p.sems["ckdup"], 16)
            g.dma_start(sb["krL"][64:128, :],
                        sb["krL"][0:64, :]).then_inc(p.sems["ckdup"], 16)
        blk.gpsimd(gdup)
        p.add("ckdup", 32)

        _phase_barrier(p, "phCK", ("ckfin", "ckst", "ckdup"))

    p.begin_scope()
    with contextlib.ExitStack() as ph:
        e = ph.enter_context
        wukh = e(nc.sbuf_tensor("kv_wukh", [128, 4 * 2048], BF16))
        wukl = e(nc.sbuf_tensor("kv_wukl", [128, 4 * 2048], BF16))
        wuvh = e(nc.sbuf_tensor("kv_wuvh", [128, 4 * 2048], BF16))
        wuvl = e(nc.sbuf_tensor("kv_wuvl", [128, 4 * 2048], BF16))
        wtm2 = e(nc.sbuf_tensor("kv_wtmp", [128, 2 * 2048], F32))
        ckh = e(nc.sbuf_tensor("kv_ckh", [128, 2 * 2048], BF16))
        ckl = e(nc.sbuf_tensor("kv_ckl", [128, 2 * 2048], BF16))
        kstg = e(nc.sbuf_tensor("kv_kstg", [128, 4 * 512], BF16))
        kstl = e(nc.sbuf_tensor("kv_kstl", [128, 4 * 512], BF16))
        vstg = e(nc.sbuf_tensor("kv_vstg", [128, 4 * 512], BF16))
        vstl = e(nc.sbuf_tensor("kv_vstl", [128, 4 * 512], BF16))

        for nm in ("kvwl", "kvwv", "kvcl0", "kvcl1", "kvm", "kvhe", "kvle",
                   "kvks0", "kvks1", "kvks2", "kvks3",
                   "kvvm", "kvve", "kvvl", "kvvs0", "kvvs1", "kvvs2", "kvvs3",
                   "krs"):
            p.sem(nm)
        bkv = {nm: p.val(nm) for nm in ("kvwl", "kvwv")}

        def syw2(sy):
            for lc in range(4):
                if lc >= 1:
                    sy.wait_ge(p.sems["kvwv"], bkv["kvwv"] + lc)
                sy.dma_start(wtm2[:, 0:2048],
                             T["wuk"][lc * 128:(lc + 1) * 128, :]).then_inc(
                    p.sems["kvwl"], 16)
                sy.dma_start(wtm2[:, 2048:4096],
                             T["wuv"][lc * 128:(lc + 1) * 128, :]).then_inc(
                    p.sems["kvwl"], 16)
        blk.sync(syw2)
        p.add("kvwl", 32 * 4)

        def vw2(v):
            for lc in range(4):
                v.wait_ge(p.sems["kvwl"], bkv["kvwl"] + 32 * (lc + 1))
                v.tensor_copy(wukh[:, lc * 2048:(lc + 1) * 2048],
                              wtm2[:, 0:2048])
                v.tensor_copy(wuvh[:, lc * 2048:(lc + 1) * 2048],
                              wtm2[:, 2048:4096])
                v.drain()
                v.tensor_tensor(wukl[:, lc * 2048:(lc + 1) * 2048],
                                wtm2[:, 0:2048],
                                wukh[:, lc * 2048:(lc + 1) * 2048],
                                ALU.subtract)
                v.tensor_tensor(wuvl[:, lc * 2048:(lc + 1) * 2048],
                                wtm2[:, 2048:4096],
                                wuvh[:, lc * 2048:(lc + 1) * 2048],
                                ALU.subtract).then_inc(p.sems["kvwv"], 1)
        blk.vector(vw2)
        p.add("kvwv", 4)

        kgplan = []
        for kg in range(8):
            d = {"kg": kg}
            d["cl"] = p.add("kvcl" + str(kg % 2), 128)
            d["k"] = [p.add("kvm", 1) for _ in range(16)]
            d["khe"] = [p.add("kvhe", 1) for _ in range(16)]
            d["kle"] = [p.add("kvle", 1) for _ in range(16)]
            d["kst"] = [p.add("kvks" + str((kg * 16 + j) % 4), 32)
                        for j in range(16)]
            d["v"] = [p.add("kvvm", 1) for _ in range(16)]
            d["ve"] = [p.add("kvve", 1) for _ in range(16)]
            d["vl"] = [p.add("kvvl", 1) for _ in range(16)]
            d["vst"] = [p.add("kvvs" + str((kg * 16 + j) % 4), 32)
                        for j in range(16)]
            kgplan.append(d)

        def syc(sy):
            for d in kgplan:
                kg = d["kg"]
                if kg >= 2:
                    sy.wait_ge(p.sems["kvvm"], kgplan[kg - 2]["v"][15])
                o = (kg % 2) * 2048
                for lc in range(4):
                    sy.dma_start(ckh[:, o + lc * 512:o + (lc + 1) * 512],
                                 T["CKH"][lc, :, kg * 512:(kg + 1) * 512]
                                 ).then_inc(p.sems["kvcl" + str(kg % 2)], 16)
                    sy.dma_start(ckl[:, o + lc * 512:o + (lc + 1) * 512],
                                 T["CKL"][lc, :, kg * 512:(kg + 1) * 512]
                                 ).then_inc(p.sems["kvcl" + str(kg % 2)], 16)
        blk.sync(syc)

        def t_kv(t):
            t.wait_ge(p.sems["kvwv"], p.val("kvwv"))
            for d in kgplan:
                kg = d["kg"]
                o = (kg % 2) * 2048
                t.wait_ge(p.sems["kvcl" + str(kg % 2)], d["cl"])
                for kfc in range(16):
                    i = kg * 16 + kfc
                    bank = PS[i % 4]
                    if i >= 4:
                        pi = i - 4
                        t.wait_ge(p.sems["kvle"],
                                  kgplan[pi // 16]["kle"][pi % 16])
                    for lc in range(4):
                        ws = slice(lc * 2048 + kfc * 128,
                                   lc * 2048 + kfc * 128 + 128)
                        cs = slice(o + lc * 512, o + (lc + 1) * 512)
                        t.matmul(bank[:, 0:512], wukh[:, ws], ckh[:, cs],
                                 start=(lc == 0), stop=False)
                        t.matmul(bank[:, 0:512], wukh[:, ws], ckl[:, cs],
                                 start=False, stop=False)
                        ins = t.matmul(bank[:, 0:512], wukl[:, ws],
                                       ckh[:, cs],
                                       start=False, stop=(lc == 3))
                        if lc == 3:
                            ins.then_inc(p.sems["kvm"], 1)
                for tb4 in range(4):
                    for vfc in range(4):
                        i = kg * 16 + tb4 * 4 + vfc
                        bank = PS[4 + i % 3]
                        if i >= 3:
                            pi = i - 3
                            t.wait_ge(p.sems["kvvl"],
                                      kgplan[pi // 16]["vl"][pi % 16])
                        for lc in range(4):
                            cs = slice(o + lc * 512 + tb4 * 128,
                                       o + lc * 512 + tb4 * 128 + 128)
                            ws = slice(lc * 2048 + vfc * 512,
                                       lc * 2048 + vfc * 512 + 512)
                            t.matmul(bank[:, 0:512], ckh[:, cs], wuvh[:, ws],
                                     start=(lc == 0), stop=False)
                            t.matmul(bank[:, 0:512], ckh[:, cs], wuvl[:, ws],
                                     start=False, stop=False)
                            ins = t.matmul(bank[:, 0:512], ckl[:, cs],
                                           wuvh[:, ws],
                                           start=False, stop=(lc == 3))
                            if lc == 3:
                                ins.then_inc(p.sems["kvvm"], 1)
        blk.tensor(t_kv)

        def sc_kv(sc):
            for d in kgplan:
                kg = d["kg"]
                for kfc in range(16):
                    i = kg * 16 + kfc
                    bank = PS[i % 4]
                    sc.wait_ge(p.sems["kvm"], d["k"][kfc])
                    if i >= 4:
                        pi = i - 4
                        sc.wait_ge(p.sems["kvks" + str(pi % 4)],
                                   kgplan[pi // 16]["kst"][pi % 16])
                    sc.activation(kstg[:, (i % 4) * 512:(i % 4) * 512 + 512],
                                  bank[:, 0:512], AF.Identity).then_inc(
                        p.sems["kvhe"], 1)
                for tb4 in range(4):
                    for vfc in range(4):
                        i = kg * 16 + tb4 * 4 + vfc
                        bank = PS[4 + i % 3]
                        sc.wait_ge(p.sems["kvvm"], d["v"][tb4 * 4 + vfc])
                        if i >= 4:
                            pi = i - 4
                            sc.wait_ge(p.sems["kvvs" + str(pi % 4)],
                                       kgplan[pi // 16]["vst"][pi % 16])
                        sc.activation(
                            vstg[:, (i % 4) * 512:(i % 4) * 512 + 512],
                            bank[:, 0:512], AF.Identity).then_inc(
                            p.sems["kvve"], 1)
        blk.scalar(sc_kv)

        def v_kv(v):
            for d in kgplan:
                kg = d["kg"]
                for kfc in range(16):
                    i = kg * 16 + kfc
                    bank = PS[i % 4]
                    v.wait_ge(p.sems["kvhe"], d["khe"][kfc])
                    v.scalar_tensor_tensor(
                        kstl[:, (i % 4) * 512:(i % 4) * 512 + 512],
                        bank[:, 0:512], 1.0,
                        kstg[:, (i % 4) * 512:(i % 4) * 512 + 512],
                        ALU.mult, ALU.subtract).then_inc(p.sems["kvle"], 1)
                for j in range(16):
                    i = kg * 16 + j
                    bank = PS[4 + i % 3]
                    v.wait_ge(p.sems["kvve"], d["ve"][j])
                    if i >= 4:
                        pi = i - 4
                        v.wait_ge(p.sems["kvvs" + str(pi % 4)],
                                  kgplan[pi // 16]["vst"][pi % 16])
                    v.scalar_tensor_tensor(
                        vstl[:, (i % 4) * 512:(i % 4) * 512 + 512],
                        bank[:, 0:512], 1.0,
                        vstg[:, (i % 4) * 512:(i % 4) * 512 + 512],
                        ALU.mult, ALU.subtract).then_inc(p.sems["kvvl"], 1)
        blk.vector(v_kv)

        def g_kv(g):
            for d in kgplan:
                kg = d["kg"]
                for kfc in range(16):
                    i = kg * 16 + kfc
                    g.wait_ge(p.sems["kvle"], d["kle"][kfc])
                    g.dma_start(T["KH"][kg, kfc],
                                kstg[:, (i % 4) * 512:(i % 4) * 512 + 512]
                                ).then_inc(p.sems["kvks" + str(i % 4)], 16)
                    g.dma_start(T["KL"][kg, kfc],
                                kstl[:, (i % 4) * 512:(i % 4) * 512 + 512]
                                ).then_inc(p.sems["kvks" + str(i % 4)], 16)
                for tb4 in range(4):
                    for vfc in range(4):
                        i = kg * 16 + tb4 * 4 + vfc
                        g.wait_ge(p.sems["kvvl"], d["vl"][tb4 * 4 + vfc])
                        for src_b, coff in ((vstg, 0), (vstl, 128)):
                            dap = bass.AP(
                                T["VSI"].tensor,
                                ((kg * 16 + vfc * 4) * 4 + tb4) * 128 * 256
                                + coff,
                                [[256, 128], [4 * 128 * 256, 4], [1, 128]])
                            g.dma_start(
                                dap,
                                src_b[:, (i % 4) * 512:(i % 4) * 512 + 512]
                                .rearrange("p (h c) -> p h c", h=4)
                            ).then_inc(p.sems["kvvs" + str(i % 4)], 16)
        blk.gpsimd(g_kv)

        _phase_barrier(p, "phKV", ("kvks0", "kvks1", "kvks2", "kvks3",
                                   "kvvs0", "kvvs1", "kvvs2", "kvvs3"))


def ph_attn(p):
    nc, blk, T, sb = p.nc, p.blk, p.T, p.sb
    PS, PB, PBF = p.PS, p.PB, p.PBF

    p.attn_stack = contextlib.ExitStack()
    ea = p.attn_stack.enter_context
    afh = ea(nc.sbuf_tensor("at_afh", [128, 16 * TOK], BF16))
    afl = ea(nc.sbuf_tensor("at_afl", [128, 16 * TOK], BF16))
    p.afh, p.afl = afh, afl

    p.begin_scope()
    with contextlib.ExitStack() as ph:
        e = ph.enter_context
        qn = e(nc.sbuf_tensor("at_qn", [128, 2 * 512], BF16))
        qr = e(nc.sbuf_tensor("at_qr", [128, 2 * 512], BF16))
        kbuf = e(nc.sbuf_tensor("at_k", [128, 4 * 2048], BF16))
        vb = e(nc.sbuf_tensor("at_vb", [128, 4 * 2048], BF16))
        pbuf = e(nc.sbuf_tensor("at_p", [128, 2 * 1024], F32))
        pbh = e(nc.sbuf_tensor("at_pbh", [128, 2 * 1024], BF16))
        pbl = e(nc.sbuf_tensor("at_pbl", [128, 2 * 1024], BF16))
        pth = e(nc.sbuf_tensor("at_pth", [128, 2 * 1024], BF16))
        ptl = e(nc.sbuf_tensor("at_ptl", [128, 2 * 1024], BF16))
        lmat = e(nc.sbuf_tensor("at_l", [128, 16], F32))
        rlc = e(nc.sbuf_tensor("at_rl", [128, 4], F32))
        msk = e(nc.sbuf_tensor("at_msk", [128, 512], F32))
        ofold = e(nc.sbuf_tensor("at_of", [128, 256], F32))
        aohi = e(nc.sbuf_tensor("at_aoh", [128, 256], BF16))
        aolo = e(nc.sbuf_tensor("at_aol", [128, 256], BF16))

        PS4B = PS[4][:, :].bitcast(BF16)
        PS5B = PS[5][:, :].bitcast(BF16)

        for nm in ("aQ0", "aQ1", "aK0", "aK1", "aK2", "aK3", "aSC", "aEX",
                   "aEH", "aPL", "aTR", "aTV", "aAV", "aMK", "aRL", "aFD",
                   "aHI", "aLO", "aTA", "aAE", "aQZ"):
            p.sem(nm)

        def g_z(g):
            for qs2 in (0, 1):
                for hh2 in (0, 1):
                    co = qs2 * 512 + hh2 * 256 + 128
                    ins = g.memset(qr[64:128, co:co + 128], 0.0)
                    if qs2 == 1 and hh2 == 1:
                        ins.then_inc(p.sems["aQZ"], 1)
        blk.gpsimd(g_z)
        p.add("aQZ", 1)

        slots = [(s, hp) for s in range(1, 9) for hp in range(8)]
        ck_of = {}
        n = 0
        for s, hp in slots:
            for kc in range(s):
                ck_of[(s, hp, kc)] = n
                n += 1
        NCK = n

        for nm in ("aV0", "aV1", "aV2", "aV3"):
            p.sem(nm)
        base = {nm: p.val(nm) for nm in ("aQ0", "aQ1", "aK0", "aK1", "aK2",
                                         "aK3", "aV0", "aV1", "aV2", "aV3",
                                         "aSC", "aEX", "aEH", "aPL", "aTR",
                                         "aTV", "aAV", "aMK", "aRL", "aFD",
                                         "aHI", "aLO", "aTA", "aAE")}
        qtot = {0: base["aQ0"], 1: base["aQ1"]}
        ktot = {i: base["aK" + str(i)] for i in range(4)}
        vtot = {i: base["aV" + str(i)] for i in range(4)}
        q_tgt = {}
        k_tgt = {}
        v_tgt = {}
        for i, (s, hp) in enumerate(slots):
            qs = i % 2
            qtot[qs] += 160
            q_tgt[(s, hp)] = (qs, qtot[qs])
            for kc in range(s):
                ck = ck_of[(s, hp, kc)]
                sl4 = ck % 4
                ktot[sl4] += 32
                k_tgt[(s, hp, kc)] = (sl4, ktot[sl4])
                vtot[sl4] += 16
                v_tgt[(s, hp, kc)] = (sl4, vtot[sl4])
        sc_after = {}
        ex_after = {}
        av_after = {}
        nn = 0
        for i, (s, hp) in enumerate(slots):
            nn += s
            sc_after[i] = base["aSC"] + nn
            ex_after[i] = base["aEX"] + nn
            av_after[i] = base["aAV"] + nn

        def A(name, k):
            return base[name] + k

        def sy_fn(sy):
            for i, (s, hp) in enumerate(slots):
                h1 = 2 * hp
                for kc in range(s):
                    ck = ck_of[(s, hp, kc)]
                    sl4, kv_ = k_tgt[(s, hp, kc)]
                    ks = "aK" + str(sl4)
                    if ck >= 4:
                        sy.wait_ge(p.sems["aSC"], A("aSC", ck - 3))
                    o = sl4 * 2048
                    sy.dma_start(
                        kbuf[:, o:o + 1024].rearrange("p (h c) -> p h c",
                                                      h=2),
                        T["KH"][kc, h1:h1 + 2].rearrange("h p c -> p h c")
                    ).then_inc(p.sems[ks], 16)
                    sy.dma_start(
                        kbuf[:, o + 1024:o + 2048].rearrange(
                            "p (h c) -> p h c", h=2),
                        T["KL"][kc, h1:h1 + 2].rearrange("h p c -> p h c")
                    ).then_inc(p.sems[ks], 16)
        blk.sync(sy_fn)
        for sl4 in range(4):
            p.add("aK" + str(sl4), ktot[sl4] - base["aK" + str(sl4)])

        def g_fn(g):
            for i, (s, hp) in enumerate(slots):
                qs, qv = q_tgt[(s, hp)]
                h1 = 2 * hp
                if i >= 2:
                    g.wait_ge(p.sems["aSC"], sc_after[i - 2])
                qo = qs * 512
                ro = qs * 512
                g.dma_start(
                    qn[:, qo:qo + 128], T["QNH"][h1, s - 1]).then_inc(
                    p.sems["aQ" + str(qs)], 16)
                g.dma_start(
                    qn[:, qo + 128:qo + 256], T["QNL"][h1, s - 1]).then_inc(
                    p.sems["aQ" + str(qs)], 16)
                g.dma_start(
                    qn[:, qo + 256:qo + 384], T["QNH"][h1 + 1,
                                                       s - 1]).then_inc(
                    p.sems["aQ" + str(qs)], 16)
                g.dma_start(
                    qn[:, qo + 384:qo + 512], T["QNL"][h1 + 1,
                                                       s - 1]).then_inc(
                    p.sems["aQ" + str(qs)], 16)
                for hh in range(2):
                    co = ro + hh * 256
                    rr = slice(hh * 64, hh * 64 + 64)
                    g.dma_start(qr[0:64, co:co + 128],
                                T["QRH"][hp, s - 1][rr]).then_inc(
                        p.sems["aQ" + str(qs)], 16)
                    g.dma_start(qr[64:128, co:co + 128],
                                T["QRL"][hp, s - 1][rr]).then_inc(
                        p.sems["aQ" + str(qs)], 16)
                    g.dma_start(qr[0:64, co + 128:co + 256],
                                T["QRH"][hp, s - 1][rr]).then_inc(
                        p.sems["aQ" + str(qs)], 16)
                for kc in range(s):
                    ck = ck_of[(s, hp, kc)]
                    sl4, vv_ = v_tgt[(s, hp, kc)]
                    if ck >= 4:
                        g.wait_ge(p.sems["aAV"], A("aAV", ck - 3))
                    g.dma_start(
                        vb[:, sl4 * 2048:(sl4 + 1) * 2048].rearrange(
                            "p (h kb c) -> p h kb c", h=2, kb=4),
                        T["VSI"][kc, h1:h1 + 2].rearrange(
                            "h kb p c -> p h kb c")
                    ).then_inc(p.sems["aV" + str(sl4)], 16)
        blk.gpsimd(g_fn)
        for qs in (0, 1):
            p.add("aQ" + str(qs), qtot[qs] - base["aQ" + str(qs)])
        for sl4 in range(4):
            p.add("aV" + str(sl4), vtot[sl4] - base["aV" + str(sl4)])

        def v_fn(v):
            for i, (s, hp) in enumerate(slots):
                if hp == 0:
                    v.tensor_scalar(msk[:, :], sb["kidx"][:, :],
                                    sb["qrl"][:, s - 1:s], 0.0, ALU.subtract,
                                    ALU.is_gt)
                    v.drain()
                    v.tensor_scalar(msk[:, :], msk[:, :], -1e9, None,
                                    ALU.mult)
                    v.drain()

                def subs(kc):
                    ck = ck_of[(s, hp, kc)]
                    po = (ck % 2) * 1024
                    v.wait_ge(p.sems["aEH"], A("aEH", ck + 1))
                    v.tensor_tensor(pbl[:, po:po + 512], pbuf[:, po:po + 512],
                                    pbh[:, po:po + 512], ALU.subtract)
                    v.tensor_tensor(pbl[:, po + 512:po + 1024],
                                    pbuf[:, po + 512:po + 1024],
                                    pbh[:, po + 512:po + 1024],
                                    ALU.subtract).then_inc(p.sems["aPL"], 1)

                def evac(kc):
                    ck = ck_of[(s, hp, kc)]
                    po = (ck % 2) * 1024
                    v.wait_ge(p.sems["aTR"], A("aTR", ck + 1))
                    if ck >= 2:
                        v.wait_ge(p.sems["aAV"], A("aAV", ck - 1))
                    v.tensor_copy(pth[:, po:po + 1024], PS4B)
                    v.tensor_copy(ptl[:, po:po + 1024],
                                  PS5B).then_inc(p.sems["aTV"], 1)

                for kc in range(s):
                    if kc == s - 1:
                        ck = ck_of[(s, hp, kc)]
                        v.wait_ge(p.sems["aSC"], A("aSC", ck + 1))
                        bs1 = PS[0 + (kc % 2)]
                        bs2 = PS[2 + (kc % 2)]
                        v.tensor_tensor(bs1[:, 0:512], bs1[:, 0:512],
                                        msk[:, :], ALU.add)
                        v.tensor_tensor(bs2[:, 0:512], bs2[:, 0:512],
                                        msk[:, :],
                                        ALU.add).then_inc(p.sems["aMK"], 1)
                    subs(kc)
                    if kc >= 1:
                        evac(kc - 1)
                evac(s - 1)

                v.wait_ge(p.sems["aEX"], ex_after[i])
                for hh in range(2):
                    v.tensor_reduce(rlc[:, hh:hh + 1],
                                    lmat[:, hh * 8:hh * 8 + s], AX, ALU.add)
                v.drain()
                v.reciprocal(rlc[:, 0:1], rlc[:, 0:1])
                v.reciprocal(rlc[:, 1:2], rlc[:, 1:2]).then_inc(
                    p.sems["aRL"], 1)
                v.wait_ge(p.sems["aAV"], av_after[i])
                v.tensor_copy(ofold[:, 0:128], PS[6][:, 128:256])
                v.tensor_copy(ofold[:, 128:256], PS[6][:, 384:512])
                v.drain()
                v.tensor_tensor(ofold[:, 0:128], PS[6][:, 0:128],
                                ofold[:, 0:128], ALU.add)
                v.tensor_tensor(ofold[:, 128:256], PS[6][:, 256:384],
                                ofold[:, 128:256],
                                ALU.add).then_inc(p.sems["aFD"], 1)
                v.wait_ge(p.sems["aHI"], A("aHI", i + 1))
                v.scalar_tensor_tensor(aolo[:, 0:128], ofold[:, 0:128],
                                       rlc[:, 0:1], aohi[:, 0:128],
                                       ALU.mult, ALU.subtract)
                v.scalar_tensor_tensor(aolo[:, 128:256], ofold[:, 128:256],
                                       rlc[:, 1:2], aohi[:, 128:256],
                                       ALU.mult, ALU.subtract).then_inc(
                    p.sems["aLO"], 1)
                v.wait_ge(p.sems["aTA"], A("aTA", i + 1))
                h1, h2 = 2 * hp, 2 * hp + 1
                c1 = h1 * TOK + (s - 1) * 128
                c2 = h2 * TOK + (s - 1) * 128
                v.tensor_copy(afh[:, c1:c1 + 128], PB[:, 0:128])
                v.tensor_copy(afl[:, c1:c1 + 128], PB[:, 128:256])
                v.tensor_copy(afh[:, c2:c2 + 128], PB[:, 256:384])
                v.tensor_copy(afl[:, c2:c2 + 128],
                              PB[:, 384:512]).then_inc(p.sems["aAE"], 1)
        blk.vector(v_fn)
        p.add("aPL", NCK)
        p.add("aTV", NCK)
        p.add("aMK", 64)
        p.add("aRL", 64)
        p.add("aFD", 64)
        p.add("aLO", 64)
        p.add("aAE", 64)

        def sc_fn(sc):
            for i, (s, hp) in enumerate(slots):
                for kc in range(s):
                    ck = ck_of[(s, hp, kc)]
                    po = (ck % 2) * 1024
                    sc.wait_ge(p.sems["aSC"], A("aSC", ck + 1))
                    if kc == s - 1:
                        sc.wait_ge(p.sems["aMK"], A("aMK", i + 1))
                    if ck >= 2:
                        sc.wait_ge(p.sems["aPL"], A("aPL", ck - 1))
                    sc.activation(pbuf[:, po:po + 512],
                                  PS[0 + (kc % 2)][:, 0:512], AF.Exp,
                                  scale=SCALE,
                                  accum_out=lmat[:, kc:kc + 1])
                    sc.activation(pbuf[:, po + 512:po + 1024],
                                  PS[2 + (kc % 2)][:, 0:512], AF.Exp,
                                  scale=SCALE,
                                  accum_out=lmat[:, 8 + kc:9 + kc]
                                  ).then_inc(p.sems["aEX"], 1)
                    if ck >= 2:
                        sc.wait_ge(p.sems["aTR"], A("aTR", ck - 1))
                    sc.activation(pbh[:, po:po + 512],
                                  PS[0 + (kc % 2)][:, 0:512], AF.Exp,
                                  scale=SCALE)
                    sc.activation(pbh[:, po + 512:po + 1024],
                                  PS[2 + (kc % 2)][:, 0:512], AF.Exp,
                                  scale=SCALE).then_inc(p.sems["aEH"], 1)
                sc.wait_ge(p.sems["aRL"], A("aRL", i + 1))
                sc.wait_ge(p.sems["aFD"], A("aFD", i + 1))
                if i >= 1:
                    sc.wait_ge(p.sems["aTA"], A("aTA", i))
                sc.activation(aohi[:, 0:128], ofold[:, 0:128], AF.Identity,
                              scale=rlc[:, 0:1])
                sc.activation(aohi[:, 128:256], ofold[:, 128:256],
                              AF.Identity,
                              scale=rlc[:, 1:2]).then_inc(p.sems["aHI"], 1)
        blk.scalar(sc_fn)
        p.add("aEX", NCK)
        p.add("aEH", NCK)
        p.add("aHI", 64)

        def t_fn(t):
            t.wait_ge(p.sems["phKV"], p.val("phKV"))
            t.wait_ge(p.sems["aQZ"], p.val("aQZ"))
            for i, (s, hp) in enumerate(slots):
                qs, qv = q_tgt[(s, hp)]
                t.wait_ge(p.sems["aQ" + str(qs)], qv)
                for step in range(s + 2):
                    if step < s:
                        kc = step
                        ck = ck_of[(s, hp, kc)]
                        sl4, kv_ = k_tgt[(s, hp, kc)]
                        t.wait_ge(p.sems["aK" + str(sl4)], kv_)
                        if ck >= 2:
                            t.wait_ge(p.sems["aEH"], A("aEH", ck - 1))
                        o = sl4 * 2048
                        qo = qs * 512
                        ro = qs * 512
                        krh = sb["krH"][:, kc * 512:(kc + 1) * 512]
                        krl = sb["krL"][:, kc * 512:(kc + 1) * 512]
                        for hh in range(2):
                            bank = PS[(0 if hh == 0 else 2) + (kc % 2)]
                            kh_s = kbuf[:, o + hh * 512:o + hh * 512 + 512]
                            kl_s = kbuf[:, o + 1024 + hh * 512:
                                        o + 1024 + hh * 512 + 512]
                            qnh = qn[:, qo + hh * 256:qo + hh * 256 + 128]
                            qnl = qn[:, qo + hh * 256 + 128:
                                     qo + hh * 256 + 256]
                            qrs = qr[:, ro + hh * 256:ro + hh * 256 + 128]
                            qr2 = qr[:, ro + hh * 256 + 128:
                                     ro + hh * 256 + 256]
                            t.matmul(bank[:, 0:512], qnh, kh_s,
                                     start=True, stop=False)
                            t.matmul(bank[:, 0:512], qnh, kl_s,
                                     start=False, stop=False)
                            t.matmul(bank[:, 0:512], qnl, kh_s,
                                     start=False, stop=False)
                            t.matmul(bank[:, 0:512], qrs, krh,
                                     start=False, stop=False)
                            ins = t.matmul(bank[:, 0:512], qr2, krl,
                                           start=False, stop=True)
                            if hh == 1:
                                ins.then_inc(p.sems["aSC"], 1)
                    if 1 <= step <= s:
                        kc = step - 1
                        ck = ck_of[(s, hp, kc)]
                        po = (ck % 2) * 1024
                        t.wait_ge(p.sems["aEH"], A("aEH", ck + 1))
                        t.wait_ge(p.sems["aPL"], A("aPL", ck + 1))
                        if ck >= 1:
                            t.wait_ge(p.sems["aTV"], A("aTV", ck))
                        for t8 in range(8):
                            t.transpose(PS4B[:, t8 * 128:(t8 + 1) * 128],
                                        pbh[:, po + t8 * 128:
                                            po + (t8 + 1) * 128],
                                        sb["identB"][:, :])
                        for t8 in range(8):
                            ins = t.transpose(
                                PS5B[:, t8 * 128:(t8 + 1) * 128],
                                pbl[:, po + t8 * 128:po + (t8 + 1) * 128],
                                sb["identB"][:, :])
                            if t8 == 7:
                                ins.then_inc(p.sems["aTR"], 1)
                    if 2 <= step <= s + 1:
                        kc = step - 2
                        ck = ck_of[(s, hp, kc)]
                        po = (ck % 2) * 1024
                        t.wait_ge(p.sems["aTV"], A("aTV", ck + 1))
                        sl4v, vv_ = v_tgt[(s, hp, kc)]
                        t.wait_ge(p.sems["aV" + str(sl4v)], vv_)
                        if kc == 0 and i >= 1:
                            t.wait_ge(p.sems["aFD"], A("aFD", i))
                        sl3 = ck % 4
                        for hh in range(2):
                            X = hh * 256
                            for kb in range(4):
                                lsl = slice(po + (hh * 4 + kb) * 128,
                                            po + (hh * 4 + kb) * 128 + 128)
                                v256 = slice(sl3 * 2048 + hh * 1024
                                             + kb * 256,
                                             sl3 * 2048 + hh * 1024
                                             + kb * 256 + 256)
                                v128 = slice(v256.start, v256.start + 128)
                                t.matmul(PS[6][:, X:X + 256],
                                         pth[:, lsl], vb[:, v256],
                                         start=(kc == 0 and hh == 0
                                                and kb == 0),
                                         stop=False)
                                ins = t.matmul(
                                    PS[6][:, X:X + 128],
                                    ptl[:, lsl], vb[:, v128],
                                    start=False,
                                    stop=(kc == s - 1 and hh == 1
                                          and kb == 3))
                                if hh == 1 and kb == 3:
                                    ins.then_inc(p.sems["aAV"], 1)
                t.wait_ge(p.sems["aLO"], A("aLO", i + 1))
                if i >= 1:
                    t.wait_ge(p.sems["aAE"], A("aAE", i))
                t.transpose(PB[:, 0:128], aohi[:, 0:128], sb["identB"][:, :])
                t.transpose(PB[:, 128:256], aolo[:, 0:128],
                            sb["identB"][:, :])
                t.transpose(PB[:, 256:384], aohi[:, 128:256],
                            sb["identB"][:, :])
                t.transpose(PB[:, 384:512], aolo[:, 128:256],
                            sb["identB"][:, :]).then_inc(p.sems["aTA"], 1)
        blk.tensor(t_fn)
        p.add("aSC", NCK)
        p.add("aTR", NCK)
        p.add("aAV", NCK)
        p.add("aTA", 64)

        _phase_barrier(p, "phAT", ("aAE",))


def ph_wo(p):
    nc, blk, T, sb = p.nc, p.blk, p.T, p.sb
    PS, PB, PBF = p.PS, p.PB, p.PBF
    afh, afl = p.afh, p.afl

    p.begin_scope()
    with contextlib.ExitStack() as ph:
        e = ph.enter_context
        wot = e(nc.sbuf_tensor("wo_t", [128, 6 * 1024], BF16))
        xtl = e(nc.sbuf_tensor("wo_x", [128, 2 * 8 * 512], F32))
        ystg = e(nc.sbuf_tensor("wo_y", [128, 8 * 512], F32))

        for nm in ("wl20", "wl21", "wl22", "wl23", "wl24", "wl25",
                   "wx20", "wx21", "wm2", "whc", "wev2",
                   "wst20", "wst21"):
            p.sem(nm)
        bs = {nm: p.val(nm)
              for nm in ("wl20", "wl21", "wl22", "wl23", "wl24", "wl25",
                         "wx20", "wx21", "wm2", "whc",
                         "wev2", "wst20", "wst21")}

        def sy_fn(sy):
            for dmc in range(4):
                for sl in range(8):
                    if dmc >= 2:
                        sy.wait_ge(p.sems["wev2"],
                                   bs["wev2"] + (dmc - 1) * 8)
                    sy.dma_start(
                        xtl[:, (dmc % 2) * 4096 + sl * 512:
                            (dmc % 2) * 4096 + (sl + 1) * 512],
                        T["x_own"][sl * 128:(sl + 1) * 128,
                                   dmc * 512:(dmc + 1) * 512]).then_inc(
                        p.sems["wx2" + str(dmc % 2)], 16)
                for hc in range(16):
                    idx = dmc * 16 + hc
                    if idx >= 6:
                        sy.wait_ge(p.sems["whc"], bs["whc"] + idx - 5)
                    o = (idx % 6) * 1024
                    sy.dma_start(wot[:, o:o + 512],
                                 T["WOH"][hc * 128:(hc + 1) * 128,
                                          dmc * 512:(dmc + 1) * 512]
                                 ).then_inc(p.sems["wl2" + str(idx % 6)], 16)
                    sy.dma_start(wot[:, o + 512:o + 1024],
                                 T["WOL"][hc * 128:(hc + 1) * 128,
                                          dmc * 512:(dmc + 1) * 512]
                                 ).then_inc(p.sems["wl2" + str(idx % 6)], 16)
        blk.sync(sy_fn)
        for s6 in range(6):
            p.add("wl2" + str(s6),
                  32 * len([x for x in range(64) if x % 6 == s6]))
        p.add("wx20", 16 * 16)
        p.add("wx21", 16 * 16)

        def t_fn(t):
            for dmc in range(4):
                if dmc >= 1:
                    t.wait_ge(p.sems["wev2"], bs["wev2"] + dmc * 8)
                for hc in range(16):
                    idx = dmc * 16 + hc
                    t.wait_ge(p.sems["wl2" + str(idx % 6)],
                              bs["wl2" + str(idx % 6)] + 32 * (idx // 6 + 1))
                    o = (idx % 6) * 1024
                    for sl in range(8):
                        bank = PS[sl][:, 0:512] if sl < 7 else PBF
                        ac = hc * TOK + sl * 128
                        t.matmul(bank, afh[:, ac:ac + 128], wot[:, o:o + 512],
                                 start=(hc == 0), stop=False)
                        t.matmul(bank, afh[:, ac:ac + 128],
                                 wot[:, o + 512:o + 1024],
                                 start=False, stop=False)
                        ins = t.matmul(bank, afl[:, ac:ac + 128],
                                       wot[:, o:o + 512],
                                       start=False, stop=(hc == 15))
                        if sl == 7:
                            ins.then_inc(p.sems["whc"], 1)
                t.drain()
                t.nop().then_inc(p.sems["wm2"], 1)
        blk.tensor(t_fn)
        p.add("wm2", 4)
        p.add("whc", 64)

        def v_fn(v):
            for dmc in range(4):
                v.wait_ge(p.sems["wm2"], bs["wm2"] + dmc + 1)
                v.wait_ge(p.sems["wx2" + str(dmc % 2)],
                          bs["wx2" + str(dmc % 2)] + 128 * (dmc // 2 + 1))
                for sl in range(8):
                    bank = PS[sl][:, 0:512] if sl < 7 else PBF
                    if dmc >= 1:
                        v.wait_ge(p.sems["wst2" + str((dmc - 1) % 2)],
                                  bs["wst2" + str((dmc - 1) % 2)]
                                  + 128 * ((dmc - 1) // 2 + 1))
                    v.tensor_tensor(
                        ystg[:, sl * 512:(sl + 1) * 512], bank,
                        xtl[:, (dmc % 2) * 4096 + sl * 512:
                            (dmc % 2) * 4096 + (sl + 1) * 512],
                        ALU.add)
                    v.drain()
                    v.tensor_copy(ystg[:, sl * 512:sl * 512 + 1],
                                  ystg[:, sl * 512:sl * 512 + 1]).then_inc(
                        p.sems["wev2"], 1)
        blk.vector(v_fn)
        p.add("wev2", 32)

        def g_fn(g):
            for dmc in range(4):
                for sl in range(8):
                    g.wait_ge(p.sems["wev2"], bs["wev2"] + dmc * 8 + sl + 1)
                    g.dma_start(T["YS"][sl, :, dmc * 512:(dmc + 1) * 512],
                                ystg[:, sl * 512:(sl + 1) * 512]).then_inc(
                        p.sems["wst2" + str(dmc % 2)], 16)
        blk.gpsimd(g_fn)
        p.add("wst20", 16 * 16)
        p.add("wst21", 16 * 16)

        _phase_barrier(p, "phWO", ("wst20", "wst21"))

    p.attn_stack.close()


def ph_ffn(p):
    nc, blk, T, sb = p.nc, p.blk, p.T, p.sb
    PS, PB, PBF = p.PS, p.PB, p.PBF
    nfm = sb["nfm"]

    p.begin_scope()
    with contextlib.ExitStack() as ph:
        e = ph.enter_context
        s1b = e(nc.sbuf_tensor("f_s1b", [128, HID], F32))
        s1row = e(nc.sbuf_tensor("f_s1row", [1, HID], F32))
        w1t = e(nc.sbuf_tensor("f_w1t", [128, 6 * 512], BF16))
        hst = e(nc.sbuf_tensor("f_hst", [128, 8 * 512], F32))

        for nm in ("fsr", "fbm", "fb", "f1l0", "f1l1", "f1m", "f1dc",
                   "f1e", "f1s0", "f1s1"):
            p.sem(nm)

        bfb = {"fb": p.val("fb"), "fbm": p.val("fbm")}

        def g_b(g):
            for r in range(64):
                g.dma_start(s1row[:1, r * 128:(r + 1) * 128],
                            sb["s1T"][r:r + 1, :]).then_inc(p.sems["fsr"], 16)
        blk.gpsimd(g_b)
        p.add("fsr", 16 * 64)

        def t_b(t):
            t.wait_ge(p.sems["fsr"], p.val("fsr"))
            for c in range(16):
                if c >= 2:
                    t.wait_ge(p.sems["fb"], bfb["fb"] + c - 1)
                t.matmul(PS[c % 2][:, 0:512], sb["ones1"][:1, :],
                         s1row[:1, c * 512:(c + 1) * 512],
                         start=True, stop=True).then_inc(p.sems["fbm"], 1)
        blk.tensor(t_b)
        p.add("fbm", 16)

        def v_b(v):
            for c in range(16):
                v.wait_ge(p.sems["fbm"], bfb["fbm"] + c + 1)
                v.tensor_copy(s1b[:, c * 512:(c + 1) * 512],
                              PS[c % 2][:, 0:512]).then_inc(p.sems["fb"], 1)
        blk.vector(v_b)
        p.add("fb", 16)

        b1 = {nm: p.val(nm) for nm in ("f1l0", "f1l1", "f1l2", "f1l3",
                                       "f1l4", "f1l5", "f1m", "f1dc",
                                       "f1e", "f1s0", "f1s1")}

        def sy1(sy):
            for pas in range(2):
                for hc in range(16):
                    for dc in range(16):
                        idx = (pas * 16 + hc) * 16 + dc
                        if idx >= 6:
                            sy.wait_ge(p.sems["f1dc"], b1["f1dc"] + idx - 5)
                        o = (idx % 6) * 512
                        inap = bass.AP(T["W1T"].tensor,
                                       (hc * 4 * 16 + dc) * 128 * 128,
                                       [[128, 128], [16 * 128 * 128, 4],
                                        [1, 128]])
                        sy.dma_start(
                            w1t[:, o:o + 512].rearrange(
                                "p (j c) -> p j c", j=4),
                            inap).then_inc(p.sems["f1l" + str(idx % 6)], 16)
        blk.sync(sy1)
        for s6 in range(6):
            p.add("f1l" + str(s6),
                  16 * len([x for x in range(512) if x % 6 == s6]))

        def t1(t):
            t.wait_ge(p.sems["fb"], p.val("fb"))
            for pas in range(2):
                for hc in range(16):
                    g = pas * 16 + hc
                    bq = (hc % 2) * 4
                    if g >= 2:
                        t.wait_ge(p.sems["f1e"], b1["f1e"] + 4 * (g - 1))
                    for dc in range(16):
                        idx = g * 16 + dc
                        t.wait_ge(p.sems["f1l" + str(idx % 6)],
                                  b1["f1l" + str(idx % 6)]
                                  + 16 * (idx // 6 + 1))
                        for tb4 in range(4):
                            tb = pas * 4 + tb4
                            bank = (PS[bq + tb4][:, 0:512] if bq + tb4 < 7
                                    else PBF)
                            ins = t.matmul(bank,
                                           nfm[:, dc * TOK + tb * 128:
                                               dc * TOK + tb * 128 + 128],
                                           w1t[:, (idx % 6) * 512:
                                               (idx % 6) * 512 + 512],
                                           start=(dc == 0), stop=(dc == 15))
                            if tb4 == 3:
                                ins.then_inc(p.sems["f1dc"], 1)
                    t.drain()
                    t.nop().then_inc(p.sems["f1m"], 1)
        blk.tensor(t1)
        p.add("f1m", 32)
        p.add("f1dc", 512)

        def v1(v):
            for pas in range(2):
                for hc in range(16):
                    g = pas * 16 + hc
                    bq = (hc % 2) * 4
                    v.wait_ge(p.sems["f1m"], b1["f1m"] + g + 1)
                    for tb4 in range(4):
                        tb = pas * 4 + tb4
                        bank = (PS[bq + tb4][:, 0:512] if bq + tb4 < 7
                                else PBF)
                        so = ((g % 2) * 4 + tb4) * 512
                        if g >= 2:
                            v.wait_ge(p.sems["f1s" + str(g % 2)],
                                      b1["f1s" + str(g % 2)]
                                      + 64 * (g // 2))
                        v.scalar_tensor_tensor(
                            hst[:, so:so + 512], bank,
                            sb["rsxm"][:, tb:tb + 1],
                            s1b[:, hc * 512:(hc + 1) * 512],
                            ALU.mult, ALU.mult)
                        v.drain()
                        v.tensor_copy(hst[:, so:so + 1],
                                      hst[:, so:so + 1]).then_inc(
                            p.sems["f1e"], 1)
        blk.vector(v1)
        p.add("f1e", 32 * 4)

        def g1(g):
            for pas in range(2):
                for hc in range(16):
                    gg = pas * 16 + hc
                    for tb4 in range(4):
                        tb = pas * 4 + tb4
                        so = ((gg % 2) * 4 + tb4) * 512
                        n = gg * 4 + tb4
                        g.wait_ge(p.sems["f1e"], b1["f1e"] + n + 1)
                        g.dma_start(T["HS"][tb, :, hc * 512:(hc + 1) * 512],
                                    hst[:, so:so + 512]).then_inc(
                            p.sems["f1s" + str(gg % 2)], 16)
        blk.gpsimd(g1)
        p.add("f1s0", 16 * 64)
        p.add("f1s1", 16 * 64)

        _phase_barrier(p, "phF1", ("f1s0", "f1s1"))

    p.begin_scope()
    with contextlib.ExitStack() as ph:
        e = ph.enter_context
        hbuf = e(nc.sbuf_tensor("f_h", [128, HID], F32))
        gbuf = e(nc.sbuf_tensor("f_g", [128, HID], F32))
        c8b = e(nc.sbuf_tensor("f_c8", [128, HID], I8))
        fb1 = e(nc.sbuf_tensor("f_b1", [128, HID], BF16))
        fb2 = e(nc.sbuf_tensor("f_b2", [128, HID], BF16))
        sdb = e(nc.sbuf_tensor("f_sd", [128, HID], BF16))
        ccq = e(nc.sbuf_tensor("f_cc", [128, 8], F32))
        nstg = e(nc.sbuf_tensor("f_nstg", [128, 2 * 1024], BF16))

        for nm in ("q2l", "q2g", "q2x", "q2n", "q2t", "q2e", "q2s0",
                   "q2s1"):
            p.sem(nm)
        b2 = {nm: p.val(nm) for nm in ("q2l", "q2g", "q2x", "q2n", "q2t",
                                       "q2e", "q2s0", "q2s1")}

        def sy2(sy):
            for tb in range(NB):
                if tb >= 1:
                    sy.wait_ge(p.sems["q2n"], b2["q2n"] + tb)
                sy.dma_start(hbuf[:, :], T["HS"][tb]).then_inc(
                    p.sems["q2l"], 16)
        blk.sync(sy2)
        p.add("q2l", 16 * NB)

        def sc2(sc):
            for tb in range(NB):
                sc.wait_ge(p.sems["q2l"], b2["q2l"] + 16 * (tb + 1))
                if tb >= 1:
                    sc.wait_ge(p.sems["q2n"], b2["q2n"] + tb)
                sc.activation(gbuf[:, :], hbuf[:, :],
                              AF.Gelu_apprx_tanh).then_inc(p.sems["q2g"], 1)
                sc.wait_ge(p.sems["q2x"], b2["q2x"] + 2 * tb + 1)
                sc.activation(hbuf[:, :], gbuf[:, :], AF.Identity,
                              scale=ccq[:, 3:4])
                sc.drain()
                sc.activation(c8b[:, :], gbuf[:, :], AF.Identity,
                              scale=ccq[:, 3:4]).then_inc(p.sems["q2g"], 1)
        blk.scalar(sc2)
        p.add("q2g", 2 * NB)

        def v2(v):
            for tb in range(NB):
                v.wait_ge(p.sems["q2g"], b2["q2g"] + 2 * tb + 1)
                v.tensor_reduce(ccq[:, 0:1], gbuf[:, :], AX, ALU.max,
                                apply_absolute_value=True)
                v.drain()
                v.tensor_scalar(ccq[:, 1:2], ccq[:, 0:1], 1e-5, None, ALU.max)
                v.drain()
                v.reciprocal(ccq[:, 2:3], ccq[:, 1:2])
                v.drain()
                v.tensor_scalar(ccq[:, 3:4], ccq[:, 2:3], 127.0, None,
                                ALU.mult)
                v.drain()
                v.tensor_scalar(sb["rshm"][:, tb:tb + 1], ccq[:, 1:2],
                                1.0 / 127.0, None, ALU.mult).then_inc(
                    p.sems["q2x"], 1)
                v.wait_ge(p.sems["q2g"], b2["q2g"] + 2 * tb + 2)
                v.scalar_tensor_tensor(fb1[:, :], hbuf[:, :], -0.5,
                                       c8b[:, :], ALU.add, ALU.is_ge)
                v.scalar_tensor_tensor(fb2[:, :], hbuf[:, :], 0.5,
                                       c8b[:, :], ALU.add, ALU.is_le)
                v.drain()
                v.tensor_tensor(sdb[:, :], fb1[:, :], fb2[:, :], ALU.subtract)
                v.drain()
                v.tensor_tensor(fb1[:, :], sdb[:, :], c8b[:, :],
                                ALU.add).then_inc(p.sems["q2x"], 1)
                for g8 in range(8):
                    gi = tb * 8 + g8
                    v.wait_ge(p.sems["q2t"], b2["q2t"] + gi + 1)
                    if gi >= 2:
                        v.wait_ge(p.sems["q2s" + str(gi % 2)],
                                  b2["q2s" + str(gi % 2)] + 128 * (gi // 2))
                    v.tensor_copy(nstg[:, (gi % 2) * 1024:
                                       (gi % 2) * 1024 + 1024], PB[:, :])
                    v.drain()
                    v.memset(ccq[:, 7:8], 0.0).then_inc(
                        p.sems["q2e"], 1)
                v.drain()
                v.memset(ccq[:, 7:8], 0.0).then_inc(
                    p.sems["q2n"], 1)
        blk.vector(v2)
        p.add("q2x", 2 * NB)
        p.add("q2e", 8 * NB)
        p.add("q2n", NB)

        def t2(t):
            for tb in range(NB):
                t.wait_ge(p.sems["q2x"], b2["q2x"] + 2 * tb + 2)
                for g8 in range(8):
                    gi = tb * 8 + g8
                    if gi >= 1:
                        t.wait_ge(p.sems["q2e"], b2["q2e"] + gi)
                    for k in range(8):
                        hcc = g8 * 8 + k
                        ins = t.transpose(PB[:, k * 128:(k + 1) * 128],
                                          fb1[:, hcc * 128:(hcc + 1) * 128],
                                          sb["identB"][:, :])
                        if k == 7:
                            ins.then_inc(p.sems["q2t"], 1)
        blk.tensor(t2)
        p.add("q2t", 8 * NB)

        def g2(g):
            for tb in range(NB):
                for g8 in range(8):
                    gi = tb * 8 + g8
                    g.wait_ge(p.sems["q2e"], b2["q2e"] + gi + 1)
                    src_ = nstg[:, (gi % 2) * 1024:(gi % 2) * 1024
                                + 1024].rearrange("p (k c) -> p k c", k=8)
                    dst = T["NHS"][g8 * 8:g8 * 8 + 8, tb].rearrange(
                        "k p c -> p k c")
                    for k in range(8):
                        g.dma_start(dst[:, k], src_[:, k]).then_inc(
                            p.sems["q2s" + str(gi % 2)], 16)
        blk.gpsimd(g2)
        p.add("q2s0", 128 * 32)
        p.add("q2s1", 128 * 32)

        _phase_barrier(p, "phQ2", ("q2s0", "q2s1"))

    p.begin_scope()
    with contextlib.ExitStack() as ph:
        e = ph.enter_context
        s2b = e(nc.sbuf_tensor("f_s2b", [128, D], F32))
        s2row = e(nc.sbuf_tensor("f_s2row", [1, D], F32))
        w2t = e(nc.sbuf_tensor("f_w2t", [128, 6 * 512], BF16))
        nh2 = e(nc.sbuf_tensor("f_nh2", [128, 6 * 512], BF16))
        ytl = e(nc.sbuf_tensor("f_ytl", [128, 2 * 4 * 512], F32))
        ostg = e(nc.sbuf_tensor("f_ostg", [128, 4 * 512], F32))

        for nm in ("fcm", "fc", "n2l0", "n2l1", "n2l2", "n2l3", "n2l4",
                   "n2l5", "f2l0", "f2l1", "f2l2", "f2l3", "f2l4", "f2l5",
                   "f2y", "f2m", "f2hc", "f2e", "f2s"):
            p.sem(nm)

        def g_c(g):
            for r in range(16):
                g.dma_start(s2row[:1, r * 128:(r + 1) * 128],
                            sb["s2T"][r:r + 1, :]).then_inc(p.sems["fsr2"],
                                                           16)
        p.sem("fsr2")
        blk.gpsimd(g_c)
        p.add("fsr2", 16 * 16)

        bfc = {"fc": p.val("fc"), "fcm": p.val("fcm")}

        def t_c(t):
            t.wait_ge(p.sems["fsr2"], p.val("fsr2"))
            for c in range(4):
                if c >= 2:
                    t.wait_ge(p.sems["fc"], bfc["fc"] + c - 1)
                t.matmul(PS[c % 2][:, 0:512], sb["ones1"][:1, :],
                         s2row[:1, c * 512:(c + 1) * 512],
                         start=True, stop=True).then_inc(p.sems["fcm"], 1)
        blk.tensor(t_c)
        p.add("fcm", 4)

        def v_c(v):
            for c in range(4):
                v.wait_ge(p.sems["fcm"], bfc["fcm"] + c + 1)
                v.tensor_copy(s2b[:, c * 512:(c + 1) * 512],
                              PS[c % 2][:, 0:512]).then_inc(p.sems["fc"], 1)
        blk.vector(v_c)
        p.add("fc", 4)

        b3 = {nm: p.val(nm) for nm in ("n2l0", "n2l1", "n2l2", "n2l3",
                                       "n2l4", "n2l5", "f2l0", "f2l1",
                                       "f2l2", "f2l3", "f2l4", "f2l5",
                                       "f2y", "f2m", "f2hc", "f2e", "f2s")}

        def sy3(sy):
            for pas in range(2):
                for dmc in range(4):
                    i = pas * 4 + dmc
                    if i >= 1:
                        sy.wait_ge(p.sems["f2e"], b3["f2e"] + 4 * (i - 1) + 1)
                    for tb4 in range(4):
                        tb = pas * 4 + tb4
                        sy.dma_start(
                            ytl[:, (i % 2) * 2048 + tb4 * 512:
                                (i % 2) * 2048 + (tb4 + 1) * 512],
                            T["YS"][tb, :, dmc * 512:(dmc + 1) * 512]
                        ).then_inc(p.sems["f2y"], 16)
                    for hc in range(64):
                        idx = i * 64 + hc
                        if idx >= 6:
                            sy.wait_ge(p.sems["f2hc"], b3["f2hc"] + idx - 5)
                        o = (idx % 6) * 512
                        inap = bass.AP(T["W2T"].tensor,
                                       (hc * 16 + dmc * 4) * 128 * 128,
                                       [[128, 128], [128 * 128, 4], [1, 128]])
                        sy.dma_start(
                            w2t[:, o:o + 512].rearrange(
                                "p (j c) -> p j c", j=4),
                            inap).then_inc(p.sems["f2l" + str(idx % 6)], 16)
        blk.sync(sy3)

        def g3l(g):
            def stores(i2):
                for tb4 in range(4):
                    tb = (i2 // 4) * 4 + tb4
                    dmc2 = i2 % 4
                    so = tb4 * 512
                    g.wait_ge(p.sems["f2e"],
                              b3["f2e"] + i2 * 4 + tb4 + 1)
                    g.dma_start(T["y"][tb * 128:(tb + 1) * 128,
                                       dmc2 * 512:(dmc2 + 1) * 512],
                                ostg[:, so:so + 512]).then_inc(
                        p.sems["f2s"], 16)

            for pas in range(2):
                for dmc in range(4):
                    i = pas * 4 + dmc
                    for hc in range(64):
                        idx = i * 64 + hc
                        if idx >= 6:
                            g.wait_ge(p.sems["f2hc"], b3["f2hc"] + idx - 5)
                        o = (idx % 6) * 512
                        nsrc = T["NHS"][hc, pas * 4:pas * 4 + 4].rearrange(
                            "t p c -> p t c")
                        g.dma_start(
                            nh2[:, o:o + 512].rearrange(
                                "p (t c) -> p t c", t=4),
                            nsrc).then_inc(p.sems["n2l" + str(idx % 6)], 16)
                    if i >= 1:
                        stores(i - 1)
            stores(7)
            g.wait_ge(p.sems["f2s"], p.val("f2s") + 16 * 32)
        blk.gpsimd(g3l)
        p.add("f2s", 16 * 32)
        p.add("f2y", 16 * 32)
        for s6 in range(6):
            n6 = 16 * len([x for x in range(512) if x % 6 == s6])
            p.add("f2l" + str(s6), n6)
            p.add("n2l" + str(s6), n6)

        def t3(t):
            t.wait_ge(p.sems["fc"], p.val("fc"))
            for pas in range(2):
                for dmc in range(4):
                    i = pas * 4 + dmc
                    bq = (dmc % 2) * 4
                    if i >= 2:
                        t.wait_ge(p.sems["f2e"], b3["f2e"] + 4 * (i - 1))
                    for hc in range(64):
                        idx = i * 64 + hc
                        t.wait_ge(p.sems["f2l" + str(idx % 6)],
                                  b3["f2l" + str(idx % 6)]
                                  + 16 * (idx // 6 + 1))
                        t.wait_ge(p.sems["n2l" + str(idx % 6)],
                                  b3["n2l" + str(idx % 6)]
                                  + 16 * (idx // 6 + 1))
                        o = (idx % 6) * 512
                        for tb4 in range(4):
                            bank = (PS[bq + tb4][:, 0:512] if bq + tb4 < 7
                                    else PBF)
                            ins = t.matmul(bank,
                                           nh2[:, o + tb4 * 128:
                                               o + (tb4 + 1) * 128],
                                           w2t[:, o:o + 512],
                                           start=(hc == 0), stop=(hc == 63))
                            if tb4 == 3:
                                ins.then_inc(p.sems["f2hc"], 1)
                    t.drain()
                    t.nop().then_inc(p.sems["f2m"], 1)
        blk.tensor(t3)
        p.add("f2m", 8)
        p.add("f2hc", 512)

        def v3(v):
            for pas in range(2):
                for dmc in range(4):
                    i = pas * 4 + dmc
                    bq = (dmc % 2) * 4
                    v.wait_ge(p.sems["f2m"], b3["f2m"] + i + 1)
                    v.wait_ge(p.sems["f2y"], b3["f2y"] + 16 * 4 * (i + 1))
                    for tb4 in range(4):
                        tb = pas * 4 + tb4
                        bank = (PS[bq + tb4][:, 0:512] if bq + tb4 < 7
                                else PBF)
                        so = tb4 * 512
                        if i >= 1:
                            v.wait_ge(p.sems["f2s"],
                                      b3["f2s"] + 16 * 4 * i)
                        v.scalar_tensor_tensor(
                            ostg[:, so:so + 512], bank,
                            sb["rshm"][:, tb:tb + 1],
                            s2b[:, dmc * 512:(dmc + 1) * 512],
                            ALU.mult, ALU.mult)
                        v.drain()
                        v.tensor_tensor(
                            ostg[:, so:so + 512], ostg[:, so:so + 512],
                            ytl[:, (i % 2) * 2048 + tb4 * 512:
                                (i % 2) * 2048 + (tb4 + 1) * 512],
                            ALU.add)
                        v.drain()
                        v.tensor_copy(ostg[:, so:so + 1],
                                      ostg[:, so:so + 1]).then_inc(
                            p.sems["f2e"], 1)
        blk.vector(v3)
        p.add("f2e", 32)





_NC_CACHE = None


def _get_nc():
    global _NC_CACHE
    if _NC_CACHE is None:
        _NC_CACHE = build_nc()
    return _NC_CACHE


def kernel(L0_attn_x, L0_attn_Wq, L0_attn_Wdkv, L0_attn_Wkr, L0_attn_Wukv,
           L0_attn_Wo, L0_ffn_w1, L0_ffn_w2):
    x = np.ascontiguousarray(np.asarray(L0_attn_x, dtype=np.float32))
    Wq3 = np.asarray(L0_attn_Wq, dtype=np.float32).reshape(D, H, DH + DR)
    wqn = np.ascontiguousarray(Wq3[:, :, :DH].reshape(D, H * DH))
    wqr = np.ascontiguousarray(Wq3[:, :, DH:].reshape(D, H * DR))
    Wukv3 = np.asarray(L0_attn_Wukv, dtype=np.float32).reshape(LAT, H, 2 * DH)
    wuk = np.ascontiguousarray(Wukv3[:, :, :DH].reshape(LAT, H * DH))
    wuv = np.ascontiguousarray(Wukv3[:, :, DH:].reshape(LAT, H * DH))

    nc = _get_nc()
    in_maps = []
    for core in range(8):
        b, c = core // 4, core % 4
        blocks = _blocks_for_core(c)
        rows = np.concatenate([np.arange(j * 128, (j + 1) * 128)
                               for j in blocks])
        qr = np.zeros((128, NB), dtype=np.float32)
        for si, j in enumerate(blocks):
            s = si + 1
            qr[:, si] = j * 128 + np.arange(128) - (4 * s - 4) * 128
        in_maps.append({
            "x_own": np.ascontiguousarray(x[b][rows]),
            "x_full": np.ascontiguousarray(x[b]),
            "pos_own": rows.astype(np.float32).reshape(1, TOK),
            "qrel": qr,
            "wqn": wqn, "wqr": wqr,
            "wdkv": np.ascontiguousarray(
                np.asarray(L0_attn_Wdkv, dtype=np.float32)),
            "wkr": np.ascontiguousarray(
                np.asarray(L0_attn_Wkr, dtype=np.float32)),
            "wuk": wuk, "wuv": wuv,
            "wo": np.ascontiguousarray(
                np.asarray(L0_attn_Wo, dtype=np.float32)),
            "w1": np.ascontiguousarray(
                np.asarray(L0_ffn_w1, dtype=np.float32)),
            "w2": np.ascontiguousarray(
                np.asarray(L0_ffn_w2, dtype=np.float32)),
        })

    res = run_bass_kernel_spmd(nc, in_maps, core_ids=list(range(8)))

    out = np.empty((B, S, D), dtype=np.float32)
    for core in range(8):
        b, c = core // 4, core % 4
        blocks = _blocks_for_core(c)
        yo = np.asarray(res.results[core]["y"])
        for si, j in enumerate(blocks):
            out[b, j * 128:(j + 1) * 128] = yo[si * 128:(si + 1) * 128]
    return out


# version 17
# speedup vs baseline: 1.0174x; 1.0139x over previous
import contextlib
import math

import numpy as np
import concourse.bass as bass
import concourse.mybir as mybir
from concourse.bass_utils import run_bass_kernel_spmd

F32 = mybir.dt.float32
BF16 = mybir.dt.bfloat16
I8 = mybir.dt.int8
I16 = mybir.dt.int16
AF = mybir.ActivationFunctionType
ALU = mybir.AluOpType
AX = mybir.AxisListType.X

B, S, D = 2, 4096, 2048
H, DH, DR, LAT, HID = 16, 128, 64, 512, 8192
TOK = 1024
NB = 8
SCALE = 1.0 / math.sqrt(DH + DR)
LN2P = math.log(10000.0) / 32.0
INV2PI = 1.0 / (2.0 * math.pi)
CW1 = 6.28125
CW2 = 2.0 * math.pi - 6.28125
PIH = math.pi / 2.0


def _blocks_for_core(c):
    return [4 * (s - 1) + c if s % 2 == 1 else 4 * s - 1 - c
            for s in range(1, 9)]


GLOBAL_SEMS = {"g0", "ld0", "dv0", "sc0", "lnv", "lns", "rts", "rtv",
               "phW", "phWS", "phA2", "phCK", "phKV", "phAT", "phWO",
               "phF1", "phQ2"}

SWDGE_SEMS = {"w1s0", "w1s1", "w2s0", "w2s1", "wog0", "wog1", "qpsw", "qps",
              "cksw", "ckst", "ckdup", "kvks0", "kvks1", "kvks2", "kvks3",
              "kvvs0", "kvvs1", "kvvs2", "kvvs3", "krs", "f1s0", "f1s1",
              "q2s0", "q2s1", "wst20", "wst21", "f2s", "fsr", "fsr2",
              "aV0", "aV1", "aV2", "aV3", "aQ0", "aQ1",
              "n2l0", "n2l1", "n2l2", "n2l3", "n2l4", "n2l5"}


class _SemView:
    def __init__(self, ctx):
        self.ctx = ctx

    def __getitem__(self, name):
        return self.ctx.sem(name)


class Ctx:

    def __init__(self, nc, blk, st):
        self.nc, self.blk, self.st = nc, blk, st
        self._phys = {}
        self._cnt = {}
        self._pool = []
        self._gpool = []
        self._scope = {}
        self._globals = {}
        self._nphys = 0
        self.sems = _SemView(self)

    def _new_phys(self):
        name = f"S{self._nphys}"
        self._nphys += 1
        self._phys[name] = self.st.enter_context(self.nc.semaphore(name))
        self._cnt[name] = 0
        return name

    def begin_scope(self):
        for log, phys in self._scope.items():
            (self._gpool if log in SWDGE_SEMS else self._pool).append(phys)
        self._scope = {}

    def _resolve(self, name):
        if name in GLOBAL_SEMS:
            if name not in self._globals:
                self._globals[name] = self._new_phys()
            return self._globals[name]
        if name not in self._scope:
            pool = self._gpool if name in SWDGE_SEMS else self._pool
            self._scope[name] = (pool.pop() if pool else self._new_phys())
        return self._scope[name]

    def sem(self, name):
        return self._phys[self._resolve(name)]

    def add(self, name, by):
        phys = self._resolve(name)
        self._cnt[phys] += by
        return self._cnt[phys]

    def val(self, name):
        return self._cnt[self._resolve(name)]


def build_nc():
    nc = bass.Bass("TRN2", target_bir_lowering=False)
    T = {}

    def din(name, shape, dt=F32):
        T[name] = nc.dram_tensor(name, shape, dt, kind="ExternalInput").ap()

    def dscr(name, shape, dt):
        T[name] = nc.dram_tensor(name, shape, dt, kind="Internal").ap()

    din("x_own", [TOK, D])
    din("x_full", [S, D])
    din("pos_own", [1, TOK])
    din("qrel", [128, NB])
    din("wqn", [D, 2048])
    din("wqr", [D, 1024])
    din("wdkv", [D, LAT])
    din("wkr", [D, DR])
    din("wuk", [LAT, 2048])
    din("wuv", [LAT, 2048])
    din("wo", [D, D])
    din("w1", [HID, D])
    din("w2", [D, HID])
    T["y"] = nc.dram_tensor("y", [TOK, D], F32, kind="ExternalOutput").ap()

    dscr("CKH", [4, 128, S], BF16)
    dscr("CKL", [4, 128, S], BF16)
    dscr("KH", [8, 16, 128, 512], BF16)
    dscr("KL", [8, 16, 128, 512], BF16)
    dscr("VSI", [8, 16, 4, 128, 256], BF16)
    dscr("QNH", [16, NB, 128, 128], BF16)
    dscr("QNL", [16, NB, 128, 128], BF16)
    dscr("QRH", [8, NB, 128, 128], BF16)
    dscr("QRL", [8, NB, 128, 128], BF16)
    dscr("WOH", [D, D], BF16)
    dscr("WOL", [D, D], BF16)
    dscr("W1T", [64, 16, 128, 128], BF16)
    dscr("W2T", [64, 16, 128, 128], BF16)
    dscr("HS", [NB, 128, HID], F32)
    dscr("NHS", [64, NB, 128, 128], BF16)
    dscr("YS", [NB, 128, D], F32)

    with contextlib.ExitStack() as st:
        e = st.enter_context
        blk = e(nc.Block())
        p = Ctx(nc, blk, st)
        p.T = T
        sb = {}

        def alloc(name, shape, dt=F32):
            sb[name] = e(nc.sbuf_tensor(name, shape, dt))

        alloc("identF", [128, 128])
        alloc("identB", [128, 128], BF16)
        alloc("ones1", [1, 128])
        alloc("invfr", [1, 128])
        alloc("posq", [1, TOK])
        alloc("kidx", [128, 512])
        alloc("qrl", [128, NB])
        alloc("sgncol", [128, 1])
        alloc("cst", [128, 2])
        alloc("rsxm", [128, NB])
        alloc("rshm", [128, NB])
        alloc("s1T", [64, 128])
        alloc("s2T", [16, 128])
        alloc("cosq", [128, TOK])
        alloc("sinq", [128, TOK])
        alloc("krH", [128, S], BF16)
        alloc("krL", [128, S], BF16)
        alloc("nfm", [128, 16 * TOK], BF16)
        p.sb = sb

        p.PS = [e(nc.psum_tensor(f"ps{i}", [128, 512], F32)) for i in range(7)]
        p.PB = e(nc.psum_tensor("psb", [128, 1024], BF16))
        p.PBF = p.PB[:, :].bitcast(F32)

        ph_consts(p)
        ph_wprep(p)
        ph_a2(p)
        ph_ackv_akv(p)
        ph_attn(p)
        ph_wo(p)
        ph_ffn(p)

    return nc


def _phase_barrier(p, name, wait_sems):
    blk = p.blk
    p.sem(name)

    def sy_fn(sy):
        for s in wait_sems:
            sy.wait_ge(p.sems[s], p.val(s))
        sy.nop().then_inc(p.sems[name], 1)
    blk.sync(sy_fn)
    tgt = p.add(name, 1)
    for engname in ("gpsimd", "vector", "scalar", "tensor"):
        def e_fn(eng, tgt=tgt, name=name):
            eng.wait_ge(p.sems[name], tgt)
        getattr(blk, engname)(e_fn)


def ph_consts(p):
    blk, sb, T = p.blk, p.sb, p.T
    for nm in ("g0", "ld0", "dv0", "sc0"):
        p.sem(nm)

    def g_fn(g):
        g.memset(sb["identF"][:, :], 0.0)
        g.memset(sb["identB"][:, :], 0.0)
        g.memset(sb["ones1"][:1, :], 1.0)
        g.memset(sb["sgncol"][0:32, :], -1.0)
        g.memset(sb["sgncol"][32:64, :], 1.0)
        g.memset(sb["sgncol"][64:96, :], -1.0)
        g.memset(sb["sgncol"][96:128, :], 1.0)
        g.memset(sb["cst"][:, 0:1], PIH)
        g.memset(sb["cst"][:, 1:2], 1e-5)
        g.drain()
        g.affine_select(out=sb["identF"][:, :], in_=sb["identF"][:, :],
                        compare_op=ALU.not_equal, fill=1.0,
                        base=0, pattern=[[-1, 128]], channel_multiplier=1)
        g.affine_select(out=sb["identB"][:, :], in_=sb["identB"][:, :],
                        compare_op=ALU.not_equal, fill=1.0,
                        base=0, pattern=[[-1, 128]], channel_multiplier=1)
        g.iota(sb["invfr"][:1, :], pattern=[[1, 128]], base=0,
               channel_multiplier=0, allow_small_or_imprecise_dtypes=True)
        g.iota(sb["kidx"][:, :], pattern=[[1, 512]], base=0,
               channel_multiplier=0, allow_small_or_imprecise_dtypes=True)
        g.drain()
        g.nop().then_inc(p.sems["g0"], 1)
    blk.gpsimd(g_fn)
    p.add("g0", 1)

    def sy_fn(sy):
        sy.dma_start(sb["posq"][:1, :], T["pos_own"]).then_inc(
            p.sems["ld0"], 16)
        sy.dma_start(sb["qrl"][:, :], T["qrel"]).then_inc(p.sems["ld0"], 16)
    blk.sync(sy_fn)
    p.add("ld0", 32)

    def v_fn(v):
        v.wait_ge(p.sems["g0"], p.val("g0"))
        for k in range(1, 4):
            v.tensor_copy(sb["invfr"][:1, 32 * k:32 * k + 32],
                          sb["invfr"][:1, 0:32])
        v.drain()
        v.tensor_copy(sb["invfr"][:1, 0:1],
                      sb["invfr"][:1, 0:1]).then_inc(p.sems["dv0"], 1)
    blk.vector(v_fn)
    p.add("dv0", 1)

    def sc_fn(sc):
        sc.wait_ge(p.sems["dv0"], p.val("dv0"))
        sc.activation(sb["invfr"][:1, :], sb["invfr"][:1, :], AF.Exp,
                      scale=-LN2P).then_inc(p.sems["sc0"], 1)
    blk.scalar(sc_fn)
    p.add("sc0", 1)


def ph_wprep(p):
    nc, blk, T, sb = p.nc, p.blk, p.T, p.sb
    PS, PB = p.PS, p.PB
    CH = 2048

    p.begin_scope()
    with contextlib.ExitStack() as ph:
        e = ph.enter_context
        wld = e(nc.sbuf_tensor("wp_ld", [128, 2 * CH], F32))
        tb0 = e(nc.sbuf_tensor("wp_t0", [128, 2 * CH], BF16))
        ab = e(nc.sbuf_tensor("wp_ab", [128, CH], BF16))
        bb = e(nc.sbuf_tensor("wp_bb", [128, CH], BF16))
        scol = e(nc.sbuf_tensor("wp_s", [128, 8], F32))
        s1m = e(nc.sbuf_tensor("wp_s1m", [128, 64], F32))
        s2m = e(nc.sbuf_tensor("wp_s2m", [128, 16], F32))
        tstg = e(nc.sbuf_tensor("wp_tstg", [128, 2 * 1024], BF16))
        wdum = e(nc.sbuf_tensor("wp_dum", [128, CH], BF16))

        plans = [("w1", T["w1"], T["W1T"], HID // 128, D, s1m, True),
                 ("w2", T["w2"], T["W2T"], D // 128, HID, s2m, False)]
        for tag, wsrc, wdst, ntile, ncols, smat, rowmaj in plans:
            nch = ncols // CH
            reload = nch > 1
            npl = nch * (2 if reload else 1)
            for nm in ("l0", "l1", "c", "v", "t", "ev", "s0", "s1", "r"):
                p.sem(tag + nm)
            bs = {nm: p.val(tag + nm) for nm in ("l0", "l1", "c", "v", "t",
                                                 "ev", "s0", "s1", "r")}

            prev_c = (("w1c", p.val("w1c")) if tag == "w2" else None)
            prev_t = (("w1t", p.val("w1t")) if tag == "w2" else None)
            prev_s = ((("w1s0", p.val("w1s0")), ("w1s1", p.val("w1s1")))
                      if tag == "w2" else None)

            def sy_fn(sy, tag=tag, wsrc=wsrc, ntile=ntile, nch=nch,
                      reload=reload, npl=npl, bs=bs, prev_c=prev_c):
                L = 0
                for i in range(ntile):
                    for ph2 in range(2 if reload else 1):
                        for ch in range(nch):
                            if L < 2 and prev_c is not None:
                                sy.wait_ge(p.sems[prev_c[0]], prev_c[1])
                            if L >= 2:
                                sy.wait_ge(p.sems[tag + "c"],
                                           bs["c"] + L - 1)
                            sy.dma_start(
                                wld[:, (L % 2) * CH:(L % 2) * CH
                                    + min(CH, ncols)],
                                wsrc[i * 128:(i + 1) * 128,
                                     ch * CH:ch * CH + min(CH, ncols)]
                            ).then_inc(p.sems[tag + "l" + str(L % 2)], 16)
                            L += 1
            blk.sync(sy_fn)
            for sl_ in range(2):
                nl = len([x for x in range(ntile * npl) if x % 2 == sl_])
                p.add(tag + "l" + str(sl_), 16 * nl)

            def sc_fn(sc, tag=tag, ntile=ntile, nch=nch, ncols=ncols,
                      reload=reload, npl=npl, bs=bs, prev_s=prev_s):

                def emit_evacs(i, gch):
                    for g8 in range(2):
                        gidx = (i * nch + gch) * 2 + g8
                        sc.wait_ge(p.sems[tag + "t"], bs["t"] + gidx + 1)
                        if gidx < 2 and prev_s is not None:
                            sc.wait_ge(p.sems[prev_s[0][0]], prev_s[0][1])
                            sc.wait_ge(p.sems[prev_s[1][0]], prev_s[1][1])
                        if gidx >= 2:
                            sc.wait_ge(p.sems[tag + "s" + str(gidx % 2)],
                                       bs["s" + str(gidx % 2)]
                                       + 16 * (gidx // 2))
                        sc.activation(
                            tstg[:, (gidx % 2) * 1024:
                                 (gidx % 2) * 1024 + 1024],
                            PB[:, :], AF.Identity).then_inc(
                            p.sems[tag + "ev"], 1)

                for i in range(ntile):
                    if i >= 1:
                        sc.wait_ge(p.sems[tag + "v"], bs["v"] + i * nch)
                    for ch in range(nch):
                        L = i * npl + ch
                        sc.wait_ge(p.sems[tag + "l" + str(L % 2)],
                                   bs["l" + str(L % 2)] + 16 * (L // 2 + 1))
                        w_t = wld[:, (L % 2) * CH:(L % 2) * CH
                                  + min(CH, ncols)]
                        if L > 0:
                            sc.drain()
                        ins = sc.activation(wdum[:, 0:min(CH, ncols)], w_t,
                                            AF.Abs,
                                            accum_out=scol[:, ch:ch + 1])
                        if reload:
                            ins.then_inc(p.sems[tag + "c"], 1)
                    sc.drain()
                    sc.nop().then_inc(p.sems[tag + "r"], 1)
                    if i >= 1:
                        emit_evacs(i - 1, nch - 1)
                        for gch in range(nch - 1):
                            emit_evacs(i, gch)
                    elif nch > 1:
                        for gch in range(nch - 1):
                            emit_evacs(0, gch)
                emit_evacs(ntile - 1, nch - 1)
            blk.scalar(sc_fn)
            if reload:
                p.add(tag + "c", ntile * nch)
            p.add(tag + "r", ntile)
            p.add(tag + "ev", ntile * nch * 2)

            def v_fn(v, tag=tag, ntile=ntile, nch=nch, ncols=ncols,
                     reload=reload, npl=npl, smat=smat, bs=bs,
                     prev_t=prev_t):
                L = 0
                C = 0
                for i in range(ntile):
                    L += nch
                    v.wait_ge(p.sems[tag + "r"], bs["r"] + i + 1)
                    if nch > 1:
                        v.tensor_reduce(scol[:, 6:7], scol[:, 0:nch], AX,
                                        ALU.add)
                        v.drain()
                        src_c = scol[:, 6:7]
                    else:
                        src_c = scol[:, 0:1]
                    v.tensor_scalar(scol[:, 3:4], src_c, 0.5 / ncols,
                                    0.5e-5, ALU.mult, ALU.max)
                    v.tensor_scalar(scol[:, 4:5], src_c, -0.5 / ncols,
                                    -0.5e-5, ALU.mult, ALU.min)
                    v.tensor_scalar(smat[:, i:i + 1], src_c, 1.0 / ncols,
                                    1e-5, ALU.mult, ALU.max)
                    v.drain()
                    for ch in range(nch):
                        if reload:
                            v.wait_ge(p.sems[tag + "l" + str(L % 2)],
                                      bs["l" + str(L % 2)]
                                      + 16 * (L // 2 + 1))
                            w_t = wld[:, (L % 2) * CH:(L % 2) * CH + CH]
                        else:
                            w_t = wld[:, ((L - 1) % 2) * CH:
                                      ((L - 1) % 2) * CH + ncols]
                        if C < 2 and prev_t is not None:
                            v.wait_ge(p.sems[prev_t[0]], prev_t[1])
                        if C >= 2:
                            v.wait_ge(p.sems[tag + "t"],
                                      bs["t"] + 2 * (C - 1))
                        cw = CH if reload else ncols
                        if C >= 1:
                            v.drain()
                        v.tensor_scalar(ab[:, 0:cw], w_t, scol[:, 3:4], None,
                                        ALU.is_ge)
                        v.tensor_scalar(bb[:, 0:cw], w_t, scol[:, 4:5],
                                        None, ALU.is_le).then_inc(
                            p.sems[tag + "c"], 1)
                        v.drain()
                        v.tensor_tensor(tb0[:, (C % 2) * CH:(C % 2) * CH
                                            + cw],
                                        ab[:, 0:cw], bb[:, 0:cw],
                                        ALU.subtract).then_inc(
                            p.sems[tag + "v"], 1)
                        if reload:
                            L += 1
                        C += 1
            blk.vector(v_fn)
            p.add(tag + "c", ntile * nch)
            p.add(tag + "v", ntile * nch)

            def t_fn(t, tag=tag, ntile=ntile, nch=nch, bs=bs):
                for i in range(ntile):
                    for gch in range(nch):
                        C = i * nch + gch
                        t.wait_ge(p.sems[tag + "v"], bs["v"] + C + 1)
                        for g8 in range(2):
                            gidx = C * 2 + g8
                            if gidx >= 1:
                                t.wait_ge(p.sems[tag + "ev"],
                                          bs["ev"] + gidx)
                            for k in range(8):
                                c0 = (g8 * 8 + k) * 128
                                ins = t.transpose(
                                    PB[:, k * 128:(k + 1) * 128],
                                    tb0[:, (C % 2) * CH + c0:
                                        (C % 2) * CH + c0 + 128],
                                    sb["identB"][:, :])
                                if k == 7:
                                    ins.then_inc(p.sems[tag + "t"], 1)
            blk.tensor(t_fn)
            p.add(tag + "t", ntile * nch * 2)

            def g_fn(g, tag=tag, wdst=wdst, ntile=ntile, nch=nch, bs=bs,
                     rowmaj=rowmaj):
                for i in range(ntile):
                    for gch in range(nch):
                        for g8 in range(2):
                            gidx = (i * nch + gch) * 2 + g8
                            g.wait_ge(p.sems[tag + "ev"], bs["ev"] + gidx + 1)
                            src_ = tstg[:, (gidx % 2) * 1024:
                                        (gidx % 2) * 1024 + 1024].rearrange(
                                "p (k c) -> p k c", k=8)
                            cb0 = gch * 16 + g8 * 8
                            if rowmaj:
                                dst = wdst[i, cb0:cb0 + 8].rearrange(
                                    "k p c -> p k c")
                            else:
                                dst = wdst[cb0:cb0 + 8, i].rearrange(
                                    "k p c -> p k c")
                            g.dma_start(dst, src_).then_inc(
                                p.sems[tag + "s" + str(gidx % 2)], 16)
            blk.gpsimd(g_fn)
            for sl_ in range(2):
                ng = len([x for x in range(ntile * nch * 2) if x % 2 == sl_])
                p.add(tag + "s" + str(sl_), 16 * ng)
        def t2_fn(t):
            t.wait_ge(p.sems["w1v"], p.val("w1v"))
            t.wait_ge(p.sems["w2v"], p.val("w2v"))
            t.wait_ge(p.sems["w1ev"], p.val("w1ev"))
            t.wait_ge(p.sems["w2ev"], p.val("w2ev"))
            t.transpose(PS[0][0:64, 0:128], s1m[:, :], sb["identF"][:, :])
            t.transpose(PS[1][0:16, 0:128], s2m[:, :], sb["identF"][:, :])
            t.drain()
            t.nop().then_inc(p.sem("wst"), 1)
        blk.tensor(t2_fn)
        p.add("wst", 1)

        def v2_fn(v):
            v.wait_ge(p.sems["wst"], p.val("wst"))
            v.tensor_copy(sb["s1T"][0:64, :], PS[0][0:64, 0:128])
            v.tensor_copy(sb["s2T"][0:16, :], PS[1][0:16, 0:128]).then_inc(
                p.sem("wsv"), 1)
        blk.vector(v2_fn)
        p.add("wsv", 1)

        _phase_barrier(p, "phW", ("w1s0", "w1s1", "w2s0", "w2s1", "wsv"))

    p.begin_scope()
    with contextlib.ExitStack() as ph:
        e = ph.enter_context
        wos = e(nc.sbuf_tensor("wp_wos", [128, 2 * 2 * D], F32))
        for nm in ("wol0", "wol1", "wov", "wog0", "wog1"):
            p.sem(nm)
        bwo = {nm: p.val(nm) for nm in ("wol0", "wol1", "wov", "wog0",
                                        "wog1")}

        def sy3_fn(sy):
            for i in range(16):
                if i >= 2:
                    sy.wait_ge(p.sems["wov"], bwo["wov"] + i - 1)
                    sy.wait_ge(p.sems["wog" + str(i % 2)],
                               bwo["wog" + str(i % 2)] + 32 * (i // 2))
                sy.dma_start(wos[:, (i % 2) * 2 * D:(i % 2) * 2 * D + D],
                             T["wo"][i * 128:(i + 1) * 128, :]).then_inc(
                    p.sems["wol" + str(i % 2)], 16)
        blk.sync(sy3_fn)
        p.add("wol0", 16 * 8)
        p.add("wol1", 16 * 8)

        def v3_fn(v):
            for i in range(16):
                o = (i % 2) * 2 * D
                v.wait_ge(p.sems["wol" + str(i % 2)],
                          bwo["wol" + str(i % 2)] + 16 * (i // 2 + 1))
                hi = wos[:, o + D:o + 2 * D].bitcast(BF16)[:, 0:D]
                lo = wos[:, o + D:o + 2 * D].bitcast(BF16)[:, D:2 * D]
                v.tensor_copy(hi, wos[:, o:o + D])
                v.drain()
                v.tensor_tensor(lo, wos[:, o:o + D], hi,
                                ALU.subtract).then_inc(p.sems["wov"], 1)
        blk.vector(v3_fn)
        p.add("wov", 16)

        def g3_fn(g):
            for i in range(16):
                o = (i % 2) * 2 * D
                g.wait_ge(p.sems["wov"], bwo["wov"] + i + 1)
                hi = wos[:, o + D:o + 2 * D].bitcast(BF16)[:, 0:D]
                lo = wos[:, o + D:o + 2 * D].bitcast(BF16)[:, D:2 * D]
                g.dma_start(T["WOH"][i * 128:(i + 1) * 128, :],
                            hi).then_inc(p.sems["wog" + str(i % 2)], 16)
                g.dma_start(T["WOL"][i * 128:(i + 1) * 128, :],
                            lo).then_inc(p.sems["wog" + str(i % 2)], 16)
        blk.gpsimd(g3_fn)
        p.add("wog0", 32 * 8)
        p.add("wog1", 32 * 8)

        _phase_barrier(p, "phWS", ("wog0", "wog1"))


def ph_a2(p):
    nc, blk, T, sb = p.nc, p.blk, p.T, p.sb
    PS, PB, PBF = p.PS, p.PB, p.PBF

    p.begin_scope()
    with contextlib.ExitStack() as ph:
        e = ph.enter_context
        xt = e(nc.sbuf_tensor("a2_x", [128, 2 * D], F32))
        xn = e(nc.sbuf_tensor("a2_xn", [128, 2 * D], F32))
        sq = e(nc.sbuf_tensor("a2_sq", [128, D // 2], F32))
        cc = e(nc.sbuf_tensor("a2_cc", [128, 32], F32))
        pq = e(nc.sbuf_tensor("a2_p", [128, D], F32))
        c8 = e(nc.sbuf_tensor("a2_c8", [128, D], I8))
        qb1 = e(nc.sbuf_tensor("a2_b1", [128, D], BF16))
        qb2 = e(nc.sbuf_tensor("a2_b2", [128, D], BF16))
        sd = e(nc.sbuf_tensor("a2_sd", [128, D], BF16))
        ntm = e(nc.sbuf_tensor("a2_ntm", [128, D], BF16))
        xfh = e(nc.sbuf_tensor("a2_xfh", [128, 16 * 512], BF16))
        xfl = e(nc.sbuf_tensor("a2_xfl", [128, 16 * 512], BF16))
        rs = e(nc.sbuf_tensor("a2_rs", [128, 5 * 512], F32))
        rc16 = e(nc.sbuf_tensor("a2_rc16", [128, 512], I16))
        wqt = e(nc.sbuf_tensor("a2_wqt", [128, 2 * 512], F32))
        wqh = e(nc.sbuf_tensor("a2_wqh", [128, 2 * 512], BF16))
        wql = e(nc.sbuf_tensor("a2_wql", [128, 2 * 512], BF16))
        qstg = e(nc.sbuf_tensor("a2_qstg", [128, 4 * 512], F32))
        qhib = e(nc.sbuf_tensor("a2_qhib", [128, 4 * 512], BF16))
        qlob = e(nc.sbuf_tensor("a2_qlob", [128, 4 * 512], BF16))
        rsw = e(nc.sbuf_tensor("a2_rsw", [128, 512], F32))
        cosq = e(nc.sbuf_tensor("a2_cosq", [128, TOK], F32))
        sinq = e(nc.sbuf_tensor("a2_sinq", [128, TOK], F32))

        for nm in ("rtm", "rts", "rtv", "lnv", "lns", "a2l0", "a2l1",
                   "a2sx", "qsc", "qdv", "a2t", "a2en", "a2ex", "a2fin",
                   "qpl0", "qpl1", "qpv", "qpm", "qpdc", "qpec", "qpsw",
                   "qprp", "qphl", "qps", "qpe"):
            p.sem(nm)

        base_rtm = p.val("rtm")
        base_rts = p.val("rts")
        base_qps = p.val("qps")
        base_qphl = p.val("qphl")

        def t_tab(t):
            t.wait_ge(p.sems["sc0"], p.val("sc0"))
            t.wait_ge(p.sems["ld0"], p.val("ld0"))
            for half in range(2):
                if half == 1:
                    t.wait_ge(p.sems["rts"], base_rts + 1)
                t.matmul(PS[6][:, 0:512], sb["invfr"][:1, :],
                         sb["posq"][:1, half * 512:(half + 1) * 512],
                         start=True, stop=True).then_inc(p.sems["rtm"], 1)
        blk.tensor(t_tab)
        p.add("rtm", 2)

        tabs = []
        for half in range(2):
            t1 = p.add("rts", 1)
            t2 = p.add("rtv", 1)
            t3 = p.add("rts", 1)
            t4 = p.add("rtv", 1)
            tabs.append((t1, t2, t3, t4))

        def sc_tab(sc):
            for half in range(2):
                t1, t2, t3, t4 = tabs[half]
                cos_o = cosq[:, half * 512:(half + 1) * 512]
                sin_o = sinq[:, half * 512:(half + 1) * 512]
                sc.wait_ge(p.sems["rtm"], base_rtm + half + 1)
                sc.activation(rs[:, 0:512], PS[6][:, 0:512], AF.Identity)
                sc.drain()
                sc.activation(rs[:, 512:1024], PS[6][:, 0:512], AF.Identity,
                              scale=INV2PI).then_inc(p.sems["rts"], 1)
                sc.wait_ge(p.sems["rtv"], t2)
                sc.activation(sin_o, rs[:, 0:512], AF.Sin)
                sc.drain()
                sc.activation(rs[:, 512:1024], rs[:, 0:512], AF.Abs)
                sc.drain()
                sc.activation(cos_o, rs[:, 512:1024], AF.Sin, scale=-1.0,
                              bias=sb["cst"][:, 0:1]).then_inc(p.sems["rts"], 1)
        blk.scalar(sc_tab)

        def v_tab(v):
            for half in range(2):
                t1, t2, t3, t4 = tabs[half]
                sin_o = sinq[:, half * 512:(half + 1) * 512]
                v.wait_ge(p.sems["rts"], t1)
                v.tensor_copy(rc16[:, 0:512], rs[:, 512:1024])
                v.drain()
                b1r = rs[:, 1024:1536].bitcast(BF16)[:, 0:512]
                b2r = rs[:, 1536:2048].bitcast(BF16)[:, 0:512]
                v.scalar_tensor_tensor(b1r, rs[:, 512:1024], -0.5,
                                       rc16[:, 0:512], ALU.add, ALU.is_ge)
                v.scalar_tensor_tensor(b2r, rs[:, 512:1024], 0.5,
                                       rc16[:, 0:512], ALU.add, ALU.is_le)
                v.drain()
                v.tensor_tensor(rs[:, 2048:2560], b1r, b2r, ALU.subtract)
                v.drain()
                v.tensor_tensor(rs[:, 2048:2560], rs[:, 2048:2560],
                                rc16[:, 0:512], ALU.add)
                v.drain()
                v.scalar_tensor_tensor(rs[:, 0:512], rs[:, 2048:2560], -CW1,
                                       rs[:, 0:512], ALU.mult, ALU.add)
                v.drain()
                v.scalar_tensor_tensor(rs[:, 0:512], rs[:, 2048:2560], -CW2,
                                       rs[:, 0:512], ALU.mult, ALU.add)
                v.drain()
                v.tensor_scalar(rs[:, 0:512], rs[:, 0:512], math.pi,
                                -math.pi, ALU.min, ALU.max).then_inc(
                    p.sems["rtv"], 1)
                v.wait_ge(p.sems["rts"], t3)
                v.tensor_scalar(sin_o, sin_o, sb["sgncol"][:, 0:1], None,
                                ALU.mult).then_inc(p.sems["rtv"], 1)
        blk.vector(v_tab)

        for half in range(2):
            bplan = []
            for tbl in range(4):
                tb = half * 4 + tbl
                d = {"tb": tb, "tbl": tbl}
                d["ld"] = p.add("a2l" + str(tb % 2), 16)
                d["ln1"] = p.add("lnv", 1)
                d["ln2"] = p.add("lns", 1)
                d["ln3"] = p.add("lnv", 1)
                d["ln4"] = p.add("lns", 1)
                d["sx"] = p.add("a2sx", 1)
                d["q1"] = p.add("qsc", 1)
                d["q2"] = p.add("qdv", 1)
                d["tt"] = [p.add("a2t", 1) for _ in range(6)]
                d["en"] = [p.add("a2en", 1) for _ in range(2)]
                d["ex"] = [p.add("a2ex", 1) for _ in range(4)]
                d["fin"] = p.add("a2fin", 1)
                bplan.append(d)
            allb = getattr(p, "_a2_allb", [])
            allb.extend(bplan)
            p._a2_allb = allb

            def sy_fn(sy, bplan=bplan, allb=list(allb)):
                for d in bplan:
                    tb = d["tb"]
                    if tb >= 2:
                        sy.wait_ge(p.sems["a2fin"], allb[tb - 2]["fin"])
                    sy.dma_start(
                        xt[:, (tb % 2) * D:(tb % 2) * D + D],
                        T["x_own"][tb * 128:(tb + 1) * 128, :]).then_inc(
                        p.sems["a2l" + str(tb % 2)], 16)
            blk.sync(sy_fn)

            def v_fn(v, bplan=bplan, allb=list(allb), half=half):
                for d in bplan:
                    tb, tbl = d["tb"], d["tbl"]
                    sl = (tb % 2) * D
                    co = (tb % 2) * 16
                    v.wait_ge(p.sems["a2l" + str(tb % 2)], d["ld"])
                    if tb >= 2:
                        v.wait_ge(p.sems["lns"], allb[tb - 2]["ln4"])
                    v.reduce_sum(cc[:, co:co + 1], xt[:, sl:sl + D], AX)
                    v.drain()
                    v.tensor_scalar(cc[:, co + 1:co + 2], cc[:, co:co + 1],
                                    1.0 / D, None, ALU.mult)
                    v.drain()
                    v.tensor_scalar(cc[:, co + 2:co + 3],
                                    cc[:, co + 1:co + 2], -1.0, None,
                                    ALU.mult).then_inc(p.sems["lnv"], 1)
                    v.wait_ge(p.sems["lns"], d["ln2"])
                    v.tensor_scalar(cc[:, co + 6:co + 7],
                                    cc[:, co + 1:co + 2],
                                    cc[:, co + 5:co + 6], -1.0, ALU.mult,
                                    ALU.mult).then_inc(p.sems["lnv"], 1)
                    v.wait_ge(p.sems["lns"], d["ln4"])
                    v.tensor_reduce(cc[:, co + 8:co + 9], xn[:, sl:sl + D],
                                    AX, ALU.max, apply_absolute_value=True)
                    v.drain()
                    v.tensor_scalar(cc[:, co + 9:co + 10],
                                    cc[:, co + 8:co + 9], 1e-5, None,
                                    ALU.max)
                    v.drain()
                    v.reciprocal(cc[:, co + 10:co + 11],
                                 cc[:, co + 9:co + 10])
                    v.drain()
                    v.tensor_scalar(cc[:, co + 11:co + 12],
                                    cc[:, co + 10:co + 11], 127.0, None,
                                    ALU.mult)
                    v.drain()
                    v.tensor_scalar(sb["rsxm"][:, tb:tb + 1],
                                    cc[:, co + 9:co + 10], 1.0 / 127.0, None,
                                    ALU.mult).then_inc(p.sems["a2sx"], 1)
                    v.wait_ge(p.sems["qsc"], d["q1"])
                    if tb >= 1:
                        v.wait_ge(p.sems["a2t"], allb[tb - 1]["tt"][1])
                    v.scalar_tensor_tensor(qb1[:, :], pq[:, :], -0.5,
                                           c8[:, :], ALU.add, ALU.is_ge)
                    v.scalar_tensor_tensor(qb2[:, :], pq[:, :], 0.5,
                                           c8[:, :], ALU.add, ALU.is_le)
                    v.drain()
                    v.tensor_tensor(sd[:, :], qb1[:, :], qb2[:, :],
                                    ALU.subtract)
                    v.drain()
                    v.tensor_tensor(ntm[:, :], sd[:, :], c8[:, :],
                                    ALU.add).then_inc(p.sems["qdv"], 1)
                    for g8 in range(2):
                        v.wait_ge(p.sems["a2t"], d["tt"][g8])
                        for k in range(8):
                            dc = g8 * 8 + k
                            v.tensor_copy(
                                sb["nfm"][:, dc * TOK + tb * 128:
                                          dc * TOK + tb * 128 + 128],
                                PB[:, k * 128:(k + 1) * 128])
                        v.drain()
                        v.memset(cc[:, 15:16], 0.0).then_inc(
                            p.sems["a2en"], 1)
                    for g4 in range(4):
                        v.wait_ge(p.sems["a2t"], d["tt"][2 + g4])
                        for k in range(4):
                            dc = g4 * 4 + k
                            col = dc * 512 + tbl * 128
                            v.tensor_copy(xfh[:, col:col + 128],
                                          PS[k][:, 0:128])
                            v.drain()
                            v.tensor_tensor(xfl[:, col:col + 128],
                                            PS[k][:, 0:128],
                                            xfh[:, col:col + 128],
                                            ALU.subtract)
                        v.drain()
                        v.memset(cc[:, 15:16], 0.0).then_inc(
                            p.sems["a2ex"], 1)
                    v.drain()
                    v.memset(cc[:, 15:16], 0.0).then_inc(
                        p.sems["a2fin"], 1)
            blk.vector(v_fn)

            def sc_fn(sc, bplan=bplan, allb=list(allb)):
                for d in bplan:
                    tb = d["tb"]
                    sl = (tb % 2) * D
                    co = (tb % 2) * 16
                    sc.wait_ge(p.sems["lnv"], d["ln1"])
                    sc.activation(sq[:, :], xt[:, sl:sl + D // 2], AF.Square,
                                  bias=cc[:, co + 2:co + 3],
                                  accum_out=cc[:, co + 3:co + 4])
                    sc.drain()
                    sc.activation(sq[:, :], xt[:, sl + D // 2:sl + D],
                                  AF.Square, bias=cc[:, co + 2:co + 3],
                                  accum_out=cc[:, co + 7:co + 8])
                    sc.drain()
                    sc.activation(cc[:, co + 3:co + 4],
                                  cc[:, co + 3:co + 4], AF.Identity,
                                  bias=cc[:, co + 7:co + 8])
                    sc.drain()
                    sc.activation(cc[:, co + 4:co + 5],
                                  cc[:, co + 3:co + 4], AF.Ln, bias=sb["cst"][:, 1:2],
                                  scale=1.0 / D)
                    sc.drain()
                    sc.activation(cc[:, co + 5:co + 6],
                                  cc[:, co + 4:co + 5], AF.Exp,
                                  scale=-0.5).then_inc(p.sems["lns"], 1)
                    sc.wait_ge(p.sems["lnv"], d["ln3"])
                    sc.activation(xn[:, sl:sl + D], xt[:, sl:sl + D],
                                  AF.Identity, bias=cc[:, co + 6:co + 7],
                                  scale=cc[:, co + 5:co + 6]).then_inc(
                        p.sems["lns"], 1)
                    sc.wait_ge(p.sems["a2sx"], d["sx"])
                    if tb >= 1:
                        sc.wait_ge(p.sems["qdv"], allb[tb - 1]["q2"])
                    sc.activation(pq[:, :], xn[:, sl:sl + D], AF.Identity,
                                  scale=cc[:, co + 11:co + 12])
                    sc.drain()
                    sc.activation(c8[:, :], xn[:, sl:sl + D], AF.Identity,
                                  scale=cc[:, co + 11:co + 12]).then_inc(
                        p.sems["qsc"], 1)
            blk.scalar(sc_fn)

            def t_fn(t, bplan=bplan, allb=list(allb)):
                for d in bplan:
                    tb = d["tb"]
                    sl = (tb % 2) * D
                    t.wait_ge(p.sems["qdv"], d["q2"])
                    for g8 in range(2):
                        gprev = tb * 2 + g8 - 1
                        if gprev >= 0:
                            t.wait_ge(p.sems["a2en"],
                                      allb[gprev // 2]["en"][gprev % 2])
                        for k in range(8):
                            dc = g8 * 8 + k
                            ins = t.transpose(PB[:, k * 128:(k + 1) * 128],
                                              ntm[:, dc * 128:(dc + 1) * 128],
                                              sb["identB"][:, :])
                            if k == 7:
                                ins.then_inc(p.sems["a2t"], 1)
                    t.wait_ge(p.sems["lns"], d["ln4"])
                    for g4 in range(4):
                        gprev = tb * 4 + g4 - 1
                        if gprev >= 0:
                            t.wait_ge(p.sems["a2ex"],
                                      allb[gprev // 4]["ex"][gprev % 4])
                        for k in range(4):
                            dc = g4 * 4 + k
                            ins = t.transpose(
                                PS[k][:, 0:128],
                                xn[:, sl + dc * 128:sl + (dc + 1) * 128],
                                sb["identF"][:, :])
                            if k == 3:
                                ins.then_inc(p.sems["a2t"], 1)
            blk.tensor(t_fn)

            qplan = []
            for qfg in range(6):
                for dc in range(16):
                    idx = qfg * 16 + dc
                    qplan.append({"qfg": qfg, "dc": dc,
                                  "ld": p.add("qpl" + str(idx % 2), 16),
                                  "v": p.add("qpv", 1)})


            def syq_fn(sy, qplan=qplan, half=half):
                for it in qplan:
                    qfg, dc = it["qfg"], it["dc"]
                    rope = qfg >= 4
                    wsrc = T["wqr"] if rope else T["wqn"]
                    csrc = (qfg - 4) * 512 if rope else qfg * 512
                    idx = qfg * 16 + dc
                    if idx >= 2:
                        sy.wait_ge(p.sems["qpv"], qplan[idx - 2]["v"])
                    sy.dma_start(
                        wqt[:, (idx % 2) * 512:(idx % 2) * 512 + 512],
                        wsrc[dc * 128:(dc + 1) * 128,
                             csrc:csrc + 512]).then_inc(
                        p.sems["qpl" + str(idx % 2)], 16)
            blk.sync(syq_fn)

            base_qpdc = p.val("qpdc")
            p.add("qpdc", 96)

            def mk_split(it):
                def f(v, it=it):
                    idx = it["qfg"] * 16 + it["dc"]
                    o = (idx % 2) * 512
                    v.wait_ge(p.sems["qpl" + str(idx % 2)], it["ld"])
                    if idx >= 2:
                        v.wait_ge(p.sems["qpdc"], base_qpdc + idx - 1)
                    v.tensor_copy(wqh[:, o:o + 512], wqt[:, o:o + 512])
                    v.drain()
                    v.tensor_tensor(wql[:, o:o + 512], wqt[:, o:o + 512],
                                    wqh[:, o:o + 512],
                                    ALU.subtract).then_inc(p.sems["qpv"], 1)
                return f

            base_qpe = p.val("qpe")

            def tq_fn(t, qplan=qplan, bplan=bplan, base_qpe=base_qpe):
                t.wait_ge(p.sems["a2fin"], bplan[3]["fin"])
                for qfg in range(6):
                    for dc in range(16):
                        idx = qfg * 16 + dc
                        o = (idx % 2) * 512
                        t.wait_ge(p.sems["qpv"], qplan[idx]["v"])
                        if dc == 0 and (qfg > 0 or base_qpe > 0):
                            t.wait_ge(p.sems["qpe"], base_qpe + qfg)
                        for qi in range(4):
                            bank = PS[qi][:, 0:512]
                            lo = dc * 512
                            ws = slice(o + qi * 128, o + qi * 128 + 128)
                            t.matmul(bank, wqh[:, ws], xfh[:, lo:lo + 512],
                                     start=(dc == 0), stop=False)
                            t.matmul(bank, wqh[:, ws], xfl[:, lo:lo + 512],
                                     start=False, stop=False)
                            ins = t.matmul(bank, wql[:, ws],
                                           xfh[:, lo:lo + 512],
                                           start=False, stop=(dc == 15))
                            if qi == 3:
                                ins.then_inc(p.sems["qpdc"], 1)
            blk.tensor(tq_fn)

            sc_steps, v_steps, g_steps = [], [], []
            for qfg in range(6):
                rope = qfg >= 4
                for dc in range(16):
                    v_steps.append(mk_split(qplan[qfg * 16 + dc]))
                for qi in range(4):
                    bank = PS[qi][:, 0:512]
                    stg = qstg[:, qi * 512:qi * 512 + 512]
                    hib = qhib[:, qi * 512:qi * 512 + 512]
                    lob = qlob[:, qi * 512:qi * 512 + 512]
                    qfc = (qfg - 4) * 4 + qi if rope else qfg * 4 + qi
                    gq = half * 6 + qfg
                    tec = p.add("qpec", 1)
                    sc_steps.append(lambda sc, bank=bank, stg=stg,
                                    tm=base_qpdc + 16 * (qfg + 1),
                                    prev_hl=base_qphl + 4 * gq,
                                    gq=gq: (
                        sc.wait_ge(p.sems["qpdc"], tm),
                        (sc.wait_ge(p.sems["qphl"], prev_hl)
                         if gq >= 1 else None),
                        sc.activation(stg, bank, AF.Identity).then_inc(
                            p.sems["qpec"], 1)))
                    if rope:
                        prev_rp = p.val("qprp")
                        tsw = p.add("qpsw", 64)
                        g_steps.append(lambda g, stg=stg, tec=tec,
                                       prev_rp=prev_rp: (
                            g.wait_ge(p.sems["qpec"], tec),
                            (g.wait_ge(p.sems["qprp"], prev_rp)
                             if prev_rp > 0 else None),
                            g.dma_start(rsw[0:32, :],
                                        stg[32:64, :]).then_inc(
                                p.sems["qpsw"], 16),
                            g.dma_start(rsw[32:64, :],
                                        stg[0:32, :]).then_inc(
                                p.sems["qpsw"], 16),
                            g.dma_start(rsw[64:96, :],
                                        stg[96:128, :]).then_inc(
                                p.sems["qpsw"], 16),
                            g.dma_start(rsw[96:128, :],
                                        stg[64:96, :]).then_inc(
                                p.sems["qpsw"], 16)))
                        trp = p.add("qprp", 1)
                        v_steps.append(lambda v, stg=stg, half=half,
                                       tsw=tsw: (
                            v.wait_ge(p.sems["qpsw"], tsw),
                            v.tensor_tensor(
                                stg, stg,
                                cosq[:, half * 512:half * 512 + 512],
                                ALU.mult),
                            v.tensor_tensor(
                                rsw[:, :], rsw[:, :],
                                sinq[:, half * 512:half * 512 + 512],
                                ALU.mult),
                            v.drain(),
                            v.tensor_tensor(stg, stg, rsw[:, :],
                                            ALU.add).then_inc(
                                p.sems["qprp"], 1)))
                        rdy = ("qprp", trp)
                    else:
                        rdy = ("qpec", tec)
                    thl = p.add("qphl", 1)
                    v_steps.append(lambda v, stg=stg, hib=hib, lob=lob,
                                   rdy=rdy, gq=gq, qi=qi: (
                        v.wait_ge(p.sems[rdy[0]], rdy[1]),
                        (v.wait_ge(p.sems["qps"], base_qps + 128 * 4 * gq)
                         if gq >= 1 and qi == 0 else None),
                        v.tensor_copy(hib, stg),
                        v.drain(),
                        v.tensor_tensor(lob, stg, hib,
                                        ALU.subtract).then_inc(
                            p.sems["qphl"], 1)))
                    tst = p.add("qps", 128)
                    g_steps.append(lambda g, qfc=qfc, half=half, rope=rope,
                                   thl=thl, hib=hib, lob=lob: (
                        g.wait_ge(p.sems["qphl"], thl),
                        [g.dma_start(
                            (T["QRH"] if rope else T["QNH"])[qfc,
                                                             half * 4 + sbk],
                            hib.rearrange("p (s c) -> p s c", s=4)[:, sbk]
                        ).then_inc(p.sems["qps"], 16) for sbk in range(4)],
                        [g.dma_start(
                            (T["QRL"] if rope else T["QNL"])[qfc,
                                                             half * 4 + sbk],
                            lob.rearrange("p (s c) -> p s c", s=4)[:, sbk]
                        ).then_inc(p.sems["qps"], 16) for sbk in range(4)]))
                tqe = p.add("qpe", 1)
                v_steps.append(lambda v, tqe=tqe: (
                    v.drain(),
                    v.memset(cc[:, 15:16], 0.0).then_inc(
                        p.sems["qpe"], 1)))

            blk.scalar(lambda sc, steps=sc_steps: [f(sc) for f in steps])
            blk.vector(lambda v, steps=v_steps: [f(v) for f in steps])
            blk.gpsimd(lambda g, steps=g_steps: [f(g) for f in steps])

        _phase_barrier(p, "phA2", ("qps", "a2fin", "qpe"))


def ph_ackv_akv(p):
    nc, blk, T, sb = p.nc, p.blk, p.T, p.sb
    PS, PB = p.PS, p.PB

    p.begin_scope()
    with contextlib.ExitStack() as ph:
        e = ph.enter_context
        xt = e(nc.sbuf_tensor("ck_x", [128, 2 * D], F32))
        xn = e(nc.sbuf_tensor("ck_xn", [128, 2 * D], F32))
        sq = e(nc.sbuf_tensor("ck_sq", [128, D // 2], F32))
        cc = e(nc.sbuf_tensor("ck_cc", [128, 32], F32))
        xfh = e(nc.sbuf_tensor("ck_xfh", [128, 16 * 512], BF16))
        xfl = e(nc.sbuf_tensor("ck_xfl", [128, 16 * 512], BF16))
        wdh = e(nc.sbuf_tensor("ck_wdh", [128, 16 * LAT], BF16))
        wdl = e(nc.sbuf_tensor("ck_wdl", [128, 16 * LAT], BF16))
        wrh = e(nc.sbuf_tensor("ck_wrh", [128, 16 * DR], BF16))
        wrl = e(nc.sbuf_tensor("ck_wrl", [128, 16 * DR], BF16))
        wtmp = e(nc.sbuf_tensor("ck_wtmp", [128, LAT + DR], F32))
        ckhs = e(nc.sbuf_tensor("ck_ckhs", [128, 4 * 512], BF16))
        ckls = e(nc.sbuf_tensor("ck_ckls", [128, 4 * 512], BF16))
        rs = e(nc.sbuf_tensor("ck_rs", [128, 5 * 512], F32))
        rc16 = e(nc.sbuf_tensor("ck_rc16", [128, 512], I16))
        ctab = e(nc.sbuf_tensor("ck_ctab", [128, 512], F32))
        stab = e(nc.sbuf_tensor("ck_stab", [128, 512], F32))
        krr = e(nc.sbuf_tensor("ck_krr", [128, 512], F32))
        krw = e(nc.sbuf_tensor("ck_krw", [128, 512], F32))
        posg = e(nc.sbuf_tensor("ck_posg", [1, 512], F32))

        for nm in ("ckwl", "ckwv", "ckl0", "ckl1", "ckt", "ckev", "ckm",
                   "ckec", "cklo", "cksw", "ckkr", "ckst", "ckfin", "ckdup",
                   "ckpg"):
            p.sem(nm)

        bw = {"l": p.val("ckwl"), "v": p.val("ckwv")}

        def syw(sy):
            for dc in range(16):
                if dc >= 1:
                    sy.wait_ge(p.sems["ckwv"], bw["v"] + dc)
                sy.dma_start(wtmp[:, 0:LAT],
                             T["wdkv"][dc * 128:(dc + 1) * 128, :]).then_inc(
                    p.sems["ckwl"], 16)
                sy.dma_start(wtmp[:, LAT:LAT + DR],
                             T["wkr"][dc * 128:(dc + 1) * 128, :]).then_inc(
                    p.sems["ckwl"], 16)
        blk.sync(syw)
        p.add("ckwl", 32 * 16)

        def vw(v):
            for dc in range(16):
                v.wait_ge(p.sems["ckwl"], bw["l"] + 32 * (dc + 1))
                v.tensor_copy(wdh[:, dc * LAT:(dc + 1) * LAT], wtmp[:, 0:LAT])
                v.tensor_copy(wrh[:, dc * DR:(dc + 1) * DR],
                              wtmp[:, LAT:LAT + DR])
                v.drain()
                v.tensor_tensor(wdl[:, dc * LAT:(dc + 1) * LAT],
                                wtmp[:, 0:LAT],
                                wdh[:, dc * LAT:(dc + 1) * LAT], ALU.subtract)
                v.tensor_tensor(wrl[:, dc * DR:(dc + 1) * DR],
                                wtmp[:, LAT:LAT + DR],
                                wrh[:, dc * DR:(dc + 1) * DR],
                                ALU.subtract).then_inc(p.sems["ckwv"], 1)
        blk.vector(vw)
        p.add("ckwv", 16)

        gplan = []
        for g in range(8):
            d = {"g": g, "ld": [], "ln": [], "tt": [], "ev": []}
            for bi in range(4):
                d["ld"].append(p.add("ckl" + str((g * 4 + bi) % 2), 16))
                d["ln"].append(tuple(p.add(nm, 1) for nm in
                                     ("lnv", "lns", "lnv", "lns")))
                d["tt"].append([p.add("ckt", 1) for _ in range(4)])
                d["ev"].append([p.add("ckev", 1) for _ in range(4)])
            d["pg"] = p.add("ckpg", 1)
            d["tabm"] = p.add("ckm", 1)
            d["rt1"] = p.add("rts", 1)
            d["rt2"] = p.add("rtv", 1)
            d["rt3"] = p.add("rts", 1)
            d["rt4"] = p.add("rtv", 1)
            d["mm"] = p.add("ckm", 1)
            d["ec"] = p.add("ckec", 1)
            d["lo"] = p.add("cklo", 1)
            d["st"] = p.add("ckst", 128)
            d["sw"] = p.add("cksw", 32)
            d["kr"] = p.add("ckkr", 1)
            d["fin"] = p.add("ckfin", 1)
            gplan.append(d)

        def sy_fn(sy):
            for d in gplan:
                g = d["g"]
                for bi in range(4):
                    gb = g * 4 + bi
                    if gb >= 2:
                        pg, pb = divmod(gb - 2, 4)
                        sy.wait_ge(p.sems["lns"], gplan[pg]["ln"][pb][3])
                    sy.dma_start(
                        xt[:, (gb % 2) * D:(gb % 2) * D + D],
                        T["x_full"][gb * 128:(gb + 1) * 128, :]).then_inc(
                        p.sems["ckl" + str(gb % 2)], 16)
        blk.sync(sy_fn)

        def v_fn(v):
            for d in gplan:
                g = d["g"]
                for bi in range(4):
                    gb = g * 4 + bi
                    co = (gb % 2) * 16
                    sl = (gb % 2) * D
                    v.wait_ge(p.sems["ckl" + str((g * 4 + bi) % 2)],
                              d["ld"][bi])
                    if gb >= 2:
                        pg, pb = divmod(gb - 2, 4)
                        v.wait_ge(p.sems["lns"], gplan[pg]["ln"][pb][3])
                    v.reduce_sum(cc[:, co:co + 1], xt[:, sl:sl + D], AX)
                    v.drain()
                    v.tensor_scalar(cc[:, co + 1:co + 2], cc[:, co:co + 1],
                                    1.0 / D, None, ALU.mult)
                    v.drain()
                    v.tensor_scalar(cc[:, co + 2:co + 3],
                                    cc[:, co + 1:co + 2], -1.0, None,
                                    ALU.mult).then_inc(p.sems["lnv"], 1)
                    v.wait_ge(p.sems["lns"], d["ln"][bi][1])
                    v.tensor_scalar(cc[:, co + 6:co + 7],
                                    cc[:, co + 1:co + 2],
                                    cc[:, co + 5:co + 6], -1.0, ALU.mult,
                                    ALU.mult).then_inc(p.sems["lnv"], 1)
                    for g4 in range(4):
                        v.wait_ge(p.sems["ckt"], d["tt"][bi][g4])
                        if g >= 1 and bi == 0 and g4 == 0:
                            v.wait_ge(p.sems["ckm"], gplan[g - 1]["mm"])
                        for k in range(4):
                            dc = g4 * 4 + k
                            col = dc * 512 + bi * 128
                            v.tensor_copy(xfh[:, col:col + 128],
                                          PS[k][:, 0:128])
                            v.drain()
                            v.tensor_tensor(xfl[:, col:col + 128],
                                            PS[k][:, 0:128],
                                            xfh[:, col:col + 128],
                                            ALU.subtract)
                        v.drain()
                        v.memset(cc[:, 15:16], 0.0).then_inc(
                            p.sems["ckev"], 1)
                v.wait_ge(p.sems["rts"], d["rt1"])
                v.tensor_copy(rc16[:, 0:512], rs[:, 512:1024])
                v.drain()
                b1r = rs[:, 1024:1536].bitcast(BF16)[:, 0:512]
                b2r = rs[:, 1536:2048].bitcast(BF16)[:, 0:512]
                v.scalar_tensor_tensor(b1r, rs[:, 512:1024], -0.5,
                                       rc16[:, 0:512], ALU.add, ALU.is_ge)
                v.scalar_tensor_tensor(b2r, rs[:, 512:1024], 0.5,
                                       rc16[:, 0:512], ALU.add, ALU.is_le)
                v.drain()
                v.tensor_tensor(rs[:, 2048:2560], b1r, b2r, ALU.subtract)
                v.drain()
                v.tensor_tensor(rs[:, 2048:2560], rs[:, 2048:2560],
                                rc16[:, 0:512], ALU.add)
                v.drain()
                v.scalar_tensor_tensor(rs[:, 0:512], rs[:, 2048:2560], -CW1,
                                       rs[:, 0:512], ALU.mult, ALU.add)
                v.drain()
                v.scalar_tensor_tensor(rs[:, 0:512], rs[:, 2048:2560], -CW2,
                                       rs[:, 0:512], ALU.mult, ALU.add)
                v.drain()
                v.tensor_scalar(rs[:, 0:512], rs[:, 0:512], math.pi,
                                -math.pi, ALU.min, ALU.max).then_inc(
                    p.sems["rtv"], 1)
                v.wait_ge(p.sems["rts"], d["rt3"])
                v.tensor_scalar(stab[:, :], stab[:, :], sb["sgncol"][:, 0:1],
                                None, ALU.mult).then_inc(p.sems["rtv"], 1)
                v.wait_ge(p.sems["ckec"], d["ec"])
                for lc in range(4):
                    v.scalar_tensor_tensor(
                        ckls[:, lc * 512:(lc + 1) * 512], PS[lc][:, 0:512],
                        1.0, ckhs[:, lc * 512:(lc + 1) * 512],
                        ALU.mult, ALU.subtract)
                v.drain()
                v.memset(cc[:, 15:16], 0.0).then_inc(
                    p.sems["cklo"], 1)
                v.wait_ge(p.sems["cksw"], d["sw"])
                v.tensor_tensor(krr[0:64, :], krr[0:64, :], ctab[0:64, :],
                                ALU.mult)
                v.tensor_tensor(krw[0:64, :], krw[0:64, :], stab[0:64, :],
                                ALU.mult)
                v.drain()
                v.tensor_tensor(krr[0:64, :], krr[0:64, :], krw[0:64, :],
                                ALU.add)
                v.drain()
                v.tensor_copy(sb["krH"][0:64, g * 512:(g + 1) * 512],
                              krr[0:64, :])
                v.drain()
                v.tensor_tensor(sb["krL"][0:64, g * 512:(g + 1) * 512],
                                krr[0:64, :],
                                sb["krH"][0:64, g * 512:(g + 1) * 512],
                                ALU.subtract).then_inc(p.sems["ckkr"], 1)
                v.drain()
                v.memset(cc[:, 15:16], 0.0).then_inc(
                    p.sems["ckfin"], 1)
        blk.vector(v_fn)

        def sc_fn(sc):
            for d in gplan:
                g = d["g"]
                for bi in range(4):
                    gb = g * 4 + bi
                    co = (gb % 2) * 16
                    sl = (gb % 2) * D
                    sc.wait_ge(p.sems["lnv"], d["ln"][bi][0])
                    sc.activation(sq[:, :], xt[:, sl:sl + D // 2], AF.Square,
                                  bias=cc[:, co + 2:co + 3],
                                  accum_out=cc[:, co + 3:co + 4])
                    sc.drain()
                    sc.activation(sq[:, :], xt[:, sl + D // 2:sl + D],
                                  AF.Square, bias=cc[:, co + 2:co + 3],
                                  accum_out=cc[:, co + 7:co + 8])
                    sc.drain()
                    sc.activation(cc[:, co + 3:co + 4],
                                  cc[:, co + 3:co + 4], AF.Identity,
                                  bias=cc[:, co + 7:co + 8])
                    sc.drain()
                    sc.activation(cc[:, co + 4:co + 5],
                                  cc[:, co + 3:co + 4], AF.Ln, bias=sb["cst"][:, 1:2],
                                  scale=1.0 / D)
                    sc.drain()
                    sc.activation(cc[:, co + 5:co + 6],
                                  cc[:, co + 4:co + 5], AF.Exp,
                                  scale=-0.5).then_inc(p.sems["lns"], 1)
                    sc.wait_ge(p.sems["lnv"], d["ln"][bi][2])
                    if gb >= 2:
                        pg, pb = divmod(gb - 2, 4)
                        sc.wait_ge(p.sems["ckt"], gplan[pg]["tt"][pb][3])
                    sc.activation(xn[:, sl:sl + D], xt[:, sl:sl + D],
                                  AF.Identity, bias=cc[:, co + 6:co + 7],
                                  scale=cc[:, co + 5:co + 6]).then_inc(
                        p.sems["lns"], 1)
                sc.wait_ge(p.sems["ckm"], d["tabm"])
                sc.activation(rs[:, 0:512], PS[4][:, 0:512], AF.Identity)
                sc.drain()
                sc.activation(rs[:, 512:1024], PS[4][:, 0:512], AF.Identity,
                              scale=INV2PI).then_inc(p.sems["rts"], 1)
                sc.wait_ge(p.sems["rtv"], d["rt2"])
                sc.activation(stab[:, :], rs[:, 0:512], AF.Sin)
                sc.drain()
                sc.activation(rs[:, 512:1024], rs[:, 0:512], AF.Abs)
                sc.drain()
                sc.activation(ctab[:, :], rs[:, 512:1024], AF.Sin, scale=-1.0,
                              bias=sb["cst"][:, 0:1]).then_inc(p.sems["rts"], 1)
                sc.wait_ge(p.sems["ckm"], d["mm"])
                if g >= 1:
                    sc.wait_ge(p.sems["ckst"], gplan[g - 1]["st"])
                    sc.wait_ge(p.sems["ckkr"], gplan[g - 1]["kr"])
                for lc in range(4):
                    sc.activation(ckhs[:, lc * 512:(lc + 1) * 512],
                                  PS[lc][:, 0:512], AF.Identity)
                sc.drain()
                sc.activation(krr[0:64, :], PS[5][0:64, 0:512],
                              AF.Identity).then_inc(p.sems["ckec"], 1)
        blk.scalar(sc_fn)

        def t_fn(t):
            t.wait_ge(p.sems["ckwv"], p.val("ckwv"))
            for d in gplan:
                g = d["g"]
                if g >= 1:
                    t.wait_ge(p.sems["cklo"], gplan[g - 1]["lo"])
                    t.wait_ge(p.sems["ckec"], gplan[g - 1]["ec"])
                for bi in range(4):
                    gb = g * 4 + bi
                    t.wait_ge(p.sems["lns"], d["ln"][bi][3])
                    for g4 in range(4):
                        gid = gb * 4 + g4
                        if gid >= 1:
                            pg2, rem = divmod(gid - 1, 16)
                            pbi, pg4 = divmod(rem, 4)
                            t.wait_ge(p.sems["ckev"],
                                      gplan[pg2]["ev"][pbi][pg4])
                        for k in range(4):
                            dc = g4 * 4 + k
                            ins = t.transpose(
                                PS[k][:, 0:128],
                                xn[:, (gb % 2) * D + dc * 128:
                                   (gb % 2) * D + dc * 128 + 128],
                                sb["identF"][:, :])
                            if k == 3:
                                ins.then_inc(p.sems["ckt"], 1)
                t.wait_ge(p.sems["ckpg"], d["pg"])
                if g >= 1:
                    t.wait_ge(p.sems["rts"], gplan[g - 1]["rt3"])
                t.matmul(PS[4][:, 0:512], sb["invfr"][:1, :], posg[:1, :],
                         start=True, stop=True).then_inc(p.sems["ckm"], 1)
                t.wait_ge(p.sems["ckev"], d["ev"][3][3])
                for dc in range(16):
                    for lc in range(4):
                        w1s = slice(dc * LAT + lc * 128,
                                    dc * LAT + lc * 128 + 128)
                        t.matmul(PS[lc][:, 0:512], wdh[:, w1s],
                                 xfh[:, dc * 512:(dc + 1) * 512],
                                 start=(dc == 0), stop=False)
                        t.matmul(PS[lc][:, 0:512], wdh[:, w1s],
                                 xfl[:, dc * 512:(dc + 1) * 512],
                                 start=False, stop=False)
                        t.matmul(PS[lc][:, 0:512], wdl[:, w1s],
                                 xfh[:, dc * 512:(dc + 1) * 512],
                                 start=False, stop=(dc == 15))
                    wrs = slice(dc * DR, (dc + 1) * DR)
                    t.matmul(PS[5][0:64, 0:512], wrh[:, wrs],
                             xfh[:, dc * 512:(dc + 1) * 512],
                             start=(dc == 0), stop=False)
                    t.matmul(PS[5][0:64, 0:512], wrh[:, wrs],
                             xfl[:, dc * 512:(dc + 1) * 512],
                             start=False, stop=False)
                    ins = t.matmul(PS[5][0:64, 0:512], wrl[:, wrs],
                                   xfh[:, dc * 512:(dc + 1) * 512],
                                   start=False, stop=(dc == 15))
                    if dc == 15:
                        ins.then_inc(p.sems["ckm"], 1)
        blk.tensor(t_fn)

        def g_fn(g_):
            for d in gplan:
                g = d["g"]
                if g >= 1:
                    g_.wait_ge(p.sems["ckm"], gplan[g - 1]["tabm"])
                g_.iota(posg[:1, :], pattern=[[1, 512]], base=g * 512,
                        channel_multiplier=0,
                        allow_small_or_imprecise_dtypes=True)
                g_.drain()
                g_.nop().then_inc(p.sems["ckpg"], 1)
                g_.wait_ge(p.sems["ckec"], d["ec"])
                g_.dma_start(krw[0:32, :], krr[32:64, :]).then_inc(
                    p.sems["cksw"], 16)
                g_.dma_start(krw[32:64, :], krr[0:32, :]).then_inc(
                    p.sems["cksw"], 16)
                g_.wait_ge(p.sems["cklo"], d["lo"])
                for lc in range(4):
                    g_.dma_start(T["CKH"][lc, :, g * 512:(g + 1) * 512],
                                 ckhs[:, lc * 512:(lc + 1) * 512]).then_inc(
                        p.sems["ckst"], 16)
                    g_.dma_start(T["CKL"][lc, :, g * 512:(g + 1) * 512],
                                 ckls[:, lc * 512:(lc + 1) * 512]).then_inc(
                        p.sems["ckst"], 16)
        blk.gpsimd(g_fn)

        def gdup(g):
            g.wait_ge(p.sems["ckkr"], p.val("ckkr"))
            g.dma_start(sb["krH"][64:128, :],
                        sb["krH"][0:64, :]).then_inc(p.sems["ckdup"], 16)
            g.dma_start(sb["krL"][64:128, :],
                        sb["krL"][0:64, :]).then_inc(p.sems["ckdup"], 16)
        blk.gpsimd(gdup)
        p.add("ckdup", 32)

        _phase_barrier(p, "phCK", ("ckfin", "ckst", "ckdup"))

    p.begin_scope()
    with contextlib.ExitStack() as ph:
        e = ph.enter_context
        wukh = e(nc.sbuf_tensor("kv_wukh", [128, 4 * 2048], BF16))
        wukl = e(nc.sbuf_tensor("kv_wukl", [128, 4 * 2048], BF16))
        wuvh = e(nc.sbuf_tensor("kv_wuvh", [128, 4 * 2048], BF16))
        wuvl = e(nc.sbuf_tensor("kv_wuvl", [128, 4 * 2048], BF16))
        wtm2 = e(nc.sbuf_tensor("kv_wtmp", [128, 2 * 2048], F32))
        ckh = e(nc.sbuf_tensor("kv_ckh", [128, 2 * 2048], BF16))
        ckl = e(nc.sbuf_tensor("kv_ckl", [128, 2 * 2048], BF16))
        kstg = e(nc.sbuf_tensor("kv_kstg", [128, 4 * 512], BF16))
        kstl = e(nc.sbuf_tensor("kv_kstl", [128, 4 * 512], BF16))
        vstg = e(nc.sbuf_tensor("kv_vstg", [128, 4 * 512], BF16))
        vstl = e(nc.sbuf_tensor("kv_vstl", [128, 4 * 512], BF16))

        for nm in ("kvwl", "kvwv", "kvcl0", "kvcl1", "kvm", "kvhe", "kvle",
                   "kvks0", "kvks1", "kvks2", "kvks3",
                   "kvvm", "kvve", "kvvl", "kvvs0", "kvvs1", "kvvs2", "kvvs3",
                   "krs"):
            p.sem(nm)
        bkv = {nm: p.val(nm) for nm in ("kvwl", "kvwv")}

        def syw2(sy):
            for lc in range(4):
                if lc >= 1:
                    sy.wait_ge(p.sems["kvwv"], bkv["kvwv"] + lc)
                sy.dma_start(wtm2[:, 0:2048],
                             T["wuk"][lc * 128:(lc + 1) * 128, :]).then_inc(
                    p.sems["kvwl"], 16)
                sy.dma_start(wtm2[:, 2048:4096],
                             T["wuv"][lc * 128:(lc + 1) * 128, :]).then_inc(
                    p.sems["kvwl"], 16)
        blk.sync(syw2)
        p.add("kvwl", 32 * 4)

        def vw2(v):
            for lc in range(4):
                v.wait_ge(p.sems["kvwl"], bkv["kvwl"] + 32 * (lc + 1))
                v.tensor_copy(wukh[:, lc * 2048:(lc + 1) * 2048],
                              wtm2[:, 0:2048])
                v.tensor_copy(wuvh[:, lc * 2048:(lc + 1) * 2048],
                              wtm2[:, 2048:4096])
                v.drain()
                v.tensor_tensor(wukl[:, lc * 2048:(lc + 1) * 2048],
                                wtm2[:, 0:2048],
                                wukh[:, lc * 2048:(lc + 1) * 2048],
                                ALU.subtract)
                v.tensor_tensor(wuvl[:, lc * 2048:(lc + 1) * 2048],
                                wtm2[:, 2048:4096],
                                wuvh[:, lc * 2048:(lc + 1) * 2048],
                                ALU.subtract).then_inc(p.sems["kvwv"], 1)
        blk.vector(vw2)
        p.add("kvwv", 4)

        kgplan = []
        for kg in range(8):
            d = {"kg": kg}
            d["cl"] = p.add("kvcl" + str(kg % 2), 128)
            d["k"] = [p.add("kvm", 1) for _ in range(16)]
            d["khe"] = [p.add("kvhe", 1) for _ in range(16)]
            d["kle"] = [p.add("kvle", 1) for _ in range(16)]
            d["kst"] = [p.add("kvks" + str((kg * 16 + j) % 4), 32)
                        for j in range(16)]
            d["v"] = [p.add("kvvm", 1) for _ in range(16)]
            d["ve"] = [p.add("kvve", 1) for _ in range(16)]
            d["vl"] = [p.add("kvvl", 1) for _ in range(16)]
            d["vst"] = [p.add("kvvs" + str((kg * 16 + j) % 4), 32)
                        for j in range(16)]
            kgplan.append(d)

        def syc(sy):
            for d in kgplan:
                kg = d["kg"]
                if kg >= 2:
                    sy.wait_ge(p.sems["kvvm"], kgplan[kg - 2]["v"][15])
                o = (kg % 2) * 2048
                for lc in range(4):
                    sy.dma_start(ckh[:, o + lc * 512:o + (lc + 1) * 512],
                                 T["CKH"][lc, :, kg * 512:(kg + 1) * 512]
                                 ).then_inc(p.sems["kvcl" + str(kg % 2)], 16)
                    sy.dma_start(ckl[:, o + lc * 512:o + (lc + 1) * 512],
                                 T["CKL"][lc, :, kg * 512:(kg + 1) * 512]
                                 ).then_inc(p.sems["kvcl" + str(kg % 2)], 16)
        blk.sync(syc)

        def t_kv(t):
            t.wait_ge(p.sems["kvwv"], p.val("kvwv"))
            for d in kgplan:
                kg = d["kg"]
                o = (kg % 2) * 2048
                t.wait_ge(p.sems["kvcl" + str(kg % 2)], d["cl"])
                for kfc in range(16):
                    i = kg * 16 + kfc
                    bank = PS[i % 4]
                    if i >= 4:
                        pi = i - 4
                        t.wait_ge(p.sems["kvle"],
                                  kgplan[pi // 16]["kle"][pi % 16])
                    for lc in range(4):
                        ws = slice(lc * 2048 + kfc * 128,
                                   lc * 2048 + kfc * 128 + 128)
                        cs = slice(o + lc * 512, o + (lc + 1) * 512)
                        t.matmul(bank[:, 0:512], wukh[:, ws], ckh[:, cs],
                                 start=(lc == 0), stop=False)
                        t.matmul(bank[:, 0:512], wukh[:, ws], ckl[:, cs],
                                 start=False, stop=False)
                        ins = t.matmul(bank[:, 0:512], wukl[:, ws],
                                       ckh[:, cs],
                                       start=False, stop=(lc == 3))
                        if lc == 3:
                            ins.then_inc(p.sems["kvm"], 1)
                for tb4 in range(4):
                    for vfc in range(4):
                        i = kg * 16 + tb4 * 4 + vfc
                        bank = PS[4 + i % 3]
                        if i >= 3:
                            pi = i - 3
                            t.wait_ge(p.sems["kvvl"],
                                      kgplan[pi // 16]["vl"][pi % 16])
                        for lc in range(4):
                            cs = slice(o + lc * 512 + tb4 * 128,
                                       o + lc * 512 + tb4 * 128 + 128)
                            ws = slice(lc * 2048 + vfc * 512,
                                       lc * 2048 + vfc * 512 + 512)
                            t.matmul(bank[:, 0:512], ckh[:, cs], wuvh[:, ws],
                                     start=(lc == 0), stop=False)
                            t.matmul(bank[:, 0:512], ckh[:, cs], wuvl[:, ws],
                                     start=False, stop=False)
                            ins = t.matmul(bank[:, 0:512], ckl[:, cs],
                                           wuvh[:, ws],
                                           start=False, stop=(lc == 3))
                            if lc == 3:
                                ins.then_inc(p.sems["kvvm"], 1)
        blk.tensor(t_kv)

        def sc_kv(sc):
            for d in kgplan:
                kg = d["kg"]
                for kfc in range(16):
                    i = kg * 16 + kfc
                    bank = PS[i % 4]
                    sc.wait_ge(p.sems["kvm"], d["k"][kfc])
                    if i >= 4:
                        pi = i - 4
                        sc.wait_ge(p.sems["kvks" + str(pi % 4)],
                                   kgplan[pi // 16]["kst"][pi % 16])
                    sc.activation(kstg[:, (i % 4) * 512:(i % 4) * 512 + 512],
                                  bank[:, 0:512], AF.Identity).then_inc(
                        p.sems["kvhe"], 1)
                for tb4 in range(4):
                    for vfc in range(4):
                        i = kg * 16 + tb4 * 4 + vfc
                        bank = PS[4 + i % 3]
                        sc.wait_ge(p.sems["kvvm"], d["v"][tb4 * 4 + vfc])
                        if i >= 4:
                            pi = i - 4
                            sc.wait_ge(p.sems["kvvs" + str(pi % 4)],
                                       kgplan[pi // 16]["vst"][pi % 16])
                        sc.activation(
                            vstg[:, (i % 4) * 512:(i % 4) * 512 + 512],
                            bank[:, 0:512], AF.Identity).then_inc(
                            p.sems["kvve"], 1)
        blk.scalar(sc_kv)

        def v_kv(v):
            for d in kgplan:
                kg = d["kg"]
                for kfc in range(16):
                    i = kg * 16 + kfc
                    bank = PS[i % 4]
                    v.wait_ge(p.sems["kvhe"], d["khe"][kfc])
                    v.scalar_tensor_tensor(
                        kstl[:, (i % 4) * 512:(i % 4) * 512 + 512],
                        bank[:, 0:512], 1.0,
                        kstg[:, (i % 4) * 512:(i % 4) * 512 + 512],
                        ALU.mult, ALU.subtract).then_inc(p.sems["kvle"], 1)
                for j in range(16):
                    i = kg * 16 + j
                    bank = PS[4 + i % 3]
                    v.wait_ge(p.sems["kvve"], d["ve"][j])
                    if i >= 4:
                        pi = i - 4
                        v.wait_ge(p.sems["kvvs" + str(pi % 4)],
                                  kgplan[pi // 16]["vst"][pi % 16])
                    v.scalar_tensor_tensor(
                        vstl[:, (i % 4) * 512:(i % 4) * 512 + 512],
                        bank[:, 0:512], 1.0,
                        vstg[:, (i % 4) * 512:(i % 4) * 512 + 512],
                        ALU.mult, ALU.subtract).then_inc(p.sems["kvvl"], 1)
        blk.vector(v_kv)

        def g_kv(g):
            for d in kgplan:
                kg = d["kg"]
                for kfc in range(16):
                    i = kg * 16 + kfc
                    g.wait_ge(p.sems["kvle"], d["kle"][kfc])
                    g.dma_start(T["KH"][kg, kfc],
                                kstg[:, (i % 4) * 512:(i % 4) * 512 + 512]
                                ).then_inc(p.sems["kvks" + str(i % 4)], 16)
                    g.dma_start(T["KL"][kg, kfc],
                                kstl[:, (i % 4) * 512:(i % 4) * 512 + 512]
                                ).then_inc(p.sems["kvks" + str(i % 4)], 16)
                for tb4 in range(4):
                    for vfc in range(4):
                        i = kg * 16 + tb4 * 4 + vfc
                        g.wait_ge(p.sems["kvvl"], d["vl"][tb4 * 4 + vfc])
                        for src_b, coff in ((vstg, 0), (vstl, 128)):
                            dap = bass.AP(
                                T["VSI"].tensor,
                                ((kg * 16 + vfc * 4) * 4 + tb4) * 128 * 256
                                + coff,
                                [[256, 128], [4 * 128 * 256, 4], [1, 128]])
                            g.dma_start(
                                dap,
                                src_b[:, (i % 4) * 512:(i % 4) * 512 + 512]
                                .rearrange("p (h c) -> p h c", h=4)
                            ).then_inc(p.sems["kvvs" + str(i % 4)], 16)
        blk.gpsimd(g_kv)

        _phase_barrier(p, "phKV", ("kvks0", "kvks1", "kvks2", "kvks3",
                                   "kvvs0", "kvvs1", "kvvs2", "kvvs3"))


def ph_attn(p):
    nc, blk, T, sb = p.nc, p.blk, p.T, p.sb
    PS, PB, PBF = p.PS, p.PB, p.PBF

    p.attn_stack = contextlib.ExitStack()
    ea = p.attn_stack.enter_context
    afh = ea(nc.sbuf_tensor("at_afh", [128, 16 * TOK], BF16))
    afl = ea(nc.sbuf_tensor("at_afl", [128, 16 * TOK], BF16))
    p.afh, p.afl = afh, afl

    p.begin_scope()
    with contextlib.ExitStack() as ph:
        e = ph.enter_context
        qn = e(nc.sbuf_tensor("at_qn", [128, 2 * 512], BF16))
        qr = e(nc.sbuf_tensor("at_qr", [128, 2 * 512], BF16))
        kbuf = e(nc.sbuf_tensor("at_k", [128, 4 * 2048], BF16))
        vb = e(nc.sbuf_tensor("at_vb", [128, 4 * 2048], BF16))
        pbuf = e(nc.sbuf_tensor("at_p", [128, 2 * 1024], F32))
        pbh = e(nc.sbuf_tensor("at_pbh", [128, 2 * 1024], BF16))
        pbl = e(nc.sbuf_tensor("at_pbl", [128, 2 * 1024], BF16))
        pth = e(nc.sbuf_tensor("at_pth", [128, 2 * 1024], BF16))
        ptl = e(nc.sbuf_tensor("at_ptl", [128, 2 * 1024], BF16))
        lmat = e(nc.sbuf_tensor("at_l", [128, 16], F32))
        rlc = e(nc.sbuf_tensor("at_rl", [128, 4], F32))
        msk = e(nc.sbuf_tensor("at_msk", [128, 512], F32))
        ofold = e(nc.sbuf_tensor("at_of", [128, 256], F32))
        aohi = e(nc.sbuf_tensor("at_aoh", [128, 256], BF16))
        aolo = e(nc.sbuf_tensor("at_aol", [128, 256], BF16))

        PS4B = PS[4][:, :].bitcast(BF16)
        PS5B = PS[5][:, :].bitcast(BF16)

        for nm in ("aQ0", "aQ1", "aK0", "aK1", "aK2", "aK3", "aSC", "aEX",
                   "aEH", "aPL", "aTR", "aTV", "aAV", "aMK", "aRL", "aFD",
                   "aHI", "aLO", "aTA", "aAE", "aQZ"):
            p.sem(nm)

        def g_z(g):
            for qs2 in (0, 1):
                for hh2 in (0, 1):
                    co = qs2 * 512 + hh2 * 256 + 128
                    ins = g.memset(qr[64:128, co:co + 128], 0.0)
                    if qs2 == 1 and hh2 == 1:
                        ins.then_inc(p.sems["aQZ"], 1)
        blk.gpsimd(g_z)
        p.add("aQZ", 1)

        slots = [(s, hp) for s in range(1, 9) for hp in range(8)]
        ck_of = {}
        n = 0
        for s, hp in slots:
            for kc in range(s):
                ck_of[(s, hp, kc)] = n
                n += 1
        NCK = n

        for nm in ("aV0", "aV1", "aV2", "aV3"):
            p.sem(nm)
        base = {nm: p.val(nm) for nm in ("aQ0", "aQ1", "aK0", "aK1", "aK2",
                                         "aK3", "aV0", "aV1", "aV2", "aV3",
                                         "aSC", "aEX", "aEH", "aPL", "aTR",
                                         "aTV", "aAV", "aMK", "aRL", "aFD",
                                         "aHI", "aLO", "aTA", "aAE")}
        qtot = {0: base["aQ0"], 1: base["aQ1"]}
        ktot = {i: base["aK" + str(i)] for i in range(4)}
        vtot = {i: base["aV" + str(i)] for i in range(4)}
        q_tgt = {}
        k_tgt = {}
        v_tgt = {}
        for i, (s, hp) in enumerate(slots):
            qs = i % 2
            qtot[qs] += 160
            q_tgt[(s, hp)] = (qs, qtot[qs])
            for kc in range(s):
                ck = ck_of[(s, hp, kc)]
                sl4 = ck % 4
                ktot[sl4] += 32
                k_tgt[(s, hp, kc)] = (sl4, ktot[sl4])
                vtot[sl4] += 16
                v_tgt[(s, hp, kc)] = (sl4, vtot[sl4])
        sc_after = {}
        ex_after = {}
        av_after = {}
        nn = 0
        for i, (s, hp) in enumerate(slots):
            nn += s
            sc_after[i] = base["aSC"] + nn
            ex_after[i] = base["aEX"] + nn
            av_after[i] = base["aAV"] + nn

        def A(name, k):
            return base[name] + k

        def sy_fn(sy):
            for i, (s, hp) in enumerate(slots):
                h1 = 2 * hp
                for kc in range(s):
                    ck = ck_of[(s, hp, kc)]
                    sl4, kv_ = k_tgt[(s, hp, kc)]
                    ks = "aK" + str(sl4)
                    if ck >= 4:
                        sy.wait_ge(p.sems["aSC"], A("aSC", ck - 3))
                    o = sl4 * 2048
                    sy.dma_start(
                        kbuf[:, o:o + 1024].rearrange("p (h c) -> p h c",
                                                      h=2),
                        T["KH"][kc, h1:h1 + 2].rearrange("h p c -> p h c")
                    ).then_inc(p.sems[ks], 16)
                    sy.dma_start(
                        kbuf[:, o + 1024:o + 2048].rearrange(
                            "p (h c) -> p h c", h=2),
                        T["KL"][kc, h1:h1 + 2].rearrange("h p c -> p h c")
                    ).then_inc(p.sems[ks], 16)
        blk.sync(sy_fn)
        for sl4 in range(4):
            p.add("aK" + str(sl4), ktot[sl4] - base["aK" + str(sl4)])

        def g_fn(g):
            for i, (s, hp) in enumerate(slots):
                qs, qv = q_tgt[(s, hp)]
                h1 = 2 * hp
                if i >= 2:
                    g.wait_ge(p.sems["aSC"], sc_after[i - 2])
                qo = qs * 512
                ro = qs * 512
                g.dma_start(
                    qn[:, qo:qo + 128], T["QNH"][h1, s - 1]).then_inc(
                    p.sems["aQ" + str(qs)], 16)
                g.dma_start(
                    qn[:, qo + 128:qo + 256], T["QNL"][h1, s - 1]).then_inc(
                    p.sems["aQ" + str(qs)], 16)
                g.dma_start(
                    qn[:, qo + 256:qo + 384], T["QNH"][h1 + 1,
                                                       s - 1]).then_inc(
                    p.sems["aQ" + str(qs)], 16)
                g.dma_start(
                    qn[:, qo + 384:qo + 512], T["QNL"][h1 + 1,
                                                       s - 1]).then_inc(
                    p.sems["aQ" + str(qs)], 16)
                for hh in range(2):
                    co = ro + hh * 256
                    rr = slice(hh * 64, hh * 64 + 64)
                    g.dma_start(qr[0:64, co:co + 128],
                                T["QRH"][hp, s - 1][rr]).then_inc(
                        p.sems["aQ" + str(qs)], 16)
                    g.dma_start(qr[64:128, co:co + 128],
                                T["QRL"][hp, s - 1][rr]).then_inc(
                        p.sems["aQ" + str(qs)], 16)
                    g.dma_start(qr[0:64, co + 128:co + 256],
                                T["QRH"][hp, s - 1][rr]).then_inc(
                        p.sems["aQ" + str(qs)], 16)
                for kc in range(s):
                    ck = ck_of[(s, hp, kc)]
                    sl4, vv_ = v_tgt[(s, hp, kc)]
                    if ck >= 4:
                        g.wait_ge(p.sems["aAV"], A("aAV", ck - 3))
                    g.dma_start(
                        vb[:, sl4 * 2048:(sl4 + 1) * 2048].rearrange(
                            "p (h kb c) -> p h kb c", h=2, kb=4),
                        T["VSI"][kc, h1:h1 + 2].rearrange(
                            "h kb p c -> p h kb c")
                    ).then_inc(p.sems["aV" + str(sl4)], 16)
        blk.gpsimd(g_fn)
        for qs in (0, 1):
            p.add("aQ" + str(qs), qtot[qs] - base["aQ" + str(qs)])
        for sl4 in range(4):
            p.add("aV" + str(sl4), vtot[sl4] - base["aV" + str(sl4)])

        def v_fn(v):
            for i, (s, hp) in enumerate(slots):
                if hp == 0:
                    v.tensor_scalar(msk[:, :], sb["kidx"][:, :],
                                    sb["qrl"][:, s - 1:s], 0.0, ALU.subtract,
                                    ALU.is_gt)
                    v.drain()
                    v.tensor_scalar(msk[:, :], msk[:, :], -1e9, None,
                                    ALU.mult)
                    v.drain()

                def subs(kc):
                    ck = ck_of[(s, hp, kc)]
                    po = (ck % 2) * 1024
                    v.wait_ge(p.sems["aEH"], A("aEH", ck + 1))
                    v.tensor_tensor(pbl[:, po:po + 512], pbuf[:, po:po + 512],
                                    pbh[:, po:po + 512], ALU.subtract)
                    v.tensor_tensor(pbl[:, po + 512:po + 1024],
                                    pbuf[:, po + 512:po + 1024],
                                    pbh[:, po + 512:po + 1024],
                                    ALU.subtract).then_inc(p.sems["aPL"], 1)

                def evac(kc):
                    ck = ck_of[(s, hp, kc)]
                    po = (ck % 2) * 1024
                    v.wait_ge(p.sems["aTR"], A("aTR", ck + 1))
                    if ck >= 2:
                        v.wait_ge(p.sems["aAV"], A("aAV", ck - 1))
                    v.tensor_copy(pth[:, po:po + 1024], PS4B)
                    v.tensor_copy(ptl[:, po:po + 1024],
                                  PS5B).then_inc(p.sems["aTV"], 1)

                for kc in range(s):
                    if kc == s - 1:
                        ck = ck_of[(s, hp, kc)]
                        v.wait_ge(p.sems["aSC"], A("aSC", ck + 1))
                        bs1 = PS[0 + (kc % 2)]
                        bs2 = PS[2 + (kc % 2)]
                        v.tensor_tensor(bs1[:, 0:512], bs1[:, 0:512],
                                        msk[:, :], ALU.add)
                        v.tensor_tensor(bs2[:, 0:512], bs2[:, 0:512],
                                        msk[:, :],
                                        ALU.add).then_inc(p.sems["aMK"], 1)
                    subs(kc)
                    if kc >= 1:
                        evac(kc - 1)
                evac(s - 1)

                v.wait_ge(p.sems["aEX"], ex_after[i])
                for hh in range(2):
                    v.tensor_reduce(rlc[:, hh:hh + 1],
                                    lmat[:, hh * 8:hh * 8 + s], AX, ALU.add)
                v.drain()
                v.reciprocal(rlc[:, 0:1], rlc[:, 0:1])
                v.reciprocal(rlc[:, 1:2], rlc[:, 1:2]).then_inc(
                    p.sems["aRL"], 1)
                v.wait_ge(p.sems["aAV"], av_after[i])
                v.tensor_copy(ofold[:, 0:128], PS[6][:, 128:256])
                v.tensor_copy(ofold[:, 128:256], PS[6][:, 384:512])
                v.drain()
                v.tensor_tensor(ofold[:, 0:128], PS[6][:, 0:128],
                                ofold[:, 0:128], ALU.add)
                v.tensor_tensor(ofold[:, 128:256], PS[6][:, 256:384],
                                ofold[:, 128:256],
                                ALU.add).then_inc(p.sems["aFD"], 1)
                v.wait_ge(p.sems["aHI"], A("aHI", i + 1))
                v.scalar_tensor_tensor(aolo[:, 0:128], ofold[:, 0:128],
                                       rlc[:, 0:1], aohi[:, 0:128],
                                       ALU.mult, ALU.subtract)
                v.scalar_tensor_tensor(aolo[:, 128:256], ofold[:, 128:256],
                                       rlc[:, 1:2], aohi[:, 128:256],
                                       ALU.mult, ALU.subtract).then_inc(
                    p.sems["aLO"], 1)
                v.wait_ge(p.sems["aTA"], A("aTA", i + 1))
                h1, h2 = 2 * hp, 2 * hp + 1
                c1 = h1 * TOK + (s - 1) * 128
                c2 = h2 * TOK + (s - 1) * 128
                v.tensor_copy(afh[:, c1:c1 + 128], PB[:, 0:128])
                v.tensor_copy(afl[:, c1:c1 + 128], PB[:, 128:256])
                v.tensor_copy(afh[:, c2:c2 + 128], PB[:, 256:384])
                v.tensor_copy(afl[:, c2:c2 + 128],
                              PB[:, 384:512]).then_inc(p.sems["aAE"], 1)
        blk.vector(v_fn)
        p.add("aPL", NCK)
        p.add("aTV", NCK)
        p.add("aMK", 64)
        p.add("aRL", 64)
        p.add("aFD", 64)
        p.add("aLO", 64)
        p.add("aAE", 64)

        def sc_fn(sc):
            for i, (s, hp) in enumerate(slots):
                for kc in range(s):
                    ck = ck_of[(s, hp, kc)]
                    po = (ck % 2) * 1024
                    sc.wait_ge(p.sems["aSC"], A("aSC", ck + 1))
                    if kc == s - 1:
                        sc.wait_ge(p.sems["aMK"], A("aMK", i + 1))
                    if ck >= 2:
                        sc.wait_ge(p.sems["aPL"], A("aPL", ck - 1))
                    sc.activation(pbuf[:, po:po + 512],
                                  PS[0 + (kc % 2)][:, 0:512], AF.Exp,
                                  scale=SCALE,
                                  accum_out=lmat[:, kc:kc + 1])
                    sc.activation(pbuf[:, po + 512:po + 1024],
                                  PS[2 + (kc % 2)][:, 0:512], AF.Exp,
                                  scale=SCALE,
                                  accum_out=lmat[:, 8 + kc:9 + kc]
                                  ).then_inc(p.sems["aEX"], 1)
                    if ck >= 2:
                        sc.wait_ge(p.sems["aTR"], A("aTR", ck - 1))
                    sc.activation(pbh[:, po:po + 512],
                                  PS[0 + (kc % 2)][:, 0:512], AF.Exp,
                                  scale=SCALE)
                    sc.activation(pbh[:, po + 512:po + 1024],
                                  PS[2 + (kc % 2)][:, 0:512], AF.Exp,
                                  scale=SCALE).then_inc(p.sems["aEH"], 1)
                sc.wait_ge(p.sems["aRL"], A("aRL", i + 1))
                sc.wait_ge(p.sems["aFD"], A("aFD", i + 1))
                if i >= 1:
                    sc.wait_ge(p.sems["aTA"], A("aTA", i))
                sc.activation(aohi[:, 0:128], ofold[:, 0:128], AF.Identity,
                              scale=rlc[:, 0:1])
                sc.activation(aohi[:, 128:256], ofold[:, 128:256],
                              AF.Identity,
                              scale=rlc[:, 1:2]).then_inc(p.sems["aHI"], 1)
        blk.scalar(sc_fn)
        p.add("aEX", NCK)
        p.add("aEH", NCK)
        p.add("aHI", 64)

        def t_fn(t):
            t.wait_ge(p.sems["phKV"], p.val("phKV"))
            t.wait_ge(p.sems["aQZ"], p.val("aQZ"))
            for i, (s, hp) in enumerate(slots):
                qs, qv = q_tgt[(s, hp)]
                t.wait_ge(p.sems["aQ" + str(qs)], qv)
                for step in range(s + 2):
                    if step < s:
                        kc = step
                        ck = ck_of[(s, hp, kc)]
                        sl4, kv_ = k_tgt[(s, hp, kc)]
                        t.wait_ge(p.sems["aK" + str(sl4)], kv_)
                        if ck >= 2:
                            t.wait_ge(p.sems["aEH"], A("aEH", ck - 1))
                        o = sl4 * 2048
                        qo = qs * 512
                        ro = qs * 512
                        krh = sb["krH"][:, kc * 512:(kc + 1) * 512]
                        krl = sb["krL"][:, kc * 512:(kc + 1) * 512]
                        for hh in range(2):
                            bank = PS[(0 if hh == 0 else 2) + (kc % 2)]
                            kh_s = kbuf[:, o + hh * 512:o + hh * 512 + 512]
                            kl_s = kbuf[:, o + 1024 + hh * 512:
                                        o + 1024 + hh * 512 + 512]
                            qnh = qn[:, qo + hh * 256:qo + hh * 256 + 128]
                            qnl = qn[:, qo + hh * 256 + 128:
                                     qo + hh * 256 + 256]
                            qrs = qr[:, ro + hh * 256:ro + hh * 256 + 128]
                            qr2 = qr[:, ro + hh * 256 + 128:
                                     ro + hh * 256 + 256]
                            t.matmul(bank[:, 0:512], qnh, kh_s,
                                     start=True, stop=False)
                            t.matmul(bank[:, 0:512], qnh, kl_s,
                                     start=False, stop=False)
                            t.matmul(bank[:, 0:512], qnl, kh_s,
                                     start=False, stop=False)
                            t.matmul(bank[:, 0:512], qrs, krh,
                                     start=False, stop=False)
                            ins = t.matmul(bank[:, 0:512], qr2, krl,
                                           start=False, stop=True)
                            if hh == 1:
                                ins.then_inc(p.sems["aSC"], 1)
                    if 1 <= step <= s:
                        kc = step - 1
                        ck = ck_of[(s, hp, kc)]
                        po = (ck % 2) * 1024
                        t.wait_ge(p.sems["aEH"], A("aEH", ck + 1))
                        t.wait_ge(p.sems["aPL"], A("aPL", ck + 1))
                        if ck >= 1:
                            t.wait_ge(p.sems["aTV"], A("aTV", ck))
                        for t8 in range(8):
                            t.transpose(PS4B[:, t8 * 128:(t8 + 1) * 128],
                                        pbh[:, po + t8 * 128:
                                            po + (t8 + 1) * 128],
                                        sb["identB"][:, :])
                        for t8 in range(8):
                            ins = t.transpose(
                                PS5B[:, t8 * 128:(t8 + 1) * 128],
                                pbl[:, po + t8 * 128:po + (t8 + 1) * 128],
                                sb["identB"][:, :])
                            if t8 == 7:
                                ins.then_inc(p.sems["aTR"], 1)
                    if 2 <= step <= s + 1:
                        kc = step - 2
                        ck = ck_of[(s, hp, kc)]
                        po = (ck % 2) * 1024
                        t.wait_ge(p.sems["aTV"], A("aTV", ck + 1))
                        sl4v, vv_ = v_tgt[(s, hp, kc)]
                        t.wait_ge(p.sems["aV" + str(sl4v)], vv_)
                        if kc == 0 and i >= 1:
                            t.wait_ge(p.sems["aFD"], A("aFD", i))
                        sl3 = ck % 4
                        for hh in range(2):
                            X = hh * 256
                            for kb in range(4):
                                lsl = slice(po + (hh * 4 + kb) * 128,
                                            po + (hh * 4 + kb) * 128 + 128)
                                v256 = slice(sl3 * 2048 + hh * 1024
                                             + kb * 256,
                                             sl3 * 2048 + hh * 1024
                                             + kb * 256 + 256)
                                v128 = slice(v256.start, v256.start + 128)
                                t.matmul(PS[6][:, X:X + 256],
                                         pth[:, lsl], vb[:, v256],
                                         start=(kc == 0 and hh == 0
                                                and kb == 0),
                                         stop=False)
                                ins = t.matmul(
                                    PS[6][:, X:X + 128],
                                    ptl[:, lsl], vb[:, v128],
                                    start=False,
                                    stop=(kc == s - 1 and hh == 1
                                          and kb == 3))
                                if hh == 1 and kb == 3:
                                    ins.then_inc(p.sems["aAV"], 1)
                t.wait_ge(p.sems["aLO"], A("aLO", i + 1))
                if i >= 1:
                    t.wait_ge(p.sems["aAE"], A("aAE", i))
                t.transpose(PB[:, 0:128], aohi[:, 0:128], sb["identB"][:, :])
                t.transpose(PB[:, 128:256], aolo[:, 0:128],
                            sb["identB"][:, :])
                t.transpose(PB[:, 256:384], aohi[:, 128:256],
                            sb["identB"][:, :])
                t.transpose(PB[:, 384:512], aolo[:, 128:256],
                            sb["identB"][:, :]).then_inc(p.sems["aTA"], 1)
        blk.tensor(t_fn)
        p.add("aSC", NCK)
        p.add("aTR", NCK)
        p.add("aAV", NCK)
        p.add("aTA", 64)

        _phase_barrier(p, "phAT", ("aAE",))


def ph_wo(p):
    nc, blk, T, sb = p.nc, p.blk, p.T, p.sb
    PS, PB, PBF = p.PS, p.PB, p.PBF
    afh, afl = p.afh, p.afl

    p.begin_scope()
    with contextlib.ExitStack() as ph:
        e = ph.enter_context
        wot = e(nc.sbuf_tensor("wo_t", [128, 6 * 1024], BF16))
        xtl = e(nc.sbuf_tensor("wo_x", [128, 2 * 8 * 512], F32))
        ystg = e(nc.sbuf_tensor("wo_y", [128, 8 * 512], F32))

        for nm in ("wl20", "wl21", "wl22", "wl23", "wl24", "wl25",
                   "wx20", "wx21", "wm2", "whc", "wev2",
                   "wst20", "wst21"):
            p.sem(nm)
        bs = {nm: p.val(nm)
              for nm in ("wl20", "wl21", "wl22", "wl23", "wl24", "wl25",
                         "wx20", "wx21", "wm2", "whc",
                         "wev2", "wst20", "wst21")}

        def sy_fn(sy):
            for dmc in range(4):
                for sl in range(8):
                    if dmc >= 2:
                        sy.wait_ge(p.sems["wev2"],
                                   bs["wev2"] + (dmc - 1) * 8)
                    sy.dma_start(
                        xtl[:, (dmc % 2) * 4096 + sl * 512:
                            (dmc % 2) * 4096 + (sl + 1) * 512],
                        T["x_own"][sl * 128:(sl + 1) * 128,
                                   dmc * 512:(dmc + 1) * 512]).then_inc(
                        p.sems["wx2" + str(dmc % 2)], 16)
                for hc in range(16):
                    idx = dmc * 16 + hc
                    if idx >= 6:
                        sy.wait_ge(p.sems["whc"], bs["whc"] + idx - 5)
                    o = (idx % 6) * 1024
                    sy.dma_start(wot[:, o:o + 512],
                                 T["WOH"][hc * 128:(hc + 1) * 128,
                                          dmc * 512:(dmc + 1) * 512]
                                 ).then_inc(p.sems["wl2" + str(idx % 6)], 16)
                    sy.dma_start(wot[:, o + 512:o + 1024],
                                 T["WOL"][hc * 128:(hc + 1) * 128,
                                          dmc * 512:(dmc + 1) * 512]
                                 ).then_inc(p.sems["wl2" + str(idx % 6)], 16)
        blk.sync(sy_fn)
        for s6 in range(6):
            p.add("wl2" + str(s6),
                  32 * len([x for x in range(64) if x % 6 == s6]))
        p.add("wx20", 16 * 16)
        p.add("wx21", 16 * 16)

        def t_fn(t):
            for dmc in range(4):
                if dmc >= 1:
                    t.wait_ge(p.sems["wev2"], bs["wev2"] + dmc * 8)
                for hc in range(16):
                    idx = dmc * 16 + hc
                    t.wait_ge(p.sems["wl2" + str(idx % 6)],
                              bs["wl2" + str(idx % 6)] + 32 * (idx // 6 + 1))
                    o = (idx % 6) * 1024
                    for sl in range(8):
                        bank = PS[sl][:, 0:512] if sl < 7 else PBF
                        ac = hc * TOK + sl * 128
                        t.matmul(bank, afh[:, ac:ac + 128], wot[:, o:o + 512],
                                 start=(hc == 0), stop=False)
                        t.matmul(bank, afh[:, ac:ac + 128],
                                 wot[:, o + 512:o + 1024],
                                 start=False, stop=False)
                        ins = t.matmul(bank, afl[:, ac:ac + 128],
                                       wot[:, o:o + 512],
                                       start=False, stop=(hc == 15))
                        if sl == 7:
                            ins.then_inc(p.sems["whc"], 1)
        blk.tensor(t_fn)
        p.add("whc", 64)

        def v_fn(v):
            for dmc in range(4):
                v.wait_ge(p.sems["whc"], bs["whc"] + 16 * (dmc + 1))
                v.wait_ge(p.sems["wx2" + str(dmc % 2)],
                          bs["wx2" + str(dmc % 2)] + 128 * (dmc // 2 + 1))
                for sl in range(8):
                    bank = PS[sl][:, 0:512] if sl < 7 else PBF
                    if dmc >= 1:
                        v.wait_ge(p.sems["wst2" + str((dmc - 1) % 2)],
                                  bs["wst2" + str((dmc - 1) % 2)]
                                  + 128 * ((dmc - 1) // 2 + 1))
                    v.tensor_tensor(
                        ystg[:, sl * 512:(sl + 1) * 512], bank,
                        xtl[:, (dmc % 2) * 4096 + sl * 512:
                            (dmc % 2) * 4096 + (sl + 1) * 512],
                        ALU.add)
                    v.drain()
                    v.tensor_copy(ystg[:, sl * 512:sl * 512 + 1],
                                  ystg[:, sl * 512:sl * 512 + 1]).then_inc(
                        p.sems["wev2"], 1)
        blk.vector(v_fn)
        p.add("wev2", 32)

        def g_fn(g):
            for dmc in range(4):
                for sl in range(8):
                    g.wait_ge(p.sems["wev2"], bs["wev2"] + dmc * 8 + sl + 1)
                    g.dma_start(T["YS"][sl, :, dmc * 512:(dmc + 1) * 512],
                                ystg[:, sl * 512:(sl + 1) * 512]).then_inc(
                        p.sems["wst2" + str(dmc % 2)], 16)
        blk.gpsimd(g_fn)
        p.add("wst20", 16 * 16)
        p.add("wst21", 16 * 16)

        _phase_barrier(p, "phWO", ("wst20", "wst21"))

    p.attn_stack.close()


def ph_ffn(p):
    nc, blk, T, sb = p.nc, p.blk, p.T, p.sb
    PS, PB, PBF = p.PS, p.PB, p.PBF
    nfm = sb["nfm"]

    p.begin_scope()
    with contextlib.ExitStack() as ph:
        e = ph.enter_context
        s1b = e(nc.sbuf_tensor("f_s1b", [128, HID], F32))
        s1row = e(nc.sbuf_tensor("f_s1row", [1, HID], F32))
        w1t = e(nc.sbuf_tensor("f_w1t", [128, 6 * 512], BF16))
        hst = e(nc.sbuf_tensor("f_hst", [128, 8 * 512], F32))

        for nm in ("fsr", "fbm", "fb", "f1l0", "f1l1", "f1m", "f1dc",
                   "f1e", "f1s0", "f1s1"):
            p.sem(nm)

        bfb = {"fb": p.val("fb"), "fbm": p.val("fbm")}

        def g_b(g):
            for r in range(64):
                g.dma_start(s1row[:1, r * 128:(r + 1) * 128],
                            sb["s1T"][r:r + 1, :]).then_inc(p.sems["fsr"], 16)
        blk.gpsimd(g_b)
        p.add("fsr", 16 * 64)

        def t_b(t):
            t.wait_ge(p.sems["fsr"], p.val("fsr"))
            for c in range(16):
                if c >= 2:
                    t.wait_ge(p.sems["fb"], bfb["fb"] + c - 1)
                t.matmul(PS[c % 2][:, 0:512], sb["ones1"][:1, :],
                         s1row[:1, c * 512:(c + 1) * 512],
                         start=True, stop=True).then_inc(p.sems["fbm"], 1)
        blk.tensor(t_b)
        p.add("fbm", 16)

        def v_b(v):
            for c in range(16):
                v.wait_ge(p.sems["fbm"], bfb["fbm"] + c + 1)
                v.tensor_copy(s1b[:, c * 512:(c + 1) * 512],
                              PS[c % 2][:, 0:512]).then_inc(p.sems["fb"], 1)
        blk.vector(v_b)
        p.add("fb", 16)

        b1 = {nm: p.val(nm) for nm in ("f1l0", "f1l1", "f1l2", "f1l3",
                                       "f1l4", "f1l5", "f1m", "f1dc",
                                       "f1e", "f1s0", "f1s1")}

        def sy1(sy):
            for pas in range(2):
                for hc in range(16):
                    for dc in range(16):
                        idx = (pas * 16 + hc) * 16 + dc
                        if idx >= 6:
                            sy.wait_ge(p.sems["f1dc"], b1["f1dc"] + idx - 5)
                        o = (idx % 6) * 512
                        inap = bass.AP(T["W1T"].tensor,
                                       (hc * 4 * 16 + dc) * 128 * 128,
                                       [[128, 128], [16 * 128 * 128, 4],
                                        [1, 128]])
                        sy.dma_start(
                            w1t[:, o:o + 512].rearrange(
                                "p (j c) -> p j c", j=4),
                            inap).then_inc(p.sems["f1l" + str(idx % 6)], 16)
        blk.sync(sy1)
        for s6 in range(6):
            p.add("f1l" + str(s6),
                  16 * len([x for x in range(512) if x % 6 == s6]))

        def t1(t):
            t.wait_ge(p.sems["fb"], p.val("fb"))
            for pas in range(2):
                for hc in range(16):
                    g = pas * 16 + hc
                    bq = (hc % 2) * 4
                    if g >= 2:
                        t.wait_ge(p.sems["f1e"], b1["f1e"] + 4 * (g - 1))
                    for dc in range(16):
                        idx = g * 16 + dc
                        t.wait_ge(p.sems["f1l" + str(idx % 6)],
                                  b1["f1l" + str(idx % 6)]
                                  + 16 * (idx // 6 + 1))
                        for tb4 in range(4):
                            tb = pas * 4 + tb4
                            bank = (PS[bq + tb4][:, 0:512] if bq + tb4 < 7
                                    else PBF)
                            ins = t.matmul(bank,
                                           nfm[:, dc * TOK + tb * 128:
                                               dc * TOK + tb * 128 + 128],
                                           w1t[:, (idx % 6) * 512:
                                               (idx % 6) * 512 + 512],
                                           start=(dc == 0), stop=(dc == 15))
                            if tb4 == 3:
                                ins.then_inc(p.sems["f1dc"], 1)
        blk.tensor(t1)
        p.add("f1dc", 512)

        def v1(v):
            for pas in range(2):
                for hc in range(16):
                    g = pas * 16 + hc
                    bq = (hc % 2) * 4
                    v.wait_ge(p.sems["f1dc"], b1["f1dc"] + 16 * (g + 1))
                    for tb4 in range(4):
                        tb = pas * 4 + tb4
                        bank = (PS[bq + tb4][:, 0:512] if bq + tb4 < 7
                                else PBF)
                        so = ((g % 2) * 4 + tb4) * 512
                        if g >= 2:
                            v.wait_ge(p.sems["f1s" + str(g % 2)],
                                      b1["f1s" + str(g % 2)]
                                      + 64 * (g // 2))
                        v.scalar_tensor_tensor(
                            hst[:, so:so + 512], bank,
                            sb["rsxm"][:, tb:tb + 1],
                            s1b[:, hc * 512:(hc + 1) * 512],
                            ALU.mult, ALU.mult)
                        v.drain()
                        v.tensor_copy(hst[:, so:so + 1],
                                      hst[:, so:so + 1]).then_inc(
                            p.sems["f1e"], 1)
        blk.vector(v1)
        p.add("f1e", 32 * 4)

        def g1(g):
            for pas in range(2):
                for hc in range(16):
                    gg = pas * 16 + hc
                    for tb4 in range(4):
                        tb = pas * 4 + tb4
                        so = ((gg % 2) * 4 + tb4) * 512
                        n = gg * 4 + tb4
                        g.wait_ge(p.sems["f1e"], b1["f1e"] + n + 1)
                        g.dma_start(T["HS"][tb, :, hc * 512:(hc + 1) * 512],
                                    hst[:, so:so + 512]).then_inc(
                            p.sems["f1s" + str(gg % 2)], 16)
        blk.gpsimd(g1)
        p.add("f1s0", 16 * 64)
        p.add("f1s1", 16 * 64)

        _phase_barrier(p, "phF1", ("f1s0", "f1s1"))

    p.begin_scope()
    with contextlib.ExitStack() as ph:
        e = ph.enter_context
        hbuf = e(nc.sbuf_tensor("f_h", [128, HID], F32))
        gbuf = e(nc.sbuf_tensor("f_g", [128, HID], F32))
        c8b = e(nc.sbuf_tensor("f_c8", [128, HID], I8))
        fb1 = e(nc.sbuf_tensor("f_b1", [128, HID], BF16))
        fb2 = e(nc.sbuf_tensor("f_b2", [128, HID], BF16))
        sdb = e(nc.sbuf_tensor("f_sd", [128, HID], BF16))
        ccq = e(nc.sbuf_tensor("f_cc", [128, 8], F32))
        nstg = e(nc.sbuf_tensor("f_nstg", [128, 2 * 1024], BF16))

        for nm in ("q2l", "q2g", "q2x", "q2n", "q2t", "q2e", "q2s0",
                   "q2s1"):
            p.sem(nm)
        b2 = {nm: p.val(nm) for nm in ("q2l", "q2g", "q2x", "q2n", "q2t",
                                       "q2e", "q2s0", "q2s1")}

        def sy2(sy):
            for tb in range(NB):
                if tb >= 1:
                    sy.wait_ge(p.sems["q2n"], b2["q2n"] + tb)
                sy.dma_start(hbuf[:, :], T["HS"][tb]).then_inc(
                    p.sems["q2l"], 16)
        blk.sync(sy2)
        p.add("q2l", 16 * NB)

        def sc2(sc):
            for tb in range(NB):
                sc.wait_ge(p.sems["q2l"], b2["q2l"] + 16 * (tb + 1))
                if tb >= 1:
                    sc.wait_ge(p.sems["q2n"], b2["q2n"] + tb)
                sc.activation(gbuf[:, :], hbuf[:, :],
                              AF.Gelu_apprx_tanh).then_inc(p.sems["q2g"], 1)
                sc.wait_ge(p.sems["q2x"], b2["q2x"] + 2 * tb + 1)
                sc.activation(hbuf[:, :], gbuf[:, :], AF.Identity,
                              scale=ccq[:, 3:4])
                sc.drain()
                sc.activation(c8b[:, :], gbuf[:, :], AF.Identity,
                              scale=ccq[:, 3:4]).then_inc(p.sems["q2g"], 1)
        blk.scalar(sc2)
        p.add("q2g", 2 * NB)

        def v2(v):
            for tb in range(NB):
                v.wait_ge(p.sems["q2g"], b2["q2g"] + 2 * tb + 1)
                v.tensor_reduce(ccq[:, 0:1], gbuf[:, :], AX, ALU.max,
                                apply_absolute_value=True)
                v.drain()
                v.tensor_scalar(ccq[:, 1:2], ccq[:, 0:1], 1e-5, None, ALU.max)
                v.drain()
                v.reciprocal(ccq[:, 2:3], ccq[:, 1:2])
                v.drain()
                v.tensor_scalar(ccq[:, 3:4], ccq[:, 2:3], 127.0, None,
                                ALU.mult)
                v.drain()
                v.tensor_scalar(sb["rshm"][:, tb:tb + 1], ccq[:, 1:2],
                                1.0 / 127.0, None, ALU.mult).then_inc(
                    p.sems["q2x"], 1)
                v.wait_ge(p.sems["q2g"], b2["q2g"] + 2 * tb + 2)
                v.scalar_tensor_tensor(fb1[:, :], hbuf[:, :], -0.5,
                                       c8b[:, :], ALU.add, ALU.is_ge)
                v.scalar_tensor_tensor(fb2[:, :], hbuf[:, :], 0.5,
                                       c8b[:, :], ALU.add, ALU.is_le)
                v.drain()
                v.tensor_tensor(sdb[:, :], fb1[:, :], fb2[:, :], ALU.subtract)
                v.drain()
                v.tensor_tensor(fb1[:, :], sdb[:, :], c8b[:, :],
                                ALU.add).then_inc(p.sems["q2x"], 1)
                for g8 in range(8):
                    gi = tb * 8 + g8
                    v.wait_ge(p.sems["q2t"], b2["q2t"] + gi + 1)
                    if gi >= 2:
                        v.wait_ge(p.sems["q2s" + str(gi % 2)],
                                  b2["q2s" + str(gi % 2)] + 128 * (gi // 2))
                    v.tensor_copy(nstg[:, (gi % 2) * 1024:
                                       (gi % 2) * 1024 + 1024], PB[:, :])
                    v.drain()
                    v.memset(ccq[:, 7:8], 0.0).then_inc(
                        p.sems["q2e"], 1)
                v.drain()
                v.memset(ccq[:, 7:8], 0.0).then_inc(
                    p.sems["q2n"], 1)
        blk.vector(v2)
        p.add("q2x", 2 * NB)
        p.add("q2e", 8 * NB)
        p.add("q2n", NB)

        def t2(t):
            for tb in range(NB):
                t.wait_ge(p.sems["q2x"], b2["q2x"] + 2 * tb + 2)
                for g8 in range(8):
                    gi = tb * 8 + g8
                    if gi >= 1:
                        t.wait_ge(p.sems["q2e"], b2["q2e"] + gi)
                    for k in range(8):
                        hcc = g8 * 8 + k
                        ins = t.transpose(PB[:, k * 128:(k + 1) * 128],
                                          fb1[:, hcc * 128:(hcc + 1) * 128],
                                          sb["identB"][:, :])
                        if k == 7:
                            ins.then_inc(p.sems["q2t"], 1)
        blk.tensor(t2)
        p.add("q2t", 8 * NB)

        def g2(g):
            for tb in range(NB):
                for g8 in range(8):
                    gi = tb * 8 + g8
                    g.wait_ge(p.sems["q2e"], b2["q2e"] + gi + 1)
                    src_ = nstg[:, (gi % 2) * 1024:(gi % 2) * 1024
                                + 1024].rearrange("p (k c) -> p k c", k=8)
                    dst = T["NHS"][g8 * 8:g8 * 8 + 8, tb].rearrange(
                        "k p c -> p k c")
                    for k in range(8):
                        g.dma_start(dst[:, k], src_[:, k]).then_inc(
                            p.sems["q2s" + str(gi % 2)], 16)
        blk.gpsimd(g2)
        p.add("q2s0", 128 * 32)
        p.add("q2s1", 128 * 32)

        _phase_barrier(p, "phQ2", ("q2s0", "q2s1"))

    p.begin_scope()
    with contextlib.ExitStack() as ph:
        e = ph.enter_context
        s2b = e(nc.sbuf_tensor("f_s2b", [128, D], F32))
        s2row = e(nc.sbuf_tensor("f_s2row", [1, D], F32))
        w2t = e(nc.sbuf_tensor("f_w2t", [128, 6 * 512], BF16))
        nh2 = e(nc.sbuf_tensor("f_nh2", [128, 6 * 512], BF16))
        ytl = e(nc.sbuf_tensor("f_ytl", [128, 2 * 4 * 512], F32))
        ostg = e(nc.sbuf_tensor("f_ostg", [128, 4 * 512], F32))

        for nm in ("fcm", "fc", "n2l0", "n2l1", "n2l2", "n2l3", "n2l4",
                   "n2l5", "f2l0", "f2l1", "f2l2", "f2l3", "f2l4", "f2l5",
                   "f2y", "f2m", "f2hc", "f2e", "f2s"):
            p.sem(nm)

        def g_c(g):
            for r in range(16):
                g.dma_start(s2row[:1, r * 128:(r + 1) * 128],
                            sb["s2T"][r:r + 1, :]).then_inc(p.sems["fsr2"],
                                                           16)
        p.sem("fsr2")
        blk.gpsimd(g_c)
        p.add("fsr2", 16 * 16)

        bfc = {"fc": p.val("fc"), "fcm": p.val("fcm")}

        def t_c(t):
            t.wait_ge(p.sems["fsr2"], p.val("fsr2"))
            for c in range(4):
                if c >= 2:
                    t.wait_ge(p.sems["fc"], bfc["fc"] + c - 1)
                t.matmul(PS[c % 2][:, 0:512], sb["ones1"][:1, :],
                         s2row[:1, c * 512:(c + 1) * 512],
                         start=True, stop=True).then_inc(p.sems["fcm"], 1)
        blk.tensor(t_c)
        p.add("fcm", 4)

        def v_c(v):
            for c in range(4):
                v.wait_ge(p.sems["fcm"], bfc["fcm"] + c + 1)
                v.tensor_copy(s2b[:, c * 512:(c + 1) * 512],
                              PS[c % 2][:, 0:512]).then_inc(p.sems["fc"], 1)
        blk.vector(v_c)
        p.add("fc", 4)

        b3 = {nm: p.val(nm) for nm in ("n2l0", "n2l1", "n2l2", "n2l3",
                                       "n2l4", "n2l5", "f2l0", "f2l1",
                                       "f2l2", "f2l3", "f2l4", "f2l5",
                                       "f2y", "f2m", "f2hc", "f2e", "f2s")}

        def sy3(sy):
            for pas in range(2):
                for dmc in range(4):
                    i = pas * 4 + dmc
                    if i >= 1:
                        sy.wait_ge(p.sems["f2e"], b3["f2e"] + 4 * (i - 1) + 1)
                    for tb4 in range(4):
                        tb = pas * 4 + tb4
                        sy.dma_start(
                            ytl[:, (i % 2) * 2048 + tb4 * 512:
                                (i % 2) * 2048 + (tb4 + 1) * 512],
                            T["YS"][tb, :, dmc * 512:(dmc + 1) * 512]
                        ).then_inc(p.sems["f2y"], 16)
                    for hc in range(64):
                        idx = i * 64 + hc
                        if idx >= 6:
                            sy.wait_ge(p.sems["f2hc"], b3["f2hc"] + idx - 5)
                        o = (idx % 6) * 512
                        inap = bass.AP(T["W2T"].tensor,
                                       (hc * 16 + dmc * 4) * 128 * 128,
                                       [[128, 128], [128 * 128, 4], [1, 128]])
                        sy.dma_start(
                            w2t[:, o:o + 512].rearrange(
                                "p (j c) -> p j c", j=4),
                            inap).then_inc(p.sems["f2l" + str(idx % 6)], 16)
        blk.sync(sy3)

        def g3l(g):
            def stores(i2):
                for tb4 in range(4):
                    tb = (i2 // 4) * 4 + tb4
                    dmc2 = i2 % 4
                    so = tb4 * 512
                    g.wait_ge(p.sems["f2e"],
                              b3["f2e"] + i2 * 4 + tb4 + 1)
                    g.dma_start(T["y"][tb * 128:(tb + 1) * 128,
                                       dmc2 * 512:(dmc2 + 1) * 512],
                                ostg[:, so:so + 512]).then_inc(
                        p.sems["f2s"], 16)

            for pas in range(2):
                for dmc in range(4):
                    i = pas * 4 + dmc
                    for hc in range(64):
                        idx = i * 64 + hc
                        if idx >= 6:
                            g.wait_ge(p.sems["f2hc"], b3["f2hc"] + idx - 5)
                        o = (idx % 6) * 512
                        nsrc = T["NHS"][hc, pas * 4:pas * 4 + 4].rearrange(
                            "t p c -> p t c")
                        g.dma_start(
                            nh2[:, o:o + 512].rearrange(
                                "p (t c) -> p t c", t=4),
                            nsrc).then_inc(p.sems["n2l" + str(idx % 6)], 16)
                    if i >= 1:
                        stores(i - 1)
            stores(7)
            g.wait_ge(p.sems["f2s"], p.val("f2s") + 16 * 32)
        blk.gpsimd(g3l)
        p.add("f2s", 16 * 32)
        p.add("f2y", 16 * 32)
        for s6 in range(6):
            n6 = 16 * len([x for x in range(512) if x % 6 == s6])
            p.add("f2l" + str(s6), n6)
            p.add("n2l" + str(s6), n6)

        def t3(t):
            t.wait_ge(p.sems["fc"], p.val("fc"))
            for pas in range(2):
                for dmc in range(4):
                    i = pas * 4 + dmc
                    bq = (dmc % 2) * 4
                    if i >= 2:
                        t.wait_ge(p.sems["f2e"], b3["f2e"] + 4 * (i - 1))
                    for hc in range(64):
                        idx = i * 64 + hc
                        t.wait_ge(p.sems["f2l" + str(idx % 6)],
                                  b3["f2l" + str(idx % 6)]
                                  + 16 * (idx // 6 + 1))
                        t.wait_ge(p.sems["n2l" + str(idx % 6)],
                                  b3["n2l" + str(idx % 6)]
                                  + 16 * (idx // 6 + 1))
                        o = (idx % 6) * 512
                        for tb4 in range(4):
                            bank = (PS[bq + tb4][:, 0:512] if bq + tb4 < 7
                                    else PBF)
                            ins = t.matmul(bank,
                                           nh2[:, o + tb4 * 128:
                                               o + (tb4 + 1) * 128],
                                           w2t[:, o:o + 512],
                                           start=(hc == 0), stop=(hc == 63))
                            if tb4 == 3:
                                ins.then_inc(p.sems["f2hc"], 1)
        blk.tensor(t3)
        p.add("f2hc", 512)

        def v3(v):
            for pas in range(2):
                for dmc in range(4):
                    i = pas * 4 + dmc
                    bq = (dmc % 2) * 4
                    v.wait_ge(p.sems["f2hc"], b3["f2hc"] + 64 * (i + 1))
                    v.wait_ge(p.sems["f2y"], b3["f2y"] + 16 * 4 * (i + 1))
                    for tb4 in range(4):
                        tb = pas * 4 + tb4
                        bank = (PS[bq + tb4][:, 0:512] if bq + tb4 < 7
                                else PBF)
                        so = tb4 * 512
                        if i >= 1:
                            v.wait_ge(p.sems["f2s"],
                                      b3["f2s"] + 16 * 4 * i)
                        v.scalar_tensor_tensor(
                            ostg[:, so:so + 512], bank,
                            sb["rshm"][:, tb:tb + 1],
                            s2b[:, dmc * 512:(dmc + 1) * 512],
                            ALU.mult, ALU.mult)
                        v.drain()
                        v.tensor_tensor(
                            ostg[:, so:so + 512], ostg[:, so:so + 512],
                            ytl[:, (i % 2) * 2048 + tb4 * 512:
                                (i % 2) * 2048 + (tb4 + 1) * 512],
                            ALU.add)
                        v.drain()
                        v.tensor_copy(ostg[:, so:so + 1],
                                      ostg[:, so:so + 1]).then_inc(
                            p.sems["f2e"], 1)
        blk.vector(v3)
        p.add("f2e", 32)





_NC_CACHE = None


def _get_nc():
    global _NC_CACHE
    if _NC_CACHE is None:
        _NC_CACHE = build_nc()
    return _NC_CACHE


def kernel(L0_attn_x, L0_attn_Wq, L0_attn_Wdkv, L0_attn_Wkr, L0_attn_Wukv,
           L0_attn_Wo, L0_ffn_w1, L0_ffn_w2):
    x = np.ascontiguousarray(np.asarray(L0_attn_x, dtype=np.float32))
    Wq3 = np.asarray(L0_attn_Wq, dtype=np.float32).reshape(D, H, DH + DR)
    wqn = np.ascontiguousarray(Wq3[:, :, :DH].reshape(D, H * DH))
    wqr = np.ascontiguousarray(Wq3[:, :, DH:].reshape(D, H * DR))
    Wukv3 = np.asarray(L0_attn_Wukv, dtype=np.float32).reshape(LAT, H, 2 * DH)
    wuk = np.ascontiguousarray(Wukv3[:, :, :DH].reshape(LAT, H * DH))
    wuv = np.ascontiguousarray(Wukv3[:, :, DH:].reshape(LAT, H * DH))

    nc = _get_nc()
    in_maps = []
    for core in range(8):
        b, c = core // 4, core % 4
        blocks = _blocks_for_core(c)
        rows = np.concatenate([np.arange(j * 128, (j + 1) * 128)
                               for j in blocks])
        qr = np.zeros((128, NB), dtype=np.float32)
        for si, j in enumerate(blocks):
            s = si + 1
            qr[:, si] = j * 128 + np.arange(128) - (4 * s - 4) * 128
        in_maps.append({
            "x_own": np.ascontiguousarray(x[b][rows]),
            "x_full": np.ascontiguousarray(x[b]),
            "pos_own": rows.astype(np.float32).reshape(1, TOK),
            "qrel": qr,
            "wqn": wqn, "wqr": wqr,
            "wdkv": np.ascontiguousarray(
                np.asarray(L0_attn_Wdkv, dtype=np.float32)),
            "wkr": np.ascontiguousarray(
                np.asarray(L0_attn_Wkr, dtype=np.float32)),
            "wuk": wuk, "wuv": wuv,
            "wo": np.ascontiguousarray(
                np.asarray(L0_attn_Wo, dtype=np.float32)),
            "w1": np.ascontiguousarray(
                np.asarray(L0_ffn_w1, dtype=np.float32)),
            "w2": np.ascontiguousarray(
                np.asarray(L0_ffn_w2, dtype=np.float32)),
        })

    res = run_bass_kernel_spmd(nc, in_maps, core_ids=list(range(8)))

    out = np.empty((B, S, D), dtype=np.float32)
    for core in range(8):
        b, c = core // 4, core % 4
        blocks = _blocks_for_core(c)
        yo = np.asarray(res.results[core]["y"])
        for si, j in enumerate(blocks):
            out[b, j * 128:(j + 1) * 128] = yo[si * 128:(si + 1) * 128]
    return out
